# version 02
# baseline (speedup 1.0000x reference)
import numpy as np
import concourse.bass as bass
import concourse.mybir as mybir
from concourse.bass_utils import run_bass_kernel_spmd

F32 = mybir.dt.float32
BF16 = mybir.dt.bfloat16
AF = mybir.ActivationFunctionType
OP = mybir.AluOpType
AX = mybir.AxisListType

NCORES = 8
TAU = 2048
NT = 16
DM = 1024
DI = 2048
NDT = 16
NST = 16
DTR = 64
II = 2816
NIT = 22
NLAYER = 2
NTB = 4
MIDS = [0.25, 0.75, 1.25, 1.75, 2.5, 3.5, 5.0]
EPS = 1e-5


class Prog:
    ENGINES = ("sync", "vector", "scalar", "tensor", "gpsimd")

    def __init__(self, nc):
        self.nc = nc
        self.q = {e: [] for e in self.ENGINES}
        self.sems = {}
        self.cnt = {}

    def sem(self, name):
        if name not in self.sems:
            self.sems[name] = self.nc.alloc_semaphore(name=name)
            self.cnt[name] = 0
        return self.sems[name]

    def op(self, eng, fn, sem=None, amt=1):
        s = self.sem(sem) if sem else None
        if sem:
            self.cnt[sem] += amt

        def run(e, fn=fn, s=s, amt=amt):
            ins = fn(e)
            if s is not None:
                ins.then_inc(s, amt)
        self.q[eng].append(run)
        return self.cnt[sem] if sem else None

    def wait(self, eng, sem, val=None):
        s = self.sem(sem)
        v = self.cnt[sem] if val is None else val

        def run(e, s=s, v=v):
            e.wait_ge(s, v)
        self.q[eng].append(run)

    def val(self, sem):
        return self.cnt.get(sem, 0)

    def replay(self, block):
        for name in self.ENGINES:
            dec = getattr(block, name)
            ops = self.q[name]

            def body(e, ops=ops):
                for f in ops:
                    f(e)
            dec(body)


SEM_IN, SEM_MM, SEM_ACT, SEM_DVE, SEM_GP, SEM_OUT, SEM_CC = (
    "s_in", "s_mm", "s_act", "s_dve", "s_gp", "s_out", "s_cc")


def build_program(dbg=False, nlay=NLAYER):
    nc = bass.Bass("TRN2", target_bir_lowering=False, debug=False)
    nc.num_devices = NCORES
    p = Prog(nc)

    def din(name, shape, dtype=F32):
        return nc.dram_tensor(name, shape, dtype, kind="ExternalInput").ap()

    x_in = din("x", [TAU, DM])
    wiT_h = din("wiT", [NLAYER, DM, 2 * DI])
    wxT_h = din("wxT", [NLAYER, DI, 96])
    wdtT_h = din("wdtT", [NLAYER, DTR, DI])
    woT_h = din("woT", [NLAYER, DI, DM])
    bi_h = din("bi_c", [128, NLAYER * 32])
    cw_h = din("cw_c", [128, NLAYER * NDT * 5])
    cb_h = din("cb_c", [128, NLAYER * NDT])
    dtb_h = din("dtb_c", [128, NLAYER * NDT])
    alog_h = din("alog_c", [128, NLAYER * NDT * NST])
    dd_h = din("d_c", [128, NLAYER * NDT])
    nw_h = din("nw_c", [128, NLAYER * 8])
    nfw_h = din("nfw_r", [1, DM])
    bo_h = din("bo_r", [1, NLAYER * DM])
    wg_h = din("wg", [II, DM])
    wu_h = din("wu", [II, DM])
    wd_h = din("wd", [DM, II])
    oh_h = din("oh", [1, 24])

    out_h = nc.dram_tensor("out", [TAU, DM], F32, kind="ExternalOutput").ap()

    def scr(name, shape, dtype=F32, force_internal=False):
        kind = "ExternalOutput" if (dbg and not force_internal) else "Internal"
        return nc.dram_tensor(name, shape, dtype, kind=kind).ap()

    wgT_q = scr("wgT_q", [DM, II], BF16)
    wuT_q = scr("wuT_q", [DM, II], BF16)
    wdT_q = scr("wdT_q", [II, DM], BF16)
    x_sp = scr("x_sp", [NT, 128, DM])
    h_sp = scr("h_sp", [NT, 128, DM])
    res_sp = scr("res_sp", [NDT, 128, TAU])
    del_sp = scr("del_sp", [NDT, 128, TAU])
    w_sp = scr("w_sp", [NDT, 128, TAU])
    yd_sp = scr("yd_sp", [NDT, 128, TAU])
    y_sp = scr("y_sp", [NDT, 128, TAU])
    u_sp = scr("u_sp", [NDT, 128, TAU])
    bc_sp = scr("bc_sp", [32, TAU])
    scan_dbg = scr("scan_dbg", [4, 128, TAU])
    xln_dbg = scr("xln_dbg", [NT, 128, DM])
    xq_dbg = scr("xq_dbg", [NT, 128, DM], BF16)
    h_dbg = scr("h_dbg", [NT, 128, II])
    gin_h = scr("gin_h", [128, NDT * 4], force_internal=True)
    gout_h = scr("gout_h", [NCORES, 128, NDT * 4], force_internal=True)
    gin_s = scr("gin_s", [128, 512], force_internal=True)
    gout_s = scr("gout_s", [NCORES, 128, 512], force_internal=True)

    def sball(name, shape, dtype=F32):
        return nc.alloc_sbuf_tensor(name, shape, dtype).ap()

    identf = sball("identf", [128, 128], F32)
    identb = sball("identb", [128, 128], BF16)
    onesr = sball("onesr", [1, 128], F32)
    bi_sb = sball("bi_sb", [128, NLAYER * 32], F32)
    cw_sb = sball("cw_sb", [128, NLAYER * NDT * 5], F32)
    cb_sb = sball("cb_sb", [128, NLAYER * NDT], F32)
    dtb_sb = sball("dtb_sb", [128, NLAYER * NDT], F32)
    a_sb = sball("a_sb", [128, NLAYER * NDT * NST], F32)
    d_sb = sball("d_sb", [128, NLAYER * NDT], F32)
    nw_sb = sball("nw_sb", [128, NLAYER * 8], F32)
    bo_sb = sball("bo_sb", [1, NLAYER * DM], F32)
    oh_sb = sball("oh_sb", [128, 24], F32)
    stat_sb = sball("stat_sb", [128, 16], F32)
    scol = sball("scol", [128, 1088], F32)
    halo_sb = sball("halo_sb", [128, NDT * 4], F32)

    stream = sball("stream", [128, NT * DM], F32)
    bigA = sball("bigA", [128, 12 * 1024], F32)
    bigB = sball("bigB", [128, 32 * 1024], BF16)
    wbuf = sball("wbuf", [128, 3 * 1024], F32)

    ps = [nc.alloc_psum_tensor(f"ps{i}", [128, 512], F32).ap() for i in range(4)]
    pst = [nc.alloc_psum_tensor(f"pst{i}", [128, 1024], BF16).ap()[:, 0:128]
           for i in range(2)]

    def wb_bf16(off_f32, n):
        return wbuf[:, off_f32:off_f32 + (n + 1) // 2].bitcast(BF16)[:, 0:n]

    stream_b = stream.bitcast(BF16)

    bank_free = {}

    def bank_wait(eng, bank):
        if id(bank) in bank_free:
            sname, v = bank_free[id(bank)]
            p.wait(eng, sname, v)

    def bank_mark(bank, sname):
        bank_free[id(bank)] = (sname, p.val(sname))

    def dma_in(dst, src):
        return p.op("sync", lambda e, d=dst, s=src: e.dma_start(out=d, in_=s), SEM_IN, 16)

    def dma_out(dst, src):
        return p.op("sync", lambda e, d=dst, s=src: e.dma_start(out=d, in_=s), SEM_OUT, 16)

    def v_op(fn):
        return p.op("vector", fn, SEM_DVE)

    def g_op(fn):
        return p.op("gpsimd", fn, SEM_GP)

    def s_op(fn):
        return p.op("scalar", fn, SEM_ACT)

    def t_op(fn):
        return p.op("tensor", fn, SEM_MM)

    def vpad():
        v_op(lambda e: e.tensor_copy(out=stat_sb[:, 12:13], in_=identf[:, 0:1]))

    dma_in(stream.rearrange("p (a b) -> p a b", b=DM), x_in.rearrange("(a p) b -> p a b", p=128))
    dma_in(bi_sb, bi_h)
    dma_in(cw_sb, cw_h)
    dma_in(cb_sb, cb_h)
    dma_in(dtb_sb, dtb_h)
    dma_in(a_sb, alog_h)
    dma_in(d_sb, dd_h)
    dma_in(nw_sb, nw_h)
    dma_in(bo_sb, bo_h)
    dma_in(oh_sb, oh_h.partition_broadcast(128))
    setup_in = p.val(SEM_IN)

    p.wait("gpsimd", SEM_IN, setup_in)
    g_op(lambda e: e.memset(identf[:], 1.0))
    g_op(lambda e: e.affine_select(out=identf[:], in_=identf[:], pattern=[[-1, 128]],
                                   compare_op=OP.is_equal, fill=0.0, base=0,
                                   channel_multiplier=1))
    g_op(lambda e: e.tensor_copy(out=identb[:], in_=identf[:]))
    ident_gp = p.val(SEM_GP)
    p.wait("vector", SEM_IN, setup_in)
    v_op(lambda e: e.memset(onesr[:], 1.0))
    p.wait("scalar", SEM_IN, setup_in)
    s_op(lambda e: e.activation(out=bigA[:, 0:NLAYER * NDT * NST], in_=a_sb[:],
                                func=AF.Exp))
    p.wait("vector", SEM_ACT, p.val(SEM_ACT))
    v_op(lambda e: e.tensor_single_scalar(out=a_sb[:],
                                          in_=bigA[:, 0:NLAYER * NDT * NST],
                                          scalar=-1.0, op=OP.mult))
    p.wait("tensor", SEM_GP, ident_gp)
    p.wait("tensor", SEM_IN, setup_in)

    def quant(dst, src, F, aoff, boff):
        nb = F // 16
        xn = bigA[:, aoff:aoff + F]
        qa = bigA[:, aoff + F:aoff + 2 * F]
        xsq = bigA[:, aoff + 2 * F:aoff + 3 * F]
        bm = bigA[:, aoff + 3 * F:aoff + 3 * F + nb]
        scl = bigA[:, aoff + 3 * F + nb:aoff + 3 * F + 2 * nb]
        rcp = bigA[:, aoff + 3 * F + 2 * nb:aoff + 3 * F + 3 * nb]
        m0 = bigB[:, boff:boff + F]
        m1 = bigB[:, boff + F:boff + 2 * F]
        m2 = bigB[:, boff + 2 * F:boff + 3 * F]
        src3 = src.rearrange("p (b k) -> p b k", k=16)
        v_op(lambda e: e.tensor_reduce(out=bm, in_=src3, axis=AX.X, op=OP.max,
                                       apply_absolute_value=True))
        vpad()
        v_op(lambda e: e.tensor_scalar(out=scl, in0=bm, scalar1=1.0 / 6.0,
                                       scalar2=1e-12, op0=OP.mult, op1=OP.max))
        vpad()
        v_op(lambda e: e.reciprocal(out=rcp, in_=scl))
        vpad()
        v_op(lambda e: e.tensor_tensor(
            out=xn.rearrange("p (b k) -> p b k", k=16), in0=src3,
            in1=rcp.rearrange("p (b o) -> p b o", o=1).broadcast_to([128, nb, 16]),
            op=OP.mult))
        dv = p.val(SEM_DVE)
        p.wait("scalar", SEM_DVE, dv)
        avq = s_op(lambda e: e.activation(out=xsq, in_=xn, func=AF.Square))

        def mk(dst_t, th):
            return lambda e: e.tensor_single_scalar(out=dst_t, in_=xsq,
                                                    scalar=th * th, op=OP.is_ge)
        p.wait("vector", SEM_ACT, avq)
        v_op(mk(m0, MIDS[0]))
        v_op(mk(m1, MIDS[1]))
        v_op(lambda e: e.tensor_tensor(out=m0, in0=m0, in1=m1, op=OP.add))
        v_op(mk(m1, MIDS[2]))
        v_op(lambda e: e.tensor_tensor(out=m0, in0=m0, in1=m1, op=OP.add))
        v_op(mk(m1, MIDS[3]))
        v_op(lambda e: e.tensor_tensor(out=m0, in0=m0, in1=m1, op=OP.add))
        v_op(mk(m2, MIDS[4]))
        v_op(mk(m1, MIDS[5]))
        v_op(lambda e: e.tensor_tensor(out=m2, in0=m2, in1=m1, op=OP.add))
        v_op(mk(m1, MIDS[6]))
        v_op(lambda e: e.scalar_tensor_tensor(out=qa, in0=m0, scalar=0.5, in1=m2,
                                              op0=OP.mult, op1=OP.add))
        v_op(lambda e: e.scalar_tensor_tensor(out=qa, in0=m1, scalar=2.0, in1=qa,
                                              op0=OP.mult, op1=OP.add))
        v_op(lambda e: e.tensor_scalar(out=m1, in0=xn, scalar1=0.0, scalar2=2.0,
                                       op0=OP.is_ge, op1=OP.mult))
        v_op(lambda e: e.tensor_tensor(out=xn, in0=qa, in1=m1, op=OP.mult))
        v_op(lambda e: e.tensor_tensor(out=qa, in0=xn, in1=qa, op=OP.subtract))
        v_op(lambda e: e.tensor_tensor(
            out=dst.rearrange("p (b k) -> p b k", k=16),
            in0=qa.rearrange("p (b k) -> p b k", k=16),
            in1=scl.rearrange("p (b o) -> p b o", o=1).broadcast_to([128, nb, 16]),
            op=OP.mult))

    def transpose_tiles(src, ntile, dst_ap_of_t):
        for t in range(ntile):
            pb = pst[t % 2]
            bank_wait("tensor", pb)
            t_op(lambda e, pb=pb, t=t: e.transpose(
                out=pb[:], in_=src[:, t * 128:(t + 1) * 128], identity=identb[:]))
            p.wait("scalar", SEM_MM, p.val(SEM_MM))
            s_op(lambda e, pb=pb, d=dst_ap_of_t(t): e.copy(out=d, in_=pb[:]))
            bank_mark(pb, SEM_ACT)

    def quant_weights():
        wtile = bigA[:, 9000:9000 + 2816]
        qtile = bigB[:, 0:2816]
        for whbm, dst, rows, cols in ((wg_h, wgT_q, NIT, DM), (wu_h, wuT_q, NIT, DM),
                                      (wd_h, wdT_q, 8, II)):
            for r in range(rows):
                p.wait("sync", SEM_DVE, p.val(SEM_DVE))
                p.wait("sync", SEM_MM, p.val(SEM_MM))
                ld = dma_in(wtile[:, 0:cols], whbm[r * 128:(r + 1) * 128, :])
                p.wait("vector", SEM_IN, ld)
                p.wait("gpsimd", SEM_IN, ld)
                quant(qtile[:, 0:cols], wtile[:, 0:cols], cols, 0, 4096)
                dv = p.val(SEM_DVE)
                p.wait("tensor", SEM_DVE, dv)
                for t in range(cols // 128):
                    pb = pst[t % 2]
                    ev = wb_bf16(2048 + (t % 2) * 64, 128)
                    bank_wait("tensor", pb)
                    t_op(lambda e, pb=pb, t=t: e.transpose(
                        out=pb[:], in_=qtile[:, t * 128:(t + 1) * 128],
                        identity=identb[:]))
                    p.wait("scalar", SEM_MM, p.val(SEM_MM))
                    p.wait("scalar", SEM_OUT, p.val(SEM_OUT))
                    av = s_op(lambda e, ev=ev, pb=pb: e.copy(out=ev, in_=pb[:]))
                    bank_mark(pb, SEM_ACT)
                    p.wait("sync", SEM_ACT, av)
                    dma_out(dst[t * 128:(t + 1) * 128, r * 128:(r + 1) * 128], ev)

    quant_weights()

    def ln_tile(src_t, dst):
        s1, s2, s3 = stat_sb[:, 0:1], stat_sb[:, 1:2], stat_sb[:, 2:3]
        t2 = bigA[:, 4096:5120]
        v_op(lambda e: e.tensor_reduce(out=s1, in_=src_t, axis=AX.X, op=OP.add))
        dv = p.val(SEM_DVE)
        p.wait("scalar", SEM_DVE, dv)
        av = s_op(lambda e: e.activation(out=t2, in_=src_t, func=AF.Square))
        p.wait("vector", SEM_ACT, av)
        v_op(lambda e: e.tensor_reduce(out=s2, in_=t2, axis=AX.X, op=OP.add))
        v_op(lambda e: e.tensor_single_scalar(out=s1, in_=s1, scalar=1.0 / DM, op=OP.mult))
        v_op(lambda e: e.tensor_single_scalar(out=s2, in_=s2, scalar=1.0 / DM, op=OP.mult))
        v_op(lambda e: e.tensor_tensor(out=s3, in0=s1, in1=s1, op=OP.mult))
        vpad()
        v_op(lambda e: e.tensor_tensor(out=s2, in0=s2, in1=s3, op=OP.subtract))
        vpad()
        v_op(lambda e: e.tensor_single_scalar(out=s2, in_=s2, scalar=EPS, op=OP.add))
        dv = p.val(SEM_DVE)
        p.wait("scalar", SEM_DVE, dv)
        av = s_op(lambda e: e.activation(out=s3, in_=s2, func=AF.Sqrt))
        p.wait("vector", SEM_ACT, av)
        v_op(lambda e: e.reciprocal(out=s2, in_=s3))
        vpad()
        v_op(lambda e: e.tensor_scalar(out=dst, in0=src_t, scalar1=s1, scalar2=s2,
                                       op0=OP.subtract, op1=OP.mult))

    def rms_rstd(src_t, rstd_col):
        s2, s3 = stat_sb[:, 4:5], stat_sb[:, 5:6]
        t2 = bigA[:, 4096:5120]
        av = s_op(lambda e: e.activation(out=t2, in_=src_t, func=AF.Square))
        p.wait("vector", SEM_ACT, av)
        v_op(lambda e: e.tensor_reduce(out=s2, in_=t2, axis=AX.X, op=OP.add))
        vpad()
        v_op(lambda e: e.tensor_scalar(out=s2, in0=s2, scalar1=1.0 / DM, scalar2=EPS,
                                       op0=OP.mult, op1=OP.add))
        dv = p.val(SEM_DVE)
        p.wait("scalar", SEM_DVE, dv)
        av = s_op(lambda e: e.activation(out=s3, in_=s2, func=AF.Sqrt))
        p.wait("vector", SEM_ACT, av)
        v_op(lambda e, r=rstd_col: e.reciprocal(out=r, in_=s3))
        vpad()

    NCH = [512, 512, 512, 512, 512, 256]

    def ffn_call(dump=False):
        p.wait("sync", SEM_OUT, p.val(SEM_OUT))
        for th in range(2):
            for tt in range(8):
                ti = th * 8 + tt
                src_t = stream[:, ti * DM:(ti + 1) * DM]
                xln = bigA[:, 5120:6144]
                ln_tile(src_t, xln)
                xq = bigB[:, 8192:9216]
                p.wait("vector", SEM_MM, p.val(SEM_MM))
                if dump:
                    dvx = p.val(SEM_DVE)
                    p.wait("sync", SEM_DVE, dvx)
                    dma_out(xln_dbg[ti], xln)
                quant(xq, xln, DM, 0, 9216)
                dv = p.val(SEM_DVE)
                p.wait("tensor", SEM_DVE, dv)
                if dump:
                    p.wait("sync", SEM_DVE, dv)
                    dma_out(xq_dbg[ti], xq)
                transpose_tiles(xq, 8,
                                lambda t, tt=tt: bigB[:, t * 1024 + tt * 128:
                                                      t * 1024 + tt * 128 + 128])
            xqT = bigB[:, 0:8192].rearrange("p (a b) -> p a b", b=1024)
            avq = p.val(SEM_ACT)

            hq_base = 8192
            for ci, ch in enumerate(NCH):
                c0 = sum(NCH[:ci])
                wgc = bigA[:, 7168:7168 + 2048].bitcast(BF16)[:, 0:8 * ch] \
                    .rearrange("p (a b) -> p a b", b=ch)
                wuc = bigA[:, 9216:9216 + 2048].bitcast(BF16)[:, 0:8 * ch] \
                    .rearrange("p (a b) -> p a b", b=ch)
                p.wait("sync", SEM_MM, p.val(SEM_MM))
                dma_in(wgc, wgT_q[:, c0:c0 + ch].rearrange("(a p) b -> p a b", p=128))
                l2 = dma_in(wuc, wuT_q[:, c0:c0 + ch].rearrange("(a p) b -> p a b", p=128))
                p.wait("tensor", SEM_IN, l2)
                p.wait("tensor", SEM_ACT, avq)
                for tt in range(8):
                    pg, pu = ps[0], ps[1]
                    bank_wait("tensor", pg)
                    for k in range(8):
                        t_op(lambda e, k=k, tt=tt, ch=ch, pg=pg, wgc=wgc: e.matmul(
                            pg[:, 0:ch], xqT[:, k, tt * 128:(tt + 1) * 128],
                            wgc[:, k, :], start=(k == 0), stop=(k == 7)))
                    bank_wait("tensor", pu)
                    for k in range(8):
                        t_op(lambda e, k=k, tt=tt, ch=ch, pu=pu, wuc=wuc: e.matmul(
                            pu[:, 0:ch], xqT[:, k, tt * 128:(tt + 1) * 128],
                            wuc[:, k, :], start=(k == 0), stop=(k == 7)))
                    mmv = p.val(SEM_MM)
                    p.wait("scalar", SEM_MM, mmv)
                    hsil = bigA[:, 6144:6656]
                    hch = bigA[:, 6656:7168]
                    p.wait("scalar", SEM_DVE, p.val(SEM_DVE))
                    av = s_op(lambda e, pg=pg, ch=ch, hsil=hsil: e.activation(
                        out=hsil[:, 0:ch], in_=pg[:, 0:ch], func=AF.Silu))
                    bank_mark(pg, SEM_ACT)
                    p.wait("vector", SEM_ACT, av)
                    v_op(lambda e, pu=pu, ch=ch, hsil=hsil, hch=hch: e.tensor_tensor(
                        out=hch[:, 0:ch], in0=hsil[:, 0:ch], in1=pu[:, 0:ch],
                        op=OP.mult))
                    bank_mark(pu, SEM_DVE)
                    if dump:
                        dvh = p.val(SEM_DVE)
                        p.wait("sync", SEM_DVE, dvh)
                        dma_out(h_dbg[th * 8 + tt][:, c0:c0 + ch], hch[:, 0:ch])
                    hqc = bigB[:, 30720:30720 + ch]
                    p.wait("vector", SEM_MM, p.val(SEM_MM))
                    quant(hqc, hch[:, 0:ch], ch, 0, 31232)
                    dv = p.val(SEM_DVE)
                    p.wait("tensor", SEM_DVE, dv)
                    transpose_tiles(hqc, ch // 128,
                                    lambda t, c0=c0, tt=tt: bigB[
                                        :, hq_base + (c0 // 128 + t) * 1024 + tt * 128:
                                        hq_base + (c0 // 128 + t) * 1024 + tt * 128 + 128])
            hqT = bigB[:, hq_base:hq_base + 22528].rearrange("p (a b) -> p a b", b=1024)
            avh = p.val(SEM_ACT)

            for dc in range(4):
                wdc = wb_bf16(0, 22 * 256).rearrange("p (a b) -> p a b", b=256)
                p.wait("sync", SEM_MM, p.val(SEM_MM))
                ld = dma_in(wdc, wdT_q[:, dc * 256:(dc + 1) * 256]
                            .rearrange("(a p) b -> p a b", p=128))
                p.wait("tensor", SEM_IN, ld)
                p.wait("tensor", SEM_ACT, avh)
                for tt in range(8):
                    ti = th * 8 + tt
                    pb = ps[2 + (tt % 2)]
                    bank_wait("tensor", pb)
                    for k in range(22):
                        t_op(lambda e, k=k, tt=tt, pb=pb, wdc=wdc: e.matmul(
                            pb[:, 0:256], hqT[:, k, tt * 128:(tt + 1) * 128],
                            wdc[:, k, :], start=(k == 0), stop=(k == 21)))
                    mmv = p.val(SEM_MM)
                    p.wait("vector", SEM_MM, mmv)
                    dst = stream[:, ti * DM + dc * 256:ti * DM + (dc + 1) * 256]
                    v_op(lambda e, pb=pb, dst=dst: e.scalar_tensor_tensor(
                        out=dst, in0=pb[:, 0:256], scalar=0.5, in1=dst,
                        op0=OP.mult, op1=OP.add))
                    bank_mark(pb, SEM_DVE)

    def mamba_layer(li):
        xn = bigB[:, 0:16384].rearrange("p (a b) -> p a b", b=DM)
        rstd = stat_sb[:, 6:7]
        p.wait("scalar", SEM_DVE, p.val(SEM_DVE))
        p.wait("sync", SEM_DVE, p.val(SEM_DVE))
        for i in range(NT):
            src_t = stream[:, i * DM:(i + 1) * DM]
            rms_rstd(src_t, rstd)
            v_op(lambda e, s=src_t, i=i: e.tensor_single_scalar(
                out=xn[:, i, :], in_=s, scalar=rstd, op=OP.mult))
            dma_out(h_sp[i], src_t)
        hspv = p.val(SEM_OUT)
        dv = p.val(SEM_DVE)
        p.wait("tensor", SEM_DVE, dv)
        p.wait("scalar", SEM_OUT, hspv)

        xnT = stream_b[:, 0:16384].rearrange("p (a b) -> p a b", b=2048)
        for i in range(NT):
            transpose_tiles(xn[:, i, :], 8,
                            lambda t, i=i: xnT[:, t, i * 128:(i + 1) * 128])
        avT = p.val(SEM_ACT)
        p.wait("tensor", SEM_ACT, avT)

        xs = bigB[:, 0:16 * 2048].rearrange("p (a b) -> p a b", b=2048)
        wic_f = bigA[:, 0:1024].rearrange("p (a b) -> p a b", b=128)
        wic = wb_bf16(2048, 8 * 128).rearrange("p (a b) -> p a b", b=128)
        rtmp = bigA[:, 1024:1536]
        for mt in range(32):
            p.wait("sync", SEM_MM, p.val(SEM_MM))
            ld = dma_in(wic_f, wiT_h[li][:, mt * 128:(mt + 1) * 128]
                        .rearrange("(a p) b -> p a b", p=128))
            p.wait("vector", SEM_IN, ld)
            v_op(lambda e: e.tensor_tensor(
                out=wic, in0=wic_f,
                in1=nw_sb[:, li * 8:(li + 1) * 8].rearrange("p (a o) -> p a o", o=1)
                    .broadcast_to([128, 8, 128]),
                op=OP.mult))
            cv = p.val(SEM_DVE)
            p.wait("tensor", SEM_DVE, cv)
            for tb in range(NTB):
                pb = ps[tb % 2]
                bank_wait("tensor", pb)
                for k in range(8):
                    t_op(lambda e, k=k, tb=tb, pb=pb: e.matmul(
                        pb[:], wic[:, k, :], xnT[:, k, tb * 512:(tb + 1) * 512],
                        start=(k == 0), stop=(k == 7)))
                p.wait("scalar", SEM_MM, p.val(SEM_MM))
                bcol = bi_sb[:, li * 32 + mt:li * 32 + mt + 1]
                if mt < 16:
                    s_op(lambda e, pb=pb, mt=mt, tb=tb, bcol=bcol: e.add(
                        out=xs[:, mt, tb * 512:(tb + 1) * 512], in_=pb[:], add=bcol))
                else:
                    p.wait("scalar", SEM_OUT, p.val(SEM_OUT))
                    av = s_op(lambda e, pb=pb, bcol=bcol: e.add(
                        out=rtmp, in_=pb[:], add=bcol))
                    p.wait("sync", SEM_ACT, av)
                    dma_out(res_sp[mt - 16][:, tb * 512:(tb + 1) * 512], rtmp)
                bank_mark(pb, SEM_ACT)

        hpf = bigA[:, 2048:2048 + 64]
        xsv = p.val(SEM_ACT)
        p.wait("vector", SEM_ACT, xsv)
        for dt in range(NDT):
            v_op(lambda e, dt=dt: e.tensor_copy(
                out=hpf[:, dt * 4:dt * 4 + 2], in_=xs[:, dt, 0:2]))
            v_op(lambda e, dt=dt: e.tensor_copy(
                out=hpf[:, dt * 4 + 2:dt * 4 + 4], in_=xs[:, dt, 2046:2048]))
        hv = p.val(SEM_DVE)
        p.wait("sync", SEM_DVE, hv)
        gv = dma_out(gin_h, hpf)
        p.wait("gpsimd", SEM_OUT, gv)
        p.op("gpsimd", lambda e: e.collective_compute(
            "AllGather", OP.bypass, replica_groups=[list(range(NCORES))],
            ins=[gin_h], outs=[gout_h]), SEM_CC)
        p.wait("sync", SEM_CC, p.val(SEM_CC))
        gl = bigA[:, 2112:2112 + 512].rearrange("p (a b) -> p a b", b=64)
        lg = dma_in(gl, gout_h.rearrange("c p f -> p c f"))
        p.wait("vector", SEM_IN, lg)
        v_op(lambda e: e.memset(halo_sb[:], 0.0))
        vpad()
        vpad()
        hl4 = halo_sb.rearrange("p (a b) -> p a b", b=4)
        for s in range(NCORES):
            v_op(lambda e, s=s: e.scalar_tensor_tensor(
                out=hl4[:, :, 0:2],
                in0=gl[:, s, :].rearrange("p (a b) -> p a b", b=4)[:, :, 2:4],
                scalar=oh_sb[:, 8 + s:9 + s],
                in1=hl4[:, :, 0:2],
                op0=OP.mult, op1=OP.add))
            v_op(lambda e, s=s: e.scalar_tensor_tensor(
                out=hl4[:, :, 2:4],
                in0=gl[:, s, :].rearrange("p (a b) -> p a b", b=4)[:, :, 0:2],
                scalar=oh_sb[:, 16 + s:17 + s],
                in1=hl4[:, :, 2:4],
                op0=OP.mult, op1=OP.add))
        halo_b = wb_bf16(1280, NDT * 4).rearrange("p (a b) -> p a b", b=4)
        vpad()
        vpad()
        v_op(lambda e: e.tensor_copy(out=halo_b.rearrange("p a b -> p (a b)"),
                                     in_=halo_sb[:]))
        hv2 = p.val(SEM_DVE)
        p.wait("tensor", SEM_DVE, hv2)

        ub = bigA[:, 0:4096].bitcast(BF16).rearrange("p (a b) -> p a b", b=512)
        uf = bigA[:, 4096:4608]
        r_t = bigA[:, 4608:5120]
        m_t = bigA[:, 5120:5632]
        e_t = bigA[:, 5632:6144]
        d_t = bigA[:, 6144:6656]
        w_t = bigA[:, 6656:7168]
        y_t = bigA[:, 7168:7680]
        ur_t = bigA[:, 7680:8192]
        diag = wb_bf16(0, 5 * 128).rearrange("p (a b) -> p a b", b=128)
        wxc_f = wbuf[:, 1536:1536 + 1536].rearrange("p (a b) -> p a b", b=96)
        wxc = wb_bf16(512, 16 * 96).rearrange("p (a b) -> p a b", b=96)
        ldx = dma_in(wxc_f, wxT_h[li].rearrange("(a p) b -> p a b", p=128))
        p.wait("vector", SEM_IN, ldx)
        v_op(lambda e: e.tensor_copy(out=wxc.rearrange("p a b -> p (a b)"),
                                     in_=wxc_f.rearrange("p a b -> p (a b)")))
        wxv = p.val(SEM_DVE)
        p.wait("tensor", SEM_DVE, wxv)
        xdbl = stream[0:96, 8192:10240]
        dtw = wbuf[0:64, 1536:1536 + 128]
        cdT = scol[:, 768:784]
        for tb in range(NTB):
            for dt in range(NDT):
                p.wait("vector", SEM_MM, p.val(SEM_MM))
                for j in range(5):
                    v_op(lambda e, j=j, dt=dt: e.tensor_single_scalar(
                        out=diag[:, j, :], in_=identf[:],
                        scalar=cw_sb[:, (li * NDT + dt) * 5 + j:
                                     (li * NDT + dt) * 5 + j + 1], op=OP.mult))
                dv = p.val(SEM_DVE)
                p.wait("tensor", SEM_DVE, dv)
                pb = ps[dt % 2]
                bank_wait("tensor", pb)
                segs = []
                for j in range(5):
                    lo = tb * 512 + j - 2
                    hi = lo + 512
                    if lo < 0:
                        nh = -lo
                        segs.append((j, pb[:, 0:nh], halo_b[:, dt, 2 - nh:2]))
                        segs.append((j, pb[:, nh:512], xs[:, dt, 0:512 - nh]))
                    elif hi > 2048:
                        nh = hi - 2048
                        segs.append((j, pb[:, 0:512 - nh], xs[:, dt, lo:2048]))
                        segs.append((j, pb[:, 512 - nh:512], halo_b[:, dt, 2:2 + nh]))
                    else:
                        segs.append((j, pb[:], xs[:, dt, lo:lo + 512]))
                for si, (j, pslice, mov) in enumerate(segs):
                    t_op(lambda e, j=j, pslice=pslice, mov=mov, si=si,
                         last=(si == len(segs) - 1): e.matmul(
                        pslice, diag[:, j, :], mov, start=(si == 0), stop=last))
                p.wait("scalar", SEM_MM, p.val(SEM_MM))
                ccol = cb_sb[:, li * NDT + dt:li * NDT + dt + 1]
                p.wait("scalar", SEM_OUT, p.val(SEM_OUT))
                p.wait("scalar", SEM_DVE, p.val(SEM_DVE))
                av = s_op(lambda e, pb=pb, ccol=ccol: e.activation(
                    out=uf, in_=pb[:], func=AF.Silu, bias=ccol))
                bank_mark(pb, SEM_ACT)
                p.wait("vector", SEM_ACT, av)
                v_op(lambda e, dt=dt: e.tensor_copy(out=ub[:, dt, :], in_=uf))
                p.wait("sync", SEM_ACT, av)
                dma_out(u_sp[dt][:, tb * 512:(tb + 1) * 512], uf)
            p.wait("sync", SEM_OUT, p.val(SEM_OUT))
            dv = p.val(SEM_DVE)
            p.wait("tensor", SEM_DVE, dv)
            pbx = ps[2]
            bank_wait("tensor", pbx)
            for k in range(NDT):
                t_op(lambda e, k=k, pbx=pbx: e.matmul(
                    pbx[0:96, :], wxc[:, k, :], ub[:, k, :],
                    start=(k == 0), stop=(k == NDT - 1)))
            mv = p.val(SEM_MM)
            p.wait("scalar", SEM_MM, mv)
            av = s_op(lambda e, pbx=pbx, tb=tb: e.copy(
                out=xdbl[:, tb * 512:(tb + 1) * 512], in_=pbx[0:96, :]))
            bank_mark(pbx, SEM_ACT)
            p.wait("tensor", SEM_ACT, av)
            for mt in range(NDT):
                p.wait("sync", SEM_MM, p.val(SEM_MM))
                p.wait("sync", SEM_DVE, p.val(SEM_DVE))
                ldt = dma_in(dtw, wdtT_h[li][:, mt * 128:(mt + 1) * 128])
                dma_in(ur_t, u_sp[mt][:, tb * 512:(tb + 1) * 512])
                p.wait("tensor", SEM_IN, ldt)
                pb2 = ps[3]
                bank_wait("tensor", pb2)
                t_op(lambda e, tb=tb, pb2=pb2: e.matmul(
                    pb2[:], dtw, xdbl[0:64, tb * 512:(tb + 1) * 512],
                    start=True, stop=True))
                mv = p.val(SEM_MM)
                p.wait("scalar", SEM_MM, mv)
                p.wait("vector", SEM_MM, mv)
                bcol = dtb_sb[:, li * NDT + mt:li * NDT + mt + 1]
                p.wait("scalar", SEM_OUT, p.val(SEM_OUT))
                p.wait("vector", SEM_OUT, p.val(SEM_OUT))
                av = s_op(lambda e, pb2=pb2, bcol=bcol: e.activation(
                    out=r_t, in_=pb2[:], func=AF.Relu, bias=bcol))
                p.wait("vector", SEM_ACT, p.val(SEM_ACT))
                v_op(lambda e, pb2=pb2: e.scalar_tensor_tensor(
                    out=m_t, in0=r_t, scalar=2.0, in1=pb2[:],
                    op0=OP.mult, op1=OP.subtract))
                v_op(lambda e, bcol=bcol: e.tensor_single_scalar(
                    out=m_t, in_=m_t, scalar=bcol, op=OP.subtract))
                bank_mark(pb2, SEM_DVE)
                dv = p.val(SEM_DVE)
                p.wait("scalar", SEM_DVE, dv)
                av = s_op(lambda e: e.activation(out=e_t, in_=m_t, func=AF.Exp,
                                                 scale=-1.0))
                p.wait("vector", SEM_ACT, av)
                v_op(lambda e: e.tensor_single_scalar(out=e_t, in_=e_t, scalar=1.0,
                                                      op=OP.add))
                dv = p.val(SEM_DVE)
                p.wait("scalar", SEM_DVE, dv)
                av = s_op(lambda e: e.activation(out=m_t, in_=e_t, func=AF.Ln))
                p.wait("vector", SEM_ACT, av)
                p.wait("vector", SEM_IN, p.val(SEM_IN))
                v_op(lambda e: e.tensor_tensor(out=d_t, in0=r_t, in1=m_t, op=OP.add))
                v_op(lambda e: e.tensor_reduce(out=stat_sb[:, 8:9], in_=d_t,
                                               axis=AX.X, op=OP.add))
                vpad()
                if tb == 0:
                    v_op(lambda e, mt=mt: e.tensor_copy(out=cdT[:, mt:mt + 1],
                                                        in_=stat_sb[:, 8:9]))
                else:
                    v_op(lambda e, mt=mt: e.tensor_tensor(
                        out=cdT[:, mt:mt + 1], in0=cdT[:, mt:mt + 1],
                        in1=stat_sb[:, 8:9], op=OP.add))
                v_op(lambda e: e.tensor_tensor(out=w_t, in0=d_t,
                                                   in1=ur_t, op=OP.mult))
                dcol = d_sb[:, li * NDT + mt:li * NDT + mt + 1]
                v_op(lambda e, dcol=dcol: e.tensor_single_scalar(
                    out=y_t, in_=ur_t, scalar=dcol, op=OP.mult))
                dv = p.val(SEM_DVE)
                p.wait("sync", SEM_DVE, dv)
                dma_out(del_sp[mt][:, tb * 512:(tb + 1) * 512], d_t)
                dma_out(w_sp[mt][:, tb * 512:(tb + 1) * 512], w_t)
                dma_out(yd_sp[mt][:, tb * 512:(tb + 1) * 512], y_t)
        p.wait("sync", SEM_ACT, p.val(SEM_ACT))
        bcv = dma_out(bc_sp, xdbl[64:96, :])
        p.wait("sync", SEM_OUT, p.val(SEM_OUT))

        reps = stream
        dlt = bigA[:, 0:2048]
        wt = bigA[:, 2048:4096]
        yt = bigA[:, 4096:6144]
        at = bigA[:, 6144:8192]
        zt = bigA[:, 8192:10240]
        ht = bigA[:, 10240:12288]
        Mv = scol[:, 0:256]
        hend = scol[:, 256:512]
        hin = scol[:, 512:768]
        tc = scol[:, 800:1056]
        for ng in range(4):
            p.wait("sync", SEM_GP, p.val(SEM_GP))
            p.wait("sync", SEM_DVE, p.val(SEM_DVE))
            p.wait("sync", SEM_MM, p.val(SEM_MM))
            p.wait("sync", SEM_OUT, p.val(SEM_OUT))
            rl = None
            for j in range(4):
                n = ng * 4 + j
                dma_in(reps[:, (2 * j) * 2048:(2 * j + 1) * 2048],
                       bc_sp[n:n + 1, :].partition_broadcast(128))
                rl = dma_in(reps[:, (2 * j + 1) * 2048:(2 * j + 2) * 2048],
                            bc_sp[16 + n:17 + n, :].partition_broadcast(128))
            p.wait("vector", SEM_IN, rl)
            p.wait("gpsimd", SEM_IN, rl)
            for dt in range(NDT):
                p.wait("sync", SEM_DVE, p.val(SEM_DVE))
                p.wait("sync", SEM_GP, p.val(SEM_GP))
                p.wait("sync", SEM_ACT, p.val(SEM_ACT))
                dma_in(dlt, del_sp[dt])
                l2 = dma_in(wt, w_sp[dt])
                ysrc = yd_sp[dt] if ng == 0 else y_sp[dt]
                l3 = dma_in(yt, ysrc)
                p.wait("scalar", SEM_IN, l2)
                p.wait("gpsimd", SEM_IN, l2)
                p.wait("vector", SEM_IN, l3)
                for j in range(4):
                    n = ng * 4 + j
                    acol = a_sb[:, (li * NDT + dt) * NST + n:
                                (li * NDT + dt) * NST + n + 1]
                    p.wait("scalar", SEM_DVE, p.val(SEM_DVE))
                    av = s_op(lambda e, acol=acol: e.activation(
                        out=at, in_=dlt, func=AF.Exp, scale=acol))
                    p.wait("gpsimd", SEM_DVE, p.val(SEM_DVE))
                    gvz = g_op(lambda e, j=j: e.tensor_tensor(
                        out=zt, in0=wt, in1=reps[:, (2 * j) * 2048:(2 * j + 1) * 2048],
                        op=OP.mult))
                    p.wait("vector", SEM_ACT, av)
                    p.wait("vector", SEM_GP, gvz)
                    v_op(lambda e: e.tensor_tensor_scan(
                        out=ht, data0=at, data1=zt, initial=0.0,
                        op0=OP.mult, op1=OP.add))
                    v_op(lambda e, j=j: e.tensor_tensor(
                        out=at, in0=ht,
                        in1=reps[:, (2 * j + 1) * 2048:(2 * j + 2) * 2048],
                        op=OP.mult))
                    v_op(lambda e: e.tensor_tensor(out=yt, in0=yt, in1=at, op=OP.add))
                    v_op(lambda e, dt=dt, n=n: e.tensor_copy(
                        out=hend[:, dt * 16 + n:dt * 16 + n + 1],
                        in_=ht[:, 2047:2048]))
                    if dbg and li == 0 and ng == 0 and dt == 0 and j == 0:
                        dvd = p.val(SEM_DVE)
                        p.wait("sync", SEM_DVE, dvd)
                        dma_out(scan_dbg[0], reps[:, 0:2048])
                        dma_out(scan_dbg[1], zt)
                        dma_out(scan_dbg[2], reps[:, 2048:4096])
                        dmp = dma_out(scan_dbg[3], wt)
                        p.wait("gpsimd", SEM_OUT, dmp)
                        p.wait("scalar", SEM_OUT, dmp)
                        p.wait("vector", SEM_OUT, dmp)
                dv = p.val(SEM_DVE)
                p.wait("sync", SEM_DVE, dv)
                dma_out(y_sp[dt], yt)

        p.wait("scalar", SEM_DVE, p.val(SEM_DVE))
        for dt in range(NDT):
            s_op(lambda e, dt=dt: e.activation(
                out=Mv[:, dt * 16:(dt + 1) * 16],
                in_=a_sb[:, (li * NDT + dt) * NST:(li * NDT + dt + 1) * NST],
                func=AF.Exp, scale=cdT[:, dt:dt + 1]))
        sv = p.val(SEM_ACT)
        p.wait("sync", SEM_ACT, sv)
        p.wait("sync", SEM_DVE, p.val(SEM_DVE))
        dma_out(gin_s[:, 0:256], Mv)
        g2 = dma_out(gin_s[:, 256:512], hend)
        p.wait("gpsimd", SEM_OUT, g2)
        p.op("gpsimd", lambda e: e.collective_compute(
            "AllGather", OP.bypass, replica_groups=[list(range(NCORES))],
            ins=[gin_s], outs=[gout_s]), SEM_CC)
        p.wait("sync", SEM_CC, p.val(SEM_CC))
        gat = bigA[:, 0:4096].rearrange("p (a b) -> p a b", b=512)
        lg = dma_in(gat, gout_s.rearrange("c p f -> p c f"))
        p.wait("vector", SEM_IN, lg)
        v_op(lambda e: e.memset(hin, 0.0))
        v_op(lambda e: e.memset(tc, 0.0))
        for s in range(NCORES):
            if s % 4 == 0:
                v_op(lambda e: e.memset(tc, 0.0))
            else:
                v_op(lambda e, s=s: e.tensor_tensor(
                    out=tc, in0=tc, in1=gat[:, s - 1, 0:256], op=OP.mult))
                v_op(lambda e, s=s: e.tensor_tensor(
                    out=tc, in0=tc, in1=gat[:, s - 1, 256:512], op=OP.add))
                v_op(lambda e, s=s: e.scalar_tensor_tensor(
                    out=hin, in0=tc, scalar=oh_sb[:, s:s + 1], in1=hin,
                    op0=OP.mult, op1=OP.add))

        zeros_t = bigA[:, 2048:4096]
        cdt_t = bigA[:, 8192:10240]
        p.wait("vector", SEM_IN, p.val(SEM_IN))
        v_op(lambda e: e.memset(zeros_t, 0.0))
        for ng in range(4):
            p.wait("sync", SEM_DVE, p.val(SEM_DVE))
            p.wait("sync", SEM_OUT, p.val(SEM_OUT))
            rl = None
            for j in range(4):
                n = ng * 4 + j
                rl = dma_in(reps[:, (2 * j + 1) * 2048:(2 * j + 2) * 2048],
                            bc_sp[16 + n:17 + n, :].partition_broadcast(128))
            p.wait("vector", SEM_IN, rl)
            for dt in range(NDT):
                p.wait("sync", SEM_DVE, p.val(SEM_DVE))
                p.wait("sync", SEM_ACT, p.val(SEM_ACT))
                dma_in(dlt, del_sp[dt])
                l3 = dma_in(yt, y_sp[dt])
                p.wait("vector", SEM_IN, l3)
                p.wait("vector", SEM_ACT, p.val(SEM_ACT))
                v_op(lambda e: e.tensor_tensor_scan(
                    out=cdt_t, data0=dlt, data1=zeros_t, initial=0.0,
                    op0=OP.add, op1=OP.add))
                for j in range(4):
                    n = ng * 4 + j
                    acol = a_sb[:, (li * NDT + dt) * NST + n:
                                (li * NDT + dt) * NST + n + 1]
                    p.wait("scalar", SEM_DVE, p.val(SEM_DVE))
                    av = s_op(lambda e, acol=acol: e.activation(
                        out=at, in_=cdt_t, func=AF.Exp, scale=acol))
                    p.wait("vector", SEM_ACT, av)
                    hcol = hin[:, dt * 16 + n:dt * 16 + n + 1]
                    v_op(lambda e, j=j, hcol=hcol: e.scalar_tensor_tensor(
                        out=at, in0=at, scalar=hcol,
                        in1=reps[:, (2 * j + 1) * 2048:(2 * j + 2) * 2048],
                        op0=OP.mult, op1=OP.mult))
                    v_op(lambda e: e.tensor_tensor(out=yt, in0=yt, in1=at, op=OP.add))
                dv = p.val(SEM_DVE)
                p.wait("sync", SEM_DVE, dv)
                dma_out(y_sp[dt], yt)

        y_g = bigB[:, 0:16 * 2048].rearrange("p (a b) -> p a b", b=2048)
        p.wait("sync", SEM_OUT, p.val(SEM_OUT))
        for dt in range(NDT):
            p.wait("sync", SEM_DVE, p.val(SEM_DVE))
            p.wait("sync", SEM_ACT, p.val(SEM_ACT))
            dma_in(yt, y_sp[dt])
            l2 = dma_in(wt, res_sp[dt])
            p.wait("scalar", SEM_IN, l2)
            av = s_op(lambda e: e.activation(out=at, in_=wt, func=AF.Silu))
            p.wait("vector", SEM_ACT, av)
            v_op(lambda e: e.tensor_tensor(out=ht, in0=yt, in1=at, op=OP.mult))
            v_op(lambda e, dt=dt: e.tensor_copy(out=y_g[:, dt, :], in_=ht))
        ygv = p.val(SEM_DVE)
        p.wait("tensor", SEM_DVE, ygv)

        woc = bigA[:, 0:8192].bitcast(BF16).rearrange("p (a b) -> p a b", b=1024)
        wof = bigA[:, 8192:10240]
        for dt2 in range(8):
            p.wait("sync", SEM_DVE, p.val(SEM_DVE))
            ld = dma_in(wof.rearrange("p (a b) -> p a b", b=DM),
                        woT_h[li][dt2 * 256:(dt2 + 1) * 256, :]
                        .rearrange("(a p) b -> p a b", p=128))
            p.wait("vector", SEM_IN, ld)
            v_op(lambda e, dt2=dt2: e.tensor_copy(
                out=woc[:, 2 * dt2:2 * dt2 + 2, :].rearrange("p a b -> p (a b)"),
                in_=wof))
        wov = p.val(SEM_DVE)
        p.wait("tensor", SEM_DVE, wov)
        p.wait("sync", SEM_OUT, p.val(SEM_OUT))
        for i in range(NT):
            p.wait("sync", SEM_DVE, p.val(SEM_DVE))
            hld = dma_in(wof[:, 0:1024], h_sp[i])
            for c in range(2):
                pb = ps[c]
                bank_wait("tensor", pb)
                for k in range(NDT):
                    t_op(lambda e, k=k, i=i, c=c, pb=pb: e.matmul(
                        pb[:], y_g[:, k, i * 128:(i + 1) * 128],
                        woc[:, k, c * 512:(c + 1) * 512],
                        start=(k == 0), stop=False))
                t_op(lambda e, c=c, pb=pb: e.matmul(
                    pb[:], onesr[0:1, 0:128],
                    bo_sb[0:1, li * DM + c * 512:li * DM + (c + 1) * 512],
                    start=False, stop=True))
            mv = p.val(SEM_MM)
            p.wait("vector", SEM_MM, mv)
            p.wait("vector", SEM_IN, hld)
            for c in range(2):
                pb = ps[c]
                v_op(lambda e, i=i, c=c, pb=pb: e.tensor_tensor(
                    out=stream[:, i * DM + c * 512:i * DM + (c + 1) * 512],
                    in0=wof[:, c * 512:(c + 1) * 512], in1=pb[:], op=OP.add))
                bank_mark(pb, SEM_DVE)
        if dbg:
            p.wait("sync", SEM_DVE, p.val(SEM_DVE))
            for i2 in range(NT):
                dma_out(xln_dbg[i2], stream[:, i2 * DM:(i2 + 1) * DM])
            p.wait("gpsimd", SEM_OUT, p.val(SEM_OUT))
            p.wait("vector", SEM_OUT, p.val(SEM_OUT))
            p.wait("scalar", SEM_OUT, p.val(SEM_OUT))

    ffn_call(dump=dbg)
    p.wait("sync", SEM_DVE, p.val(SEM_DVE))
    for i in range(NT):
        src_t = stream[:, i * DM:(i + 1) * DM]
        xv = dma_out(x_sp[i], src_t)
        xln = bigA[:, 5120:6144]
        ln_tile(src_t, xln)
        p.wait("vector", SEM_OUT, xv)
        v_op(lambda e, s=src_t, xln=xln: e.tensor_copy(out=s, in_=xln))
    for li in range(nlay):
        mamba_layer(li)
    rstd2 = stat_sb[:, 7:8]
    xr_t = bigA[:, 6144:7168]
    nfw_sb = bigA[:, 7168:8192]
    p.wait("sync", SEM_DVE, p.val(SEM_DVE))
    p.wait("sync", SEM_ACT, p.val(SEM_ACT))
    p.wait("sync", SEM_OUT, p.val(SEM_OUT))
    nfl = dma_in(nfw_sb, nfw_h.partition_broadcast(128))
    p.wait("vector", SEM_IN, nfl)
    p.wait("scalar", SEM_DVE, p.val(SEM_DVE))
    for i in range(NT):
        src_t = stream[:, i * DM:(i + 1) * DM]
        rms_rstd(src_t, rstd2)
        t1 = bigA[:, 5120:6144]
        v_op(lambda e, s=src_t, t1=t1: e.scalar_tensor_tensor(
            out=t1, in0=s, scalar=rstd2, in1=nfw_sb, op0=OP.mult, op1=OP.mult))
        p.wait("sync", SEM_DVE, p.val(SEM_DVE))
        xld = dma_in(xr_t, x_sp[i])
        p.wait("vector", SEM_IN, xld)
        vpad()
        v_op(lambda e, s=src_t, t1=t1: e.tensor_tensor(
            out=s, in0=t1, in1=xr_t, op=OP.add))
    ffn_call()
    fv = p.val(SEM_DVE)
    p.wait("sync", SEM_DVE, fv)
    ov = dma_out(out_h.rearrange("(a p) b -> p a b", p=128), stream.rearrange("p (a b) -> p a b", b=DM))
    p.wait("sync", SEM_OUT, ov)

    with nc.Block() as block:
        p.replay(block)
    return nc


_NC_CACHE = [None]


def _stage_inputs(inputs):
    x = np.asarray(inputs["L0_attn_x"], np.float32)
    ipw = np.asarray(inputs["L0_attn_in_proj_w"], np.float32)
    ipb = np.asarray(inputs["L0_attn_in_proj_b"], np.float32)
    cw = np.asarray(inputs["L0_attn_conv_w"], np.float32)
    cb = np.asarray(inputs["L0_attn_conv_b"], np.float32)
    xpw = np.asarray(inputs["L0_attn_x_proj_w"], np.float32)
    dtw = np.asarray(inputs["L0_attn_dt_proj_w"], np.float32)
    dtb = np.asarray(inputs["L0_attn_dt_proj_b"], np.float32)
    alog = np.asarray(inputs["L0_attn_A_log"], np.float32)
    dd = np.asarray(inputs["L0_attn_D"], np.float32)
    opw = np.asarray(inputs["L0_attn_out_proj_w"], np.float32)
    opb = np.asarray(inputs["L0_attn_out_proj_b"], np.float32)
    nw = np.asarray(inputs["L0_attn_norm_w"], np.float32)
    nfw = np.asarray(inputs["L0_attn_norm_f_w"], np.float32)
    wg = np.asarray(inputs["L0_ffn_w_gate"], np.float32)
    wu = np.asarray(inputs["L0_ffn_w_up"], np.float32)
    wd = np.asarray(inputs["L0_ffn_w_down"], np.float32)

    def colmap(a):
        L = a.shape[0]
        G = a.shape[1] // 128
        return np.ascontiguousarray(a.reshape(L, G, 128).transpose(2, 0, 1)
                                    .reshape(128, L * G))

    shared = {
        "wiT": np.ascontiguousarray(ipw.transpose(0, 2, 1)),
        "wxT": np.ascontiguousarray(xpw.transpose(0, 2, 1)),
        "wdtT": np.ascontiguousarray(dtw.transpose(0, 2, 1)),
        "woT": np.ascontiguousarray(opw.transpose(0, 2, 1)),
        "bi_c": colmap(ipb),
        "cw_c": np.ascontiguousarray(cw.reshape(NLAYER, NDT, 128, 5)
                                     .transpose(2, 0, 1, 3).reshape(128, -1)),
        "cb_c": colmap(cb), "dtb_c": colmap(dtb),
        "alog_c": np.ascontiguousarray(alog.reshape(NLAYER, NDT, 128, NST)
                                       .transpose(2, 0, 1, 3).reshape(128, -1)),
        "d_c": colmap(dd), "nw_c": colmap(nw),
        "nfw_r": np.ascontiguousarray(nfw.reshape(1, DM)),
        "bo_r": np.ascontiguousarray(opb.reshape(1, NLAYER * DM)),
        "wg": wg, "wu": wu, "wd": wd,
    }
    in_maps = []
    for c in range(NCORES):
        b, q = c // 4, c % 4
        oh = np.zeros((1, 24), np.float32)
        oh[0, c] = 1.0
        if q > 0:
            oh[0, 8 + (c - 1)] = 1.0
        if q < 3:
            oh[0, 16 + (c + 1)] = 1.0
        m = dict(shared)
        m["x"] = np.ascontiguousarray(x[b, q * TAU:(q + 1) * TAU, :])
        m["oh"] = oh
        in_maps.append(m)
    return in_maps


def kernel(**inputs):
    if _NC_CACHE[0] is None:
        _NC_CACHE[0] = build_program()
    nc = _NC_CACHE[0]
    in_maps = _stage_inputs(inputs)
    res = run_bass_kernel_spmd(nc, in_maps, core_ids=list(range(NCORES)))
    out = np.empty((2, 8192, DM), np.float32)
    for c in range(NCORES):
        b, q = c // 4, c % 4
        out[b, q * TAU:(q + 1) * TAU, :] = res.results[c]["out"]
    return out


# version 05
# speedup vs baseline: 1.1912x; 1.1912x over previous
import numpy as np
import concourse.bass as bass
import concourse.mybir as mybir
from concourse.bass_utils import run_bass_kernel_spmd

F32 = mybir.dt.float32
BF16 = mybir.dt.bfloat16
AF = mybir.ActivationFunctionType
OP = mybir.AluOpType
AX = mybir.AxisListType

NCORES = 8
TAU = 2048
NT = 16
DM = 1024
DI = 2048
NDT = 16
NST = 16
DTR = 64
II = 2816
NIT = 22
NLAYER = 2
NTB = 4
MIDS = [0.25, 0.75, 1.25, 1.75, 2.5, 3.5, 5.0]
EPS = 1e-5


class Prog:
    ENGINES = ("sync", "vector", "scalar", "tensor", "gpsimd")

    def __init__(self, nc):
        self.nc = nc
        self.q = {e: [] for e in self.ENGINES}
        self.sems = {}
        self.cnt = {}

    def sem(self, name):
        if name not in self.sems:
            self.sems[name] = self.nc.alloc_semaphore(name=name)
            self.cnt[name] = 0
        return self.sems[name]

    def op(self, eng, fn, sem=None, amt=1):
        s = self.sem(sem) if sem else None
        if sem:
            self.cnt[sem] += amt

        def run(e, fn=fn, s=s, amt=amt):
            ins = fn(e)
            if s is not None:
                ins.then_inc(s, amt)
        self.q[eng].append(run)
        return self.cnt[sem] if sem else None

    def wait(self, eng, sem, val=None):
        s = self.sem(sem)
        v = self.cnt[sem] if val is None else val

        def run(e, s=s, v=v):
            e.wait_ge(s, v)
        self.q[eng].append(run)

    def val(self, sem):
        return self.cnt.get(sem, 0)

    def replay(self, block):
        for name in self.ENGINES:
            dec = getattr(block, name)
            ops = self.q[name]

            def body(e, ops=ops):
                for f in ops:
                    f(e)
            dec(body)


SEM_IN, SEM_MM, SEM_ACT, SEM_DVE, SEM_GP, SEM_OUT, SEM_CC = (
    "s_in", "s_mm", "s_act", "s_dve", "s_gp", "s_out", "s_cc")


def build_program(dbg=False, nlay=NLAYER):
    nc = bass.Bass("TRN2", target_bir_lowering=False, debug=False)
    nc.num_devices = NCORES
    p = Prog(nc)

    def din(name, shape, dtype=F32):
        return nc.dram_tensor(name, shape, dtype, kind="ExternalInput").ap()

    x_in = din("x", [TAU, DM])
    wiT_h = din("wiT", [NLAYER, DM, 2 * DI])
    wxT_h = din("wxT", [NLAYER, DI, 96])
    wdtT_h = din("wdtT", [NLAYER, DTR, DI])
    woT_h = din("woT", [NLAYER, DI, DM])
    bi_h = din("bi_c", [128, NLAYER * 32])
    cw_h = din("cw_c", [128, NLAYER * NDT * 5])
    cb_h = din("cb_c", [128, NLAYER * NDT])
    dtb_h = din("dtb_c", [128, NLAYER * NDT])
    alog_h = din("alog_c", [128, NLAYER * NDT * NST])
    dd_h = din("d_c", [128, NLAYER * NDT])
    nw_h = din("nw_c", [128, NLAYER * 8])
    nfw_h = din("nfw_r", [1, DM])
    bo_h = din("bo_r", [1, NLAYER * DM])
    wg_h = din("wg", [II, DM])
    wu_h = din("wu", [II, DM])
    wd_h = din("wd", [DM, II])
    oh_h = din("oh", [1, 24])

    out_h = nc.dram_tensor("out", [TAU, DM], F32, kind="ExternalOutput").ap()

    def scr(name, shape, dtype=F32, force_internal=False):
        kind = "ExternalOutput" if (dbg and not force_internal) else "Internal"
        return nc.dram_tensor(name, shape, dtype, kind=kind).ap()

    wgT_q = scr("wgT_q", [DM, II], BF16)
    wuT_q = scr("wuT_q", [DM, II], BF16)
    wdT_q = scr("wdT_q", [II, DM], BF16)
    x_sp = scr("x_sp", [NT, 128, DM])
    h_sp = scr("h_sp", [NT, 128, DM])
    res_sp = scr("res_sp", [NDT, 128, TAU])
    del_sp = scr("del_sp", [NDT, 128, TAU])
    w_sp = scr("w_sp", [NDT, 128, TAU])
    yd_sp = scr("yd_sp", [NDT, 128, TAU])
    y_sp = scr("y_sp", [NDT, 128, TAU])
    u_sp = scr("u_sp", [NDT, 128, TAU])
    bc_sp = scr("bc_sp", [32, TAU])
    scan_dbg = scr("scan_dbg", [4, 128, TAU])
    xln_dbg = scr("xln_dbg", [NT, 128, DM])
    xq_dbg = scr("xq_dbg", [NT, 128, DM], BF16)
    h_dbg = scr("h_dbg", [NT, 128, II])
    gin_h = scr("gin_h", [128, NDT * 4], force_internal=True)
    gout_h = scr("gout_h", [NCORES, 128, NDT * 4], force_internal=True)
    gin_s = scr("gin_s", [128, 512], force_internal=True)
    gout_s = scr("gout_s", [NCORES, 128, 512], force_internal=True)

    def sball(name, shape, dtype=F32):
        return nc.alloc_sbuf_tensor(name, shape, dtype).ap()

    identf = sball("identf", [128, 128], F32)
    identb = sball("identb", [128, 128], BF16)
    onesr = sball("onesr", [1, 128], F32)
    bi_sb = sball("bi_sb", [128, NLAYER * 32], F32)
    cw_sb = sball("cw_sb", [128, NLAYER * NDT * 5], F32)
    cb_sb = sball("cb_sb", [128, NLAYER * NDT], F32)
    dtb_sb = sball("dtb_sb", [128, NLAYER * NDT], F32)
    a_sb = sball("a_sb", [128, NLAYER * NDT * NST], F32)
    d_sb = sball("d_sb", [128, NLAYER * NDT], F32)
    nw_sb = sball("nw_sb", [128, NLAYER * 8], F32)
    bo_sb = sball("bo_sb", [1, NLAYER * DM], F32)
    oh_sb = sball("oh_sb", [128, 24], F32)
    stat_sb = sball("stat_sb", [128, 16], F32)
    scol = sball("scol", [128, 1088], F32)
    halo_sb = sball("halo_sb", [128, NDT * 4], F32)

    stream = sball("stream", [128, NT * DM], F32)
    bigA = sball("bigA", [128, 12 * 1024], F32)
    bigB = sball("bigB", [128, 32 * 1024], BF16)
    wbuf = sball("wbuf", [128, 3 * 1024], F32)

    ps = [nc.alloc_psum_tensor(f"ps{i}", [128, 512], F32).ap() for i in range(4)]
    pst = [nc.alloc_psum_tensor(f"pst{i}", [128, 1024], BF16).ap()[:, 0:128]
           for i in range(4)]

    def wb_bf16(off_f32, n):
        return wbuf[:, off_f32:off_f32 + (n + 1) // 2].bitcast(BF16)[:, 0:n]

    stream_b = stream.bitcast(BF16)

    bank_free = {}

    def bank_wait(eng, bank):
        if id(bank) in bank_free:
            sname, v = bank_free[id(bank)]
            p.wait(eng, sname, v)

    def bank_mark(bank, sname):
        bank_free[id(bank)] = (sname, p.val(sname))

    def dma_in(dst, src):
        return p.op("sync", lambda e, d=dst, s=src: e.dma_start(out=d, in_=s), SEM_IN, 16)

    def dma_out(dst, src):
        return p.op("sync", lambda e, d=dst, s=src: e.dma_start(out=d, in_=s), SEM_OUT, 16)

    def v_op(fn):
        return p.op("vector", fn, SEM_DVE)

    def g_op(fn):
        return p.op("gpsimd", fn, SEM_GP)

    def s_op(fn):
        return p.op("scalar", fn, SEM_ACT)

    def t_op(fn):
        return p.op("tensor", fn, SEM_MM)

    def vpad():
        v_op(lambda e: e.tensor_copy(out=stat_sb[:, 12:13], in_=identf[:, 0:1]))

    dma_in(stream.rearrange("p (a b) -> p a b", b=DM), x_in.rearrange("(a p) b -> p a b", p=128))
    dma_in(bi_sb, bi_h)
    dma_in(cw_sb, cw_h)
    dma_in(cb_sb, cb_h)
    dma_in(dtb_sb, dtb_h)
    dma_in(a_sb, alog_h)
    dma_in(d_sb, dd_h)
    dma_in(nw_sb, nw_h)
    dma_in(bo_sb, bo_h)
    dma_in(oh_sb, oh_h.partition_broadcast(128))
    setup_in = p.val(SEM_IN)

    p.wait("gpsimd", SEM_IN, setup_in)
    g_op(lambda e: e.memset(identf[:], 1.0))
    g_op(lambda e: e.affine_select(out=identf[:], in_=identf[:], pattern=[[-1, 128]],
                                   compare_op=OP.is_equal, fill=0.0, base=0,
                                   channel_multiplier=1))
    g_op(lambda e: e.tensor_copy(out=identb[:], in_=identf[:]))
    ident_gp = p.val(SEM_GP)
    p.wait("vector", SEM_IN, setup_in)
    v_op(lambda e: e.memset(onesr[:], 1.0))
    p.wait("scalar", SEM_IN, setup_in)
    s_op(lambda e: e.activation(out=bigA[:, 0:NLAYER * NDT * NST], in_=a_sb[:],
                                func=AF.Exp))
    p.wait("vector", SEM_ACT, p.val(SEM_ACT))
    v_op(lambda e: e.tensor_single_scalar(out=a_sb[:],
                                          in_=bigA[:, 0:NLAYER * NDT * NST],
                                          scalar=-1.0, op=OP.mult))
    p.wait("tensor", SEM_GP, ident_gp)
    p.wait("tensor", SEM_IN, setup_in)

    def quant(dst, src, F, aoff, boff):
        nb = F // 16
        xn = bigA[:, aoff:aoff + F]
        qa = bigA[:, aoff + F:aoff + 2 * F]
        xsq = bigA[:, aoff + 2 * F:aoff + 3 * F]
        bm = bigA[:, aoff + 3 * F:aoff + 3 * F + nb]
        scl = bigA[:, aoff + 3 * F + nb:aoff + 3 * F + 2 * nb]
        rcp = bigA[:, aoff + 3 * F + 2 * nb:aoff + 3 * F + 3 * nb]
        m0 = bigB[:, boff:boff + F]
        m1 = bigB[:, boff + F:boff + 2 * F]
        m2 = bigB[:, boff + 2 * F:boff + 3 * F]
        src3 = src.rearrange("p (b k) -> p b k", k=16)
        v_op(lambda e: e.tensor_reduce(out=bm, in_=src3, axis=AX.X, op=OP.max,
                                       apply_absolute_value=True))
        vpad()
        v_op(lambda e: e.tensor_scalar(out=scl, in0=bm, scalar1=1.0 / 6.0,
                                       scalar2=1e-12, op0=OP.mult, op1=OP.max))
        vpad()
        v_op(lambda e: e.reciprocal(out=rcp, in_=scl))
        vpad()
        v_op(lambda e: e.tensor_tensor(
            out=xn.rearrange("p (b k) -> p b k", k=16), in0=src3,
            in1=rcp.rearrange("p (b o) -> p b o", o=1).broadcast_to([128, nb, 16]),
            op=OP.mult))
        dv = p.val(SEM_DVE)
        p.wait("scalar", SEM_DVE, dv)
        avq = s_op(lambda e: e.activation(out=xsq, in_=xn, func=AF.Square))

        def mk(dst_t, th):
            return lambda e: e.tensor_single_scalar(out=dst_t, in_=xsq,
                                                    scalar=th * th, op=OP.is_ge)
        p.wait("vector", SEM_ACT, avq)
        v_op(mk(m0, MIDS[0]))
        v_op(mk(m1, MIDS[1]))
        v_op(lambda e: e.tensor_tensor(out=m0, in0=m0, in1=m1, op=OP.add))
        v_op(mk(m1, MIDS[2]))
        v_op(lambda e: e.tensor_tensor(out=m0, in0=m0, in1=m1, op=OP.add))
        v_op(mk(m1, MIDS[3]))
        v_op(lambda e: e.tensor_tensor(out=m0, in0=m0, in1=m1, op=OP.add))
        v_op(mk(m2, MIDS[4]))
        v_op(mk(m1, MIDS[5]))
        v_op(lambda e: e.tensor_tensor(out=m2, in0=m2, in1=m1, op=OP.add))
        v_op(mk(m1, MIDS[6]))
        v_op(lambda e: e.scalar_tensor_tensor(out=qa, in0=m0, scalar=0.5, in1=m2,
                                              op0=OP.mult, op1=OP.add))
        v_op(lambda e: e.scalar_tensor_tensor(out=qa, in0=m1, scalar=2.0, in1=qa,
                                              op0=OP.mult, op1=OP.add))
        v_op(lambda e: e.tensor_scalar(out=m1, in0=xn, scalar1=0.0, scalar2=2.0,
                                       op0=OP.is_ge, op1=OP.mult))
        v_op(lambda e: e.tensor_tensor(out=xn, in0=qa, in1=m1, op=OP.mult))
        v_op(lambda e: e.tensor_tensor(out=qa, in0=xn, in1=qa, op=OP.subtract))
        v_op(lambda e: e.tensor_tensor(
            out=dst.rearrange("p (b k) -> p b k", k=16),
            in0=qa.rearrange("p (b k) -> p b k", k=16),
            in1=scl.rearrange("p (b o) -> p b o", o=1).broadcast_to([128, nb, 16]),
            op=OP.mult))

    def transpose_tiles(src, ntile, dst_ap_of_t):
        for t in range(ntile):
            pb = pst[t % 4]
            bank_wait("tensor", pb)
            t_op(lambda e, pb=pb, t=t: e.transpose(
                out=pb[:], in_=src[:, t * 128:(t + 1) * 128], identity=identb[:]))
            p.wait("scalar", SEM_MM, p.val(SEM_MM))
            s_op(lambda e, pb=pb, d=dst_ap_of_t(t): e.copy(out=d, in_=pb[:]))
            bank_mark(pb, SEM_ACT)

    def quant_weights():
        bigBf0 = bigB.bitcast(F32)
        wtiles = [bigA[:, 9000:9000 + 2816], bigBf0[:, 8192:8192 + 2816]]
        qtiles = [bigB[:, 0:2816], bigB[:, 2816:5632]]
        jobs = []
        for whbm, dst, rows, cols in ((wg_h, wgT_q, NIT, DM), (wu_h, wuT_q, NIT, DM),
                                      (wd_h, wdT_q, 8, II)):
            for r in range(rows):
                jobs.append((whbm, dst, r, cols))
        lds = {}
        wfree = {0: [], 1: []}
        qfree = {0: [], 1: []}
        for i in range(len(jobs) + 1):
            if i < len(jobs):
                whbm, dst, r, cols = jobs[i]
                par = i % 2
                for sem, v in wfree[par]:
                    p.wait("sync", sem, v)
                wfree[par] = []
                lds[i] = dma_in(wtiles[par][:, 0:cols], whbm[r * 128:(r + 1) * 128, :])
            if i == 0:
                continue
            whbm, dst, r, cols = jobs[i - 1]
            par = (i - 1) % 2
            wt_, qt_ = wtiles[par], qtiles[par]
            p.wait("vector", SEM_IN, lds[i - 1])
            for sem, v in qfree[par]:
                p.wait("vector", sem, v)
            qfree[par] = []
            quant(qt_[:, 0:cols], wt_[:, 0:cols], cols, 0, 22016)
            wfree[par] = [(SEM_DVE, p.val(SEM_DVE))]
            dv = p.val(SEM_DVE)
            p.wait("tensor", SEM_DVE, dv)
            for t in range(cols // 128):
                pb = pst[t % 4]
                ev = wb_bf16(2048 + (t % 4) * 64, 128)
                bank_wait("tensor", pb)
                t_op(lambda e, pb=pb, t=t, qt_=qt_: e.transpose(
                    out=pb[:], in_=qt_[:, t * 128:(t + 1) * 128],
                    identity=identb[:]))
                p.wait("scalar", SEM_MM, p.val(SEM_MM))
                p.wait("scalar", SEM_OUT, p.val(SEM_OUT))
                av = s_op(lambda e, ev=ev, pb=pb: e.copy(out=ev, in_=pb[:]))
                bank_mark(pb, SEM_ACT)
                p.wait("sync", SEM_ACT, av)
                dma_out(dst[t * 128:(t + 1) * 128, r * 128:(r + 1) * 128], ev)
            qfree[par] = [(SEM_MM, p.val(SEM_MM))]

    quant_weights()

    def ln_tile(src_t, dst):
        s1, s2, s3 = stat_sb[:, 0:1], stat_sb[:, 1:2], stat_sb[:, 2:3]
        t2 = bigA[:, 4096:5120]
        v_op(lambda e: e.tensor_reduce(out=s1, in_=src_t, axis=AX.X, op=OP.add))
        dv = p.val(SEM_DVE)
        p.wait("scalar", SEM_DVE, dv)
        av = s_op(lambda e: e.activation(out=t2, in_=src_t, func=AF.Square))
        p.wait("vector", SEM_ACT, av)
        v_op(lambda e: e.tensor_reduce(out=s2, in_=t2, axis=AX.X, op=OP.add))
        v_op(lambda e: e.tensor_single_scalar(out=s1, in_=s1, scalar=1.0 / DM, op=OP.mult))
        v_op(lambda e: e.tensor_single_scalar(out=s2, in_=s2, scalar=1.0 / DM, op=OP.mult))
        v_op(lambda e: e.tensor_tensor(out=s3, in0=s1, in1=s1, op=OP.mult))
        vpad()
        v_op(lambda e: e.tensor_tensor(out=s2, in0=s2, in1=s3, op=OP.subtract))
        vpad()
        v_op(lambda e: e.tensor_single_scalar(out=s2, in_=s2, scalar=EPS, op=OP.add))
        dv = p.val(SEM_DVE)
        p.wait("scalar", SEM_DVE, dv)
        av = s_op(lambda e: e.activation(out=s3, in_=s2, func=AF.Sqrt))
        p.wait("vector", SEM_ACT, av)
        v_op(lambda e: e.reciprocal(out=s2, in_=s3))
        vpad()
        v_op(lambda e: e.tensor_scalar(out=dst, in0=src_t, scalar1=s1, scalar2=s2,
                                       op0=OP.subtract, op1=OP.mult))

    def rms_rstd(src_t, rstd_col):
        s2, s3 = stat_sb[:, 4:5], stat_sb[:, 5:6]
        t2 = bigA[:, 4096:5120]
        av = s_op(lambda e: e.activation(out=t2, in_=src_t, func=AF.Square))
        p.wait("vector", SEM_ACT, av)
        v_op(lambda e: e.tensor_reduce(out=s2, in_=t2, axis=AX.X, op=OP.add))
        vpad()
        v_op(lambda e: e.tensor_scalar(out=s2, in0=s2, scalar1=1.0 / DM, scalar2=EPS,
                                       op0=OP.mult, op1=OP.add))
        dv = p.val(SEM_DVE)
        p.wait("scalar", SEM_DVE, dv)
        av = s_op(lambda e: e.activation(out=s3, in_=s2, func=AF.Sqrt))
        p.wait("vector", SEM_ACT, av)
        v_op(lambda e, r=rstd_col: e.reciprocal(out=r, in_=s3))
        vpad()

    NCH = [512, 512, 512, 512, 512, 256]

    def ffn_call(dump=False):
        p.wait("sync", SEM_OUT, p.val(SEM_OUT))
        for th in range(2):
            for tt in range(8):
                ti = th * 8 + tt
                src_t = stream[:, ti * DM:(ti + 1) * DM]
                xln = bigA[:, 5120:6144]
                ln_tile(src_t, xln)
                xq = bigB[:, 8192:9216]
                p.wait("vector", SEM_MM, p.val(SEM_MM))
                if dump:
                    dvx = p.val(SEM_DVE)
                    p.wait("sync", SEM_DVE, dvx)
                    dma_out(xln_dbg[ti], xln)
                quant(xq, xln, DM, 0, 9216)
                dv = p.val(SEM_DVE)
                p.wait("tensor", SEM_DVE, dv)
                if dump:
                    p.wait("sync", SEM_DVE, dv)
                    dma_out(xq_dbg[ti], xq)
                transpose_tiles(xq, 8,
                                lambda t, tt=tt: bigB[:, t * 1024 + tt * 128:
                                                      t * 1024 + tt * 128 + 128])
            xqT = bigB[:, 0:8192].rearrange("p (a b) -> p a b", b=1024)
            avq = p.val(SEM_ACT)

            hq_base = 8192
            for ci, ch in enumerate(NCH):
                c0 = sum(NCH[:ci])
                wgc = bigA[:, 7168:7168 + 2048].bitcast(BF16)[:, 0:8 * ch] \
                    .rearrange("p (a b) -> p a b", b=ch)
                wuc = bigA[:, 9216:9216 + 2048].bitcast(BF16)[:, 0:8 * ch] \
                    .rearrange("p (a b) -> p a b", b=ch)
                p.wait("sync", SEM_MM, p.val(SEM_MM))
                dma_in(wgc, wgT_q[:, c0:c0 + ch].rearrange("(a p) b -> p a b", p=128))
                l2 = dma_in(wuc, wuT_q[:, c0:c0 + ch].rearrange("(a p) b -> p a b", p=128))
                p.wait("tensor", SEM_IN, l2)
                p.wait("tensor", SEM_ACT, avq)
                for tt in range(8):
                    pg, pu = ps[2 * (tt % 2)], ps[2 * (tt % 2) + 1]
                    bank_wait("tensor", pg)
                    for k in range(8):
                        t_op(lambda e, k=k, tt=tt, ch=ch, pg=pg, wgc=wgc: e.matmul(
                            pg[:, 0:ch], xqT[:, k, tt * 128:(tt + 1) * 128],
                            wgc[:, k, :], start=(k == 0), stop=(k == 7)))
                    bank_wait("tensor", pu)
                    for k in range(8):
                        t_op(lambda e, k=k, tt=tt, ch=ch, pu=pu, wuc=wuc: e.matmul(
                            pu[:, 0:ch], xqT[:, k, tt * 128:(tt + 1) * 128],
                            wuc[:, k, :], start=(k == 0), stop=(k == 7)))
                    mmv = p.val(SEM_MM)
                    p.wait("scalar", SEM_MM, mmv)
                    hsil = bigA[:, 6144:6656]
                    hch = bigA[:, 6656:7168]
                    p.wait("scalar", SEM_DVE, p.val(SEM_DVE))
                    av = s_op(lambda e, pg=pg, ch=ch, hsil=hsil: e.activation(
                        out=hsil[:, 0:ch], in_=pg[:, 0:ch], func=AF.Silu))
                    bank_mark(pg, SEM_ACT)
                    p.wait("vector", SEM_ACT, av)
                    v_op(lambda e, pu=pu, ch=ch, hsil=hsil, hch=hch: e.tensor_tensor(
                        out=hch[:, 0:ch], in0=hsil[:, 0:ch], in1=pu[:, 0:ch],
                        op=OP.mult))
                    bank_mark(pu, SEM_DVE)
                    if dump:
                        dvh = p.val(SEM_DVE)
                        p.wait("sync", SEM_DVE, dvh)
                        dma_out(h_dbg[th * 8 + tt][:, c0:c0 + ch], hch[:, 0:ch])
                    hqc = bigB[:, 30720:30720 + ch]
                    p.wait("vector", SEM_MM, p.val(SEM_MM))
                    quant(hqc, hch[:, 0:ch], ch, 0, 31232)
                    dv = p.val(SEM_DVE)
                    p.wait("tensor", SEM_DVE, dv)
                    transpose_tiles(hqc, ch // 128,
                                    lambda t, c0=c0, tt=tt: bigB[
                                        :, hq_base + (c0 // 128 + t) * 1024 + tt * 128:
                                        hq_base + (c0 // 128 + t) * 1024 + tt * 128 + 128])
            hqT = bigB[:, hq_base:hq_base + 22528].rearrange("p (a b) -> p a b", b=1024)
            avh = p.val(SEM_ACT)

            for dc in range(4):
                wdc = wb_bf16(0, 22 * 256).rearrange("p (a b) -> p a b", b=256)
                p.wait("sync", SEM_MM, p.val(SEM_MM))
                ld = dma_in(wdc, wdT_q[:, dc * 256:(dc + 1) * 256]
                            .rearrange("(a p) b -> p a b", p=128))
                p.wait("tensor", SEM_IN, ld)
                p.wait("tensor", SEM_ACT, avh)
                for tt in range(8):
                    ti = th * 8 + tt
                    pb = ps[2 + (tt % 2)]
                    bank_wait("tensor", pb)
                    for k in range(22):
                        t_op(lambda e, k=k, tt=tt, pb=pb, wdc=wdc: e.matmul(
                            pb[:, 0:256], hqT[:, k, tt * 128:(tt + 1) * 128],
                            wdc[:, k, :], start=(k == 0), stop=(k == 21)))
                    mmv = p.val(SEM_MM)
                    p.wait("vector", SEM_MM, mmv)
                    dst = stream[:, ti * DM + dc * 256:ti * DM + (dc + 1) * 256]
                    v_op(lambda e, pb=pb, dst=dst: e.scalar_tensor_tensor(
                        out=dst, in0=pb[:, 0:256], scalar=0.5, in1=dst,
                        op0=OP.mult, op1=OP.add))
                    bank_mark(pb, SEM_DVE)

    def mamba_layer(li):
        xn = bigB[:, 0:16384].rearrange("p (a b) -> p a b", b=DM)
        rstd = stat_sb[:, 6:7]
        p.wait("scalar", SEM_DVE, p.val(SEM_DVE))
        p.wait("sync", SEM_DVE, p.val(SEM_DVE))
        for i in range(NT):
            src_t = stream[:, i * DM:(i + 1) * DM]
            rms_rstd(src_t, rstd)
            v_op(lambda e, s=src_t, i=i: e.tensor_single_scalar(
                out=xn[:, i, :], in_=s, scalar=rstd, op=OP.mult))
            dma_out(h_sp[i], src_t)
        hspv = p.val(SEM_OUT)
        dv = p.val(SEM_DVE)
        p.wait("tensor", SEM_DVE, dv)
        p.wait("scalar", SEM_OUT, hspv)

        xnT = stream_b[:, 0:16384].rearrange("p (a b) -> p a b", b=2048)
        for i in range(NT):
            transpose_tiles(xn[:, i, :], 8,
                            lambda t, i=i: xnT[:, t, i * 128:(i + 1) * 128])
        avT = p.val(SEM_ACT)
        p.wait("tensor", SEM_ACT, avT)

        xs = bigB[:, 0:16 * 2048].rearrange("p (a b) -> p a b", b=2048)
        wic_f = bigA[:, 0:1024].rearrange("p (a b) -> p a b", b=128)
        wic = wb_bf16(2048, 8 * 128).rearrange("p (a b) -> p a b", b=128)
        rtmp = bigA[:, 1024:1536]
        for mt in range(32):
            p.wait("sync", SEM_MM, p.val(SEM_MM))
            ld = dma_in(wic_f, wiT_h[li][:, mt * 128:(mt + 1) * 128]
                        .rearrange("(a p) b -> p a b", p=128))
            p.wait("vector", SEM_IN, ld)
            v_op(lambda e: e.tensor_tensor(
                out=wic, in0=wic_f,
                in1=nw_sb[:, li * 8:(li + 1) * 8].rearrange("p (a o) -> p a o", o=1)
                    .broadcast_to([128, 8, 128]),
                op=OP.mult))
            cv = p.val(SEM_DVE)
            p.wait("tensor", SEM_DVE, cv)
            for tb in range(NTB):
                pb = ps[tb % 2]
                bank_wait("tensor", pb)
                for k in range(8):
                    t_op(lambda e, k=k, tb=tb, pb=pb: e.matmul(
                        pb[:], wic[:, k, :], xnT[:, k, tb * 512:(tb + 1) * 512],
                        start=(k == 0), stop=(k == 7)))
                p.wait("scalar", SEM_MM, p.val(SEM_MM))
                bcol = bi_sb[:, li * 32 + mt:li * 32 + mt + 1]
                if mt < 16:
                    s_op(lambda e, pb=pb, mt=mt, tb=tb, bcol=bcol: e.add(
                        out=xs[:, mt, tb * 512:(tb + 1) * 512], in_=pb[:], add=bcol))
                else:
                    p.wait("scalar", SEM_OUT, p.val(SEM_OUT))
                    av = s_op(lambda e, pb=pb, bcol=bcol: e.add(
                        out=rtmp, in_=pb[:], add=bcol))
                    p.wait("sync", SEM_ACT, av)
                    dma_out(res_sp[mt - 16][:, tb * 512:(tb + 1) * 512], rtmp)
                bank_mark(pb, SEM_ACT)

        hpf = bigA[:, 2048:2048 + 64]
        xsv = p.val(SEM_ACT)
        p.wait("vector", SEM_ACT, xsv)
        for dt in range(NDT):
            v_op(lambda e, dt=dt: e.tensor_copy(
                out=hpf[:, dt * 4:dt * 4 + 2], in_=xs[:, dt, 0:2]))
            v_op(lambda e, dt=dt: e.tensor_copy(
                out=hpf[:, dt * 4 + 2:dt * 4 + 4], in_=xs[:, dt, 2046:2048]))
        hv = p.val(SEM_DVE)
        p.wait("sync", SEM_DVE, hv)
        gv = dma_out(gin_h, hpf)
        p.wait("gpsimd", SEM_OUT, gv)
        p.op("gpsimd", lambda e: e.collective_compute(
            "AllGather", OP.bypass, replica_groups=[list(range(NCORES))],
            ins=[gin_h], outs=[gout_h]), SEM_CC)
        p.wait("sync", SEM_CC, p.val(SEM_CC))
        gl = bigA[:, 2112:2112 + 512].rearrange("p (a b) -> p a b", b=64)
        lg = dma_in(gl, gout_h.rearrange("c p f -> p c f"))
        p.wait("vector", SEM_IN, lg)
        v_op(lambda e: e.memset(halo_sb[:], 0.0))
        vpad()
        vpad()
        hl4 = halo_sb.rearrange("p (a b) -> p a b", b=4)
        for s in range(NCORES):
            v_op(lambda e, s=s: e.scalar_tensor_tensor(
                out=hl4[:, :, 0:2],
                in0=gl[:, s, :].rearrange("p (a b) -> p a b", b=4)[:, :, 2:4],
                scalar=oh_sb[:, 8 + s:9 + s],
                in1=hl4[:, :, 0:2],
                op0=OP.mult, op1=OP.add))
            v_op(lambda e, s=s: e.scalar_tensor_tensor(
                out=hl4[:, :, 2:4],
                in0=gl[:, s, :].rearrange("p (a b) -> p a b", b=4)[:, :, 0:2],
                scalar=oh_sb[:, 16 + s:17 + s],
                in1=hl4[:, :, 2:4],
                op0=OP.mult, op1=OP.add))
        halo_b = wb_bf16(1280, NDT * 4).rearrange("p (a b) -> p a b", b=4)
        vpad()
        vpad()
        v_op(lambda e: e.tensor_copy(out=halo_b.rearrange("p a b -> p (a b)"),
                                     in_=halo_sb[:]))
        hv2 = p.val(SEM_DVE)
        p.wait("tensor", SEM_DVE, hv2)

        ub = bigA[:, 0:4096].bitcast(BF16).rearrange("p (a b) -> p a b", b=512)
        uf = bigA[:, 4096:4608]
        r_t = bigA[:, 4608:5120]
        m_t = bigA[:, 5120:5632]
        e_t = bigA[:, 5632:6144]
        d_t = bigA[:, 6144:6656]
        w_t = bigA[:, 6656:7168]
        y_t = bigA[:, 7168:7680]
        ur_t = bigA[:, 7680:8192]
        diag = wb_bf16(0, 5 * 128).rearrange("p (a b) -> p a b", b=128)
        wxc_f = wbuf[:, 1536:1536 + 1536].rearrange("p (a b) -> p a b", b=96)
        wxc = wb_bf16(512, 16 * 96).rearrange("p (a b) -> p a b", b=96)
        ldx = dma_in(wxc_f, wxT_h[li].rearrange("(a p) b -> p a b", p=128))
        p.wait("vector", SEM_IN, ldx)
        v_op(lambda e: e.tensor_copy(out=wxc.rearrange("p a b -> p (a b)"),
                                     in_=wxc_f.rearrange("p a b -> p (a b)")))
        wxv = p.val(SEM_DVE)
        p.wait("tensor", SEM_DVE, wxv)
        xdbl = stream[0:96, 8192:10240]
        dtw = wbuf[0:64, 1536:1536 + 128]
        cdT = scol[:, 768:784]
        for tb in range(NTB):
            for dt in range(NDT):
                p.wait("vector", SEM_MM, p.val(SEM_MM))
                for j in range(5):
                    v_op(lambda e, j=j, dt=dt: e.tensor_single_scalar(
                        out=diag[:, j, :], in_=identf[:],
                        scalar=cw_sb[:, (li * NDT + dt) * 5 + j:
                                     (li * NDT + dt) * 5 + j + 1], op=OP.mult))
                dv = p.val(SEM_DVE)
                p.wait("tensor", SEM_DVE, dv)
                pb = ps[dt % 2]
                bank_wait("tensor", pb)
                segs = []
                for j in range(5):
                    lo = tb * 512 + j - 2
                    hi = lo + 512
                    if lo < 0:
                        nh = -lo
                        segs.append((j, pb[:, 0:nh], halo_b[:, dt, 2 - nh:2]))
                        segs.append((j, pb[:, nh:512], xs[:, dt, 0:512 - nh]))
                    elif hi > 2048:
                        nh = hi - 2048
                        segs.append((j, pb[:, 0:512 - nh], xs[:, dt, lo:2048]))
                        segs.append((j, pb[:, 512 - nh:512], halo_b[:, dt, 2:2 + nh]))
                    else:
                        segs.append((j, pb[:], xs[:, dt, lo:lo + 512]))
                for si, (j, pslice, mov) in enumerate(segs):
                    t_op(lambda e, j=j, pslice=pslice, mov=mov, si=si,
                         last=(si == len(segs) - 1): e.matmul(
                        pslice, diag[:, j, :], mov, start=(si == 0), stop=last))
                p.wait("scalar", SEM_MM, p.val(SEM_MM))
                ccol = cb_sb[:, li * NDT + dt:li * NDT + dt + 1]
                p.wait("scalar", SEM_OUT, p.val(SEM_OUT))
                p.wait("scalar", SEM_DVE, p.val(SEM_DVE))
                av = s_op(lambda e, pb=pb, ccol=ccol: e.activation(
                    out=uf, in_=pb[:], func=AF.Silu, bias=ccol))
                bank_mark(pb, SEM_ACT)
                p.wait("vector", SEM_ACT, av)
                v_op(lambda e, dt=dt: e.tensor_copy(out=ub[:, dt, :], in_=uf))
                p.wait("sync", SEM_ACT, av)
                dma_out(u_sp[dt][:, tb * 512:(tb + 1) * 512], uf)
            p.wait("sync", SEM_OUT, p.val(SEM_OUT))
            dv = p.val(SEM_DVE)
            p.wait("tensor", SEM_DVE, dv)
            pbx = ps[2]
            bank_wait("tensor", pbx)
            for k in range(NDT):
                t_op(lambda e, k=k, pbx=pbx: e.matmul(
                    pbx[0:96, :], wxc[:, k, :], ub[:, k, :],
                    start=(k == 0), stop=(k == NDT - 1)))
            mv = p.val(SEM_MM)
            p.wait("scalar", SEM_MM, mv)
            av = s_op(lambda e, pbx=pbx, tb=tb: e.copy(
                out=xdbl[:, tb * 512:(tb + 1) * 512], in_=pbx[0:96, :]))
            bank_mark(pbx, SEM_ACT)
            p.wait("tensor", SEM_ACT, av)
            for mt in range(NDT):
                p.wait("sync", SEM_MM, p.val(SEM_MM))
                p.wait("sync", SEM_DVE, p.val(SEM_DVE))
                ldt = dma_in(dtw, wdtT_h[li][:, mt * 128:(mt + 1) * 128])
                dma_in(ur_t, u_sp[mt][:, tb * 512:(tb + 1) * 512])
                p.wait("tensor", SEM_IN, ldt)
                pb2 = ps[3]
                bank_wait("tensor", pb2)
                t_op(lambda e, tb=tb, pb2=pb2: e.matmul(
                    pb2[:], dtw, xdbl[0:64, tb * 512:(tb + 1) * 512],
                    start=True, stop=True))
                mv = p.val(SEM_MM)
                p.wait("scalar", SEM_MM, mv)
                p.wait("vector", SEM_MM, mv)
                bcol = dtb_sb[:, li * NDT + mt:li * NDT + mt + 1]
                p.wait("scalar", SEM_OUT, p.val(SEM_OUT))
                p.wait("vector", SEM_OUT, p.val(SEM_OUT))
                av = s_op(lambda e, pb2=pb2, bcol=bcol: e.activation(
                    out=r_t, in_=pb2[:], func=AF.Relu, bias=bcol))
                p.wait("vector", SEM_ACT, p.val(SEM_ACT))
                v_op(lambda e, pb2=pb2: e.scalar_tensor_tensor(
                    out=m_t, in0=r_t, scalar=2.0, in1=pb2[:],
                    op0=OP.mult, op1=OP.subtract))
                v_op(lambda e, bcol=bcol: e.tensor_single_scalar(
                    out=m_t, in_=m_t, scalar=bcol, op=OP.subtract))
                bank_mark(pb2, SEM_DVE)
                dv = p.val(SEM_DVE)
                p.wait("scalar", SEM_DVE, dv)
                av = s_op(lambda e: e.activation(out=e_t, in_=m_t, func=AF.Exp,
                                                 scale=-1.0))
                p.wait("vector", SEM_ACT, av)
                v_op(lambda e: e.tensor_single_scalar(out=e_t, in_=e_t, scalar=1.0,
                                                      op=OP.add))
                dv = p.val(SEM_DVE)
                p.wait("scalar", SEM_DVE, dv)
                av = s_op(lambda e: e.activation(out=m_t, in_=e_t, func=AF.Ln))
                p.wait("vector", SEM_ACT, av)
                p.wait("vector", SEM_IN, p.val(SEM_IN))
                v_op(lambda e: e.tensor_tensor(out=d_t, in0=r_t, in1=m_t, op=OP.add))
                v_op(lambda e: e.tensor_reduce(out=stat_sb[:, 8:9], in_=d_t,
                                               axis=AX.X, op=OP.add))
                vpad()
                if tb == 0:
                    v_op(lambda e, mt=mt: e.tensor_copy(out=cdT[:, mt:mt + 1],
                                                        in_=stat_sb[:, 8:9]))
                else:
                    v_op(lambda e, mt=mt: e.tensor_tensor(
                        out=cdT[:, mt:mt + 1], in0=cdT[:, mt:mt + 1],
                        in1=stat_sb[:, 8:9], op=OP.add))
                v_op(lambda e: e.tensor_tensor(out=w_t, in0=d_t,
                                                   in1=ur_t, op=OP.mult))
                dcol = d_sb[:, li * NDT + mt:li * NDT + mt + 1]
                v_op(lambda e, dcol=dcol: e.tensor_single_scalar(
                    out=y_t, in_=ur_t, scalar=dcol, op=OP.mult))
                dv = p.val(SEM_DVE)
                p.wait("sync", SEM_DVE, dv)
                dma_out(del_sp[mt][:, tb * 512:(tb + 1) * 512], d_t)
                dma_out(w_sp[mt][:, tb * 512:(tb + 1) * 512], w_t)
                dma_out(yd_sp[mt][:, tb * 512:(tb + 1) * 512], y_t)
        p.wait("sync", SEM_ACT, p.val(SEM_ACT))
        bcv = dma_out(bc_sp, xdbl[64:96, :])
        p.wait("sync", SEM_OUT, p.val(SEM_OUT))

        reps = stream
        bigBf = bigB.bitcast(F32)
        dltb = [bigA[:, 0:2048], bigA[:, 2048:4096]]
        wtb = [bigA[:, 4096:6144], bigA[:, 6144:8192]]
        ytb = [bigA[:, 8192:10240], bigA[:, 10240:12288]]
        ab = [bigBf[:, 0:2048], bigBf[:, 2048:4096]]
        zb = [bigBf[:, 4096:6144], bigBf[:, 6144:8192]]
        hb = [bigBf[:, 8192:10240], bigBf[:, 10240:12288]]
        tb_ = [bigBf[:, 12288:14336], bigBf[:, 14336:16384]]
        Mv = scol[:, 0:256]
        hend = scol[:, 256:512]
        hin = scol[:, 512:768]
        tc = scol[:, 800:1056]
        ld_free = {0: [], 1: []}
        ab_free = {0: [], 1: []}
        zb_free = {0: [], 1: []}
        for ng in range(4):
            p.wait("sync", SEM_GP, p.val(SEM_GP))
            p.wait("sync", SEM_DVE, p.val(SEM_DVE))
            p.wait("sync", SEM_MM, p.val(SEM_MM))
            p.wait("sync", SEM_ACT, p.val(SEM_ACT))
            p.wait("sync", SEM_OUT, p.val(SEM_OUT))
            rl = None
            for j in range(4):
                n = ng * 4 + j
                dma_in(reps[:, (2 * j) * 2048:(2 * j + 1) * 2048],
                       bc_sp[n:n + 1, :].partition_broadcast(128))
                rl = dma_in(reps[:, (2 * j + 1) * 2048:(2 * j + 2) * 2048],
                            bc_sp[16 + n:17 + n, :].partition_broadcast(128))
            p.wait("vector", SEM_IN, rl)
            p.wait("gpsimd", SEM_IN, rl)
            ld_done = {}
            for dt in range(NDT + 1):
                if dt < NDT:
                    par = dt % 2
                    for sem, v in ld_free[par]:
                        p.wait("sync", sem, v)
                    ld_free[par] = []
                    dma_in(dltb[par], del_sp[dt])
                    dma_in(wtb[par], w_sp[dt])
                    ysrc = yd_sp[dt] if ng == 0 else y_sp[dt]
                    ld_done[dt] = dma_in(ytb[par], ysrc)
                if dt == 0:
                    continue
                cdt = dt - 1
                par = cdt % 2
                dlt_c, wt_c, yt_c = dltb[par], wtb[par], ytb[par]
                p.wait("scalar", SEM_IN, ld_done[cdt])
                p.wait("gpsimd", SEM_IN, ld_done[cdt])
                p.wait("vector", SEM_IN, ld_done[cdt])
                for j in range(4):
                    n = ng * 4 + j
                    bpar = j % 2
                    acol = a_sb[:, (li * NDT + cdt) * NST + n:
                                (li * NDT + cdt) * NST + n + 1]
                    for sem, v in ab_free[bpar]:
                        p.wait("scalar", sem, v)
                    ab_free[bpar] = []
                    av = s_op(lambda e, acol=acol, A=ab[bpar], D=dlt_c: e.activation(
                        out=A, in_=D, func=AF.Exp, scale=acol))
                    for sem, v in zb_free[bpar]:
                        p.wait("vector", sem, v)
                    zb_free[bpar] = []
                    v_op(lambda e, j=j, Z=zb[bpar], W=wt_c: e.tensor_tensor(
                        out=Z, in0=W,
                        in1=reps[:, (2 * j) * 2048:(2 * j + 1) * 2048], op=OP.mult))
                    p.wait("vector", SEM_ACT, av)
                    v_op(lambda e, A=ab[bpar], Z=zb[bpar], H=hb[bpar]:
                         e.tensor_tensor_scan(out=H, data0=A, data1=Z, initial=0.0,
                                              op0=OP.mult, op1=OP.add))
                    ab_free[bpar].append((SEM_DVE, p.val(SEM_DVE)))
                    zb_free[bpar].append((SEM_DVE, p.val(SEM_DVE)))
                    p.wait("gpsimd", SEM_DVE, p.val(SEM_DVE))
                    gc = g_op(lambda e, j=j, H=hb[bpar], T=tb_[bpar]: e.tensor_tensor(
                        out=T, in0=H,
                        in1=reps[:, (2 * j + 1) * 2048:(2 * j + 2) * 2048],
                        op=OP.mult))
                    p.wait("vector", SEM_GP, gc)
                    v_op(lambda e, T=tb_[bpar], Y=yt_c: e.tensor_tensor(
                        out=Y, in0=Y, in1=T, op=OP.add))
                    v_op(lambda e, cdt=cdt, n=n, H=hb[bpar]: e.tensor_copy(
                        out=hend[:, cdt * 16 + n:cdt * 16 + n + 1],
                        in_=H[:, 2047:2048]))
                    ab_free[bpar].append((SEM_GP, p.val(SEM_GP)))
                dv = p.val(SEM_DVE)
                p.wait("sync", SEM_DVE, dv)
                ow = dma_out(y_sp[cdt], yt_c)
                ld_free[par] = [(SEM_ACT, p.val(SEM_ACT)),
                                (SEM_GP, p.val(SEM_GP)),
                                (SEM_OUT, ow)]

        p.wait("scalar", SEM_DVE, p.val(SEM_DVE))
        cdT = scol[:, 768:784]
        for dt in range(NDT):
            s_op(lambda e, dt=dt: e.activation(
                out=Mv[:, dt * 16:(dt + 1) * 16],
                in_=a_sb[:, (li * NDT + dt) * NST:(li * NDT + dt + 1) * NST],
                func=AF.Exp, scale=cdT[:, dt:dt + 1]))
        sv = p.val(SEM_ACT)
        p.wait("sync", SEM_ACT, sv)
        p.wait("sync", SEM_DVE, p.val(SEM_DVE))
        dma_out(gin_s[:, 0:256], Mv)
        g2 = dma_out(gin_s[:, 256:512], hend)
        p.wait("gpsimd", SEM_OUT, g2)
        p.op("gpsimd", lambda e: e.collective_compute(
            "AllGather", OP.bypass, replica_groups=[list(range(NCORES))],
            ins=[gin_s], outs=[gout_s]), SEM_CC)
        p.wait("sync", SEM_CC, p.val(SEM_CC))
        p.wait("sync", SEM_GP, p.val(SEM_GP))
        gat = bigBf[:, 14336 - 4096 + 4096:16384]
        gat = bigA[:, 0:4096].rearrange("p (a b) -> p a b", b=512)
        lg = dma_in(gat, gout_s.rearrange("c p f -> p c f"))
        p.wait("vector", SEM_IN, lg)
        v_op(lambda e: e.memset(hin, 0.0))
        v_op(lambda e: e.memset(tc, 0.0))
        vpad()
        vpad()
        for sc_ in range(NCORES):
            if sc_ % 4 == 0:
                v_op(lambda e: e.memset(tc, 0.0))
                vpad()
                vpad()
            else:
                v_op(lambda e, sc_=sc_: e.tensor_tensor(
                    out=tc, in0=tc, in1=gat[:, sc_ - 1, 0:256], op=OP.mult))
                v_op(lambda e, sc_=sc_: e.tensor_tensor(
                    out=tc, in0=tc, in1=gat[:, sc_ - 1, 256:512], op=OP.add))
                v_op(lambda e, sc_=sc_: e.scalar_tensor_tensor(
                    out=hin, in0=tc, scalar=oh_sb[:, sc_:sc_ + 1], in1=hin,
                    op0=OP.mult, op1=OP.add))

        zeros_t = wtb[0]
        p.wait("vector", SEM_IN, p.val(SEM_IN))
        p.wait("vector", SEM_GP, p.val(SEM_GP))
        v_op(lambda e: e.memset(zeros_t, 0.0))
        ld_free = {0: [], 1: []}
        gb_free = {0: [], 1: []}
        for ng in range(4):
            p.wait("sync", SEM_DVE, p.val(SEM_DVE))
            p.wait("sync", SEM_OUT, p.val(SEM_OUT))
            rl = None
            for j in range(4):
                n = ng * 4 + j
                rl = dma_in(reps[:, (2 * j + 1) * 2048:(2 * j + 2) * 2048],
                            bc_sp[16 + n:17 + n, :].partition_broadcast(128))
            p.wait("vector", SEM_IN, rl)
            ld_done = {}
            for dt in range(NDT + 1):
                if dt < NDT:
                    par = dt % 2
                    for sem, v in ld_free[par]:
                        p.wait("sync", sem, v)
                    ld_free[par] = []
                    dma_in(dltb[par], del_sp[dt])
                    ld_done[dt] = dma_in(ytb[par], y_sp[dt])
                if dt == 0:
                    continue
                cdt = dt - 1
                par = cdt % 2
                dlt_c, yt_c = dltb[par], ytb[par]
                cd_c = hb[par]
                p.wait("vector", SEM_IN, ld_done[cdt])
                p.wait("vector", SEM_ACT, p.val(SEM_ACT))
                v_op(lambda e, D=dlt_c, CD=cd_c: e.tensor_tensor_scan(
                    out=CD, data0=D, data1=zeros_t, initial=0.0,
                    op0=OP.add, op1=OP.add))
                dvc = p.val(SEM_DVE)
                p.wait("scalar", SEM_DVE, dvc)
                for j in range(4):
                    n = ng * 4 + j
                    bpar = j % 2
                    acol = a_sb[:, (li * NDT + cdt) * NST + n:
                                (li * NDT + cdt) * NST + n + 1]
                    for sem, v in gb_free[bpar]:
                        p.wait("scalar", sem, v)
                    gb_free[bpar] = []
                    av = s_op(lambda e, acol=acol, G=ab[bpar], CD=cd_c: e.activation(
                        out=G, in_=CD, func=AF.Exp, scale=acol))
                    p.wait("vector", SEM_ACT, av)
                    hcol = hin[:, cdt * 16 + n:cdt * 16 + n + 1]
                    v_op(lambda e, j=j, hcol=hcol, G=ab[bpar], T=tb_[bpar]:
                         e.scalar_tensor_tensor(
                        out=T, in0=G, scalar=hcol,
                        in1=reps[:, (2 * j + 1) * 2048:(2 * j + 2) * 2048],
                        op0=OP.mult, op1=OP.mult))
                    gb_free[bpar].append((SEM_DVE, p.val(SEM_DVE)))
                    v_op(lambda e, T=tb_[bpar], Y=yt_c: e.tensor_tensor(
                        out=Y, in0=Y, in1=T, op=OP.add))
                dv = p.val(SEM_DVE)
                p.wait("sync", SEM_DVE, dv)
                ow = dma_out(y_sp[cdt], yt_c)
                ld_free[par] = [(SEM_DVE, dv), (SEM_OUT, ow)]

        p.wait("sync", SEM_OUT, p.val(SEM_OUT))
        p.wait("sync", SEM_ACT, p.val(SEM_ACT))
        p.wait("sync", SEM_DVE, p.val(SEM_DVE))
        p.wait("scalar", SEM_DVE, p.val(SEM_DVE))
        gld = {}
        gfree = {0: [], 1: []}
        for dt in range(NDT + 1):
            if dt < NDT:
                par = dt % 2
                for sem, v in gfree[par]:
                    p.wait("sync", sem, v)
                gfree[par] = []
                dma_in(ytb[par], y_sp[dt])
                gld[dt] = dma_in(dltb[par], res_sp[dt])
            if dt == 0:
                continue
            cdt = dt - 1
            par = cdt % 2
            p.wait("scalar", SEM_IN, gld[cdt])
            av = s_op(lambda e, R=dltb[par], A=wtb[par]: e.activation(
                out=A, in_=R, func=AF.Silu))
            p.wait("vector", SEM_ACT, av)
            p.wait("vector", SEM_IN, gld[cdt])
            v_op(lambda e, cdt=cdt, Y=ytb[par], A=wtb[par]: e.tensor_tensor(
                out=bigB[:, cdt * 2048:(cdt + 1) * 2048], in0=Y, in1=A,
                op=OP.mult))
            gfree[par] = [(SEM_DVE, p.val(SEM_DVE)), (SEM_ACT, p.val(SEM_ACT))]
        y_g = bigB[:, 0:16 * 2048].rearrange("p (a b) -> p a b", b=2048)
        ygv = p.val(SEM_DVE)
        p.wait("tensor", SEM_DVE, ygv)

        woc = bigA[:, 0:8192].bitcast(BF16).rearrange("p (a b) -> p a b", b=1024)
        wof = bigA[:, 8192:10240]
        for dt2 in range(8):
            p.wait("sync", SEM_DVE, p.val(SEM_DVE))
            ld = dma_in(wof.rearrange("p (a b) -> p a b", b=DM),
                        woT_h[li][dt2 * 256:(dt2 + 1) * 256, :]
                        .rearrange("(a p) b -> p a b", p=128))
            p.wait("vector", SEM_IN, ld)
            v_op(lambda e, dt2=dt2: e.tensor_copy(
                out=woc[:, 2 * dt2:2 * dt2 + 2, :].rearrange("p a b -> p (a b)"),
                in_=wof))
        wov = p.val(SEM_DVE)
        p.wait("tensor", SEM_DVE, wov)
        p.wait("sync", SEM_OUT, p.val(SEM_OUT))
        for i in range(NT):
            p.wait("sync", SEM_DVE, p.val(SEM_DVE))
            hld = dma_in(wof[:, 0:1024], h_sp[i])
            for c in range(2):
                pb = ps[c]
                bank_wait("tensor", pb)
                for k in range(NDT):
                    t_op(lambda e, k=k, i=i, c=c, pb=pb: e.matmul(
                        pb[:], y_g[:, k, i * 128:(i + 1) * 128],
                        woc[:, k, c * 512:(c + 1) * 512],
                        start=(k == 0), stop=False))
                t_op(lambda e, c=c, pb=pb: e.matmul(
                    pb[:], onesr[0:1, 0:128],
                    bo_sb[0:1, li * DM + c * 512:li * DM + (c + 1) * 512],
                    start=False, stop=True))
            mv = p.val(SEM_MM)
            p.wait("vector", SEM_MM, mv)
            p.wait("vector", SEM_IN, hld)
            for c in range(2):
                pb = ps[c]
                v_op(lambda e, i=i, c=c, pb=pb: e.tensor_tensor(
                    out=stream[:, i * DM + c * 512:i * DM + (c + 1) * 512],
                    in0=wof[:, c * 512:(c + 1) * 512], in1=pb[:], op=OP.add))
                bank_mark(pb, SEM_DVE)
        if dbg:
            p.wait("sync", SEM_DVE, p.val(SEM_DVE))
            for i2 in range(NT):
                dma_out(xln_dbg[i2], stream[:, i2 * DM:(i2 + 1) * DM])
            p.wait("gpsimd", SEM_OUT, p.val(SEM_OUT))
            p.wait("vector", SEM_OUT, p.val(SEM_OUT))
            p.wait("scalar", SEM_OUT, p.val(SEM_OUT))

    ffn_call(dump=dbg)
    p.wait("sync", SEM_DVE, p.val(SEM_DVE))
    for i in range(NT):
        src_t = stream[:, i * DM:(i + 1) * DM]
        xv = dma_out(x_sp[i], src_t)
        xln = bigA[:, 5120:6144]
        ln_tile(src_t, xln)
        p.wait("vector", SEM_OUT, xv)
        v_op(lambda e, s=src_t, xln=xln: e.tensor_copy(out=s, in_=xln))
    for li in range(nlay):
        mamba_layer(li)
    rstd2 = stat_sb[:, 7:8]
    xr_t = bigA[:, 6144:7168]
    nfw_sb = bigA[:, 7168:8192]
    p.wait("sync", SEM_DVE, p.val(SEM_DVE))
    p.wait("sync", SEM_ACT, p.val(SEM_ACT))
    p.wait("sync", SEM_OUT, p.val(SEM_OUT))
    nfl = dma_in(nfw_sb, nfw_h.partition_broadcast(128))
    p.wait("vector", SEM_IN, nfl)
    p.wait("scalar", SEM_DVE, p.val(SEM_DVE))
    for i in range(NT):
        src_t = stream[:, i * DM:(i + 1) * DM]
        rms_rstd(src_t, rstd2)
        t1 = bigA[:, 5120:6144]
        v_op(lambda e, s=src_t, t1=t1: e.scalar_tensor_tensor(
            out=t1, in0=s, scalar=rstd2, in1=nfw_sb, op0=OP.mult, op1=OP.mult))
        p.wait("sync", SEM_DVE, p.val(SEM_DVE))
        xld = dma_in(xr_t, x_sp[i])
        p.wait("vector", SEM_IN, xld)
        vpad()
        v_op(lambda e, s=src_t, t1=t1: e.tensor_tensor(
            out=s, in0=t1, in1=xr_t, op=OP.add))
    ffn_call()
    fv = p.val(SEM_DVE)
    p.wait("sync", SEM_DVE, fv)
    ov = dma_out(out_h.rearrange("(a p) b -> p a b", p=128), stream.rearrange("p (a b) -> p a b", b=DM))
    p.wait("sync", SEM_OUT, ov)

    with nc.Block() as block:
        p.replay(block)
    return nc


_NC_CACHE = [None]


def _stage_inputs(inputs):
    x = np.asarray(inputs["L0_attn_x"], np.float32)
    ipw = np.asarray(inputs["L0_attn_in_proj_w"], np.float32)
    ipb = np.asarray(inputs["L0_attn_in_proj_b"], np.float32)
    cw = np.asarray(inputs["L0_attn_conv_w"], np.float32)
    cb = np.asarray(inputs["L0_attn_conv_b"], np.float32)
    xpw = np.asarray(inputs["L0_attn_x_proj_w"], np.float32)
    dtw = np.asarray(inputs["L0_attn_dt_proj_w"], np.float32)
    dtb = np.asarray(inputs["L0_attn_dt_proj_b"], np.float32)
    alog = np.asarray(inputs["L0_attn_A_log"], np.float32)
    dd = np.asarray(inputs["L0_attn_D"], np.float32)
    opw = np.asarray(inputs["L0_attn_out_proj_w"], np.float32)
    opb = np.asarray(inputs["L0_attn_out_proj_b"], np.float32)
    nw = np.asarray(inputs["L0_attn_norm_w"], np.float32)
    nfw = np.asarray(inputs["L0_attn_norm_f_w"], np.float32)
    wg = np.asarray(inputs["L0_ffn_w_gate"], np.float32)
    wu = np.asarray(inputs["L0_ffn_w_up"], np.float32)
    wd = np.asarray(inputs["L0_ffn_w_down"], np.float32)

    def colmap(a):
        L = a.shape[0]
        G = a.shape[1] // 128
        return np.ascontiguousarray(a.reshape(L, G, 128).transpose(2, 0, 1)
                                    .reshape(128, L * G))

    shared = {
        "wiT": np.ascontiguousarray(ipw.transpose(0, 2, 1)),
        "wxT": np.ascontiguousarray(xpw.transpose(0, 2, 1)),
        "wdtT": np.ascontiguousarray(dtw.transpose(0, 2, 1)),
        "woT": np.ascontiguousarray(opw.transpose(0, 2, 1)),
        "bi_c": colmap(ipb),
        "cw_c": np.ascontiguousarray(cw.reshape(NLAYER, NDT, 128, 5)
                                     .transpose(2, 0, 1, 3).reshape(128, -1)),
        "cb_c": colmap(cb), "dtb_c": colmap(dtb),
        "alog_c": np.ascontiguousarray(alog.reshape(NLAYER, NDT, 128, NST)
                                       .transpose(2, 0, 1, 3).reshape(128, -1)),
        "d_c": colmap(dd), "nw_c": colmap(nw),
        "nfw_r": np.ascontiguousarray(nfw.reshape(1, DM)),
        "bo_r": np.ascontiguousarray(opb.reshape(1, NLAYER * DM)),
        "wg": wg, "wu": wu, "wd": wd,
    }
    in_maps = []
    for c in range(NCORES):
        b, q = c // 4, c % 4
        oh = np.zeros((1, 24), np.float32)
        oh[0, c] = 1.0
        if q > 0:
            oh[0, 8 + (c - 1)] = 1.0
        if q < 3:
            oh[0, 16 + (c + 1)] = 1.0
        m = dict(shared)
        m["x"] = np.ascontiguousarray(x[b, q * TAU:(q + 1) * TAU, :])
        m["oh"] = oh
        in_maps.append(m)
    return in_maps


def kernel(**inputs):
    if _NC_CACHE[0] is None:
        _NC_CACHE[0] = build_program()
    nc = _NC_CACHE[0]
    in_maps = _stage_inputs(inputs)
    res = run_bass_kernel_spmd(nc, in_maps, core_ids=list(range(NCORES)))
    out = np.empty((2, 8192, DM), np.float32)
    for c in range(NCORES):
        b, q = c // 4, c % 4
        out[b, q * TAU:(q + 1) * TAU, :] = res.results[c]["out"]
    return out


# version 07
# speedup vs baseline: 1.1931x; 1.0016x over previous
import numpy as np
import ml_dtypes
import concourse.bass as bass
import concourse.mybir as mybir
from concourse.bass_utils import run_bass_kernel_spmd

F32 = mybir.dt.float32
BF16 = mybir.dt.bfloat16
AF = mybir.ActivationFunctionType
OP = mybir.AluOpType
AX = mybir.AxisListType

NCORES = 8
TAU = 2048
NT = 16
DM = 1024
DI = 2048
NDT = 16
NST = 16
DTR = 64
II = 2816
NIT = 22
NLAYER = 2
NTB = 4
MIDS = [0.25, 0.75, 1.25, 1.75, 2.5, 3.5, 5.0]
EPS = 1e-5


class Prog:
    ENGINES = ("sync", "vector", "scalar", "tensor", "gpsimd")

    def __init__(self, nc):
        self.nc = nc
        self.q = {e: [] for e in self.ENGINES}
        self.sems = {}
        self.cnt = {}

    def sem(self, name):
        if name not in self.sems:
            self.sems[name] = self.nc.alloc_semaphore(name=name)
            self.cnt[name] = 0
        return self.sems[name]

    def op(self, eng, fn, sem=None, amt=1):
        s = self.sem(sem) if sem else None
        if sem:
            self.cnt[sem] += amt

        def run(e, fn=fn, s=s, amt=amt):
            ins = fn(e)
            if s is not None:
                ins.then_inc(s, amt)
        self.q[eng].append(run)
        return self.cnt[sem] if sem else None

    def wait(self, eng, sem, val=None):
        s = self.sem(sem)
        v = self.cnt[sem] if val is None else val

        def run(e, s=s, v=v):
            e.wait_ge(s, v)
        self.q[eng].append(run)

    def val(self, sem):
        return self.cnt.get(sem, 0)

    def replay(self, block):
        for name in self.ENGINES:
            dec = getattr(block, name)
            ops = self.q[name]

            def body(e, ops=ops):
                for f in ops:
                    f(e)
            dec(body)


SEM_IN, SEM_MM, SEM_ACT, SEM_DVE, SEM_GP, SEM_OUT, SEM_CC = (
    "s_in", "s_mm", "s_act", "s_dve", "s_gp", "s_out", "s_cc")


def build_program(dbg=False, nlay=NLAYER):
    nc = bass.Bass("TRN2", detect_race_conditions=False, target_bir_lowering=False, debug=False)
    nc.num_devices = NCORES
    p = Prog(nc)

    def din(name, shape, dtype=F32):
        return nc.dram_tensor(name, shape, dtype, kind="ExternalInput").ap()

    x_in = din("x", [TAU, DM])
    wiT_h = din("wiT", [NLAYER, DM, 2 * DI])
    wxT_h = din("wxT", [NLAYER, DI, 96])
    wdtT_h = din("wdtT", [NLAYER, DTR, DI])
    woT_h = din("woT", [NLAYER, DI, DM])
    bi_h = din("bi_c", [128, NLAYER * 32])
    cw_h = din("cw_c", [128, NLAYER * NDT * 5])
    cb_h = din("cb_c", [128, NLAYER * NDT])
    dtb_h = din("dtb_c", [128, NLAYER * NDT])
    alog_h = din("alog_c", [128, NLAYER * NDT * NST])
    dd_h = din("d_c", [128, NLAYER * NDT])
    nw_h = din("nw_c", [128, NLAYER * 8])
    nfw_h = din("nfw_r", [1, DM])
    bo_h = din("bo_r", [1, NLAYER * DM])
    wg_h = din("wg", [II, DM])
    wu_h = din("wu", [II, DM])
    wd_h = din("wd", [DM, II])
    oh_h = din("oh", [1, 24])
    idm_h = din("idm", [128, 128])
    idmb_h = din("idmb", [128, 128], BF16)

    out_h = nc.dram_tensor("out", [TAU, DM], F32, kind="ExternalOutput").ap()

    def scr(name, shape, dtype=F32, force_internal=False):
        kind = "ExternalOutput" if (dbg and not force_internal) else "Internal"
        return nc.dram_tensor(name, shape, dtype, kind=kind).ap()

    wgT_q = scr("wgT_q", [DM, II], BF16)
    wuT_q = scr("wuT_q", [DM, II], BF16)
    wdT_q = scr("wdT_q", [II, DM], BF16)
    x_sp = scr("x_sp", [NT, 128, DM])
    h_sp = scr("h_sp", [NT, 128, DM])
    res_sp = scr("res_sp", [NDT, 128, TAU])
    del_sp = scr("del_sp", [NDT, 128, TAU])
    w_sp = scr("w_sp", [NDT, 128, TAU])
    yd_sp = scr("yd_sp", [NDT, 128, TAU])
    y_sp = scr("y_sp", [NDT, 128, TAU])
    u_sp = scr("u_sp", [NDT, 128, TAU])
    bc_sp = scr("bc_sp", [32, TAU])
    scan_dbg = scr("scan_dbg", [4, 128, TAU])
    xln_dbg = scr("xln_dbg", [NT, 128, DM])
    xq_dbg = scr("xq_dbg", [NT, 128, DM], BF16)
    h_dbg = scr("h_dbg", [NT, 128, II])
    gin_h = scr("gin_h", [128, NDT * 4], force_internal=True)
    gout_h = scr("gout_h", [NCORES, 128, NDT * 4], force_internal=True)
    gin_s = scr("gin_s", [128, 512], force_internal=True)
    gout_s = scr("gout_s", [NCORES, 128, 512], force_internal=True)

    def sball(name, shape, dtype=F32):
        return nc.alloc_sbuf_tensor(name, shape, dtype).ap()

    identf = sball("identf", [128, 128], F32)
    identb = sball("identb", [128, 128], BF16)
    onesr = sball("onesr", [1, 128], F32)
    bi_sb = sball("bi_sb", [128, NLAYER * 32], F32)
    cw_sb = sball("cw_sb", [128, NLAYER * NDT * 5], F32)
    cb_sb = sball("cb_sb", [128, NLAYER * NDT], F32)
    dtb_sb = sball("dtb_sb", [128, NLAYER * NDT], F32)
    a_sb = sball("a_sb", [128, NLAYER * NDT * NST], F32)
    d_sb = sball("d_sb", [128, NLAYER * NDT], F32)
    nw_sb = sball("nw_sb", [128, NLAYER * 8], F32)
    bo_sb = sball("bo_sb", [1, NLAYER * DM], F32)
    oh_sb = sball("oh_sb", [128, 24], F32)
    stat_sb = sball("stat_sb", [128, 16], F32)
    scol = sball("scol", [128, 1088], F32)
    halo_sb = sball("halo_sb", [128, NDT * 4], F32)

    stream = sball("stream", [128, NT * DM], F32)
    bigA = sball("bigA", [128, 12 * 1024], F32)
    bigB = sball("bigB", [128, 32 * 1024], BF16)
    wbuf = sball("wbuf", [128, 3 * 1024], F32)

    ps = [nc.alloc_psum_tensor(f"ps{i}", [128, 512], F32).ap() for i in range(4)]
    pst = [nc.alloc_psum_tensor(f"pst{i}", [128, 1024], BF16).ap()[:, 0:128]
           for i in range(4)]

    def wb_bf16(off_f32, n):
        return wbuf[:, off_f32:off_f32 + (n + 1) // 2].bitcast(BF16)[:, 0:n]

    stream_b = stream.bitcast(BF16)

    bank_free = {}

    def bank_wait(eng, bank):
        if id(bank) in bank_free:
            sname, v = bank_free[id(bank)]
            p.wait(eng, sname, v)

    def bank_mark(bank, sname):
        bank_free[id(bank)] = (sname, p.val(sname))

    def dma_in(dst, src):
        return p.op("sync", lambda e, d=dst, s=src: e.dma_start(out=d, in_=s), SEM_IN, 16)

    def dma_out(dst, src):
        return p.op("sync", lambda e, d=dst, s=src: e.dma_start(out=d, in_=s), SEM_OUT, 16)

    def v_op(fn):
        return p.op("vector", fn, SEM_DVE)

    def g_op(fn):
        return p.op("gpsimd", fn, SEM_GP)

    def s_op(fn):
        return p.op("scalar", fn, SEM_ACT)

    def t_op(fn):
        return p.op("tensor", fn, SEM_MM)

    def vpad():
        v_op(lambda e: e.tensor_copy(out=stat_sb[:, 12:13], in_=identf[:, 0:1]))

    dma_in(stream.rearrange("p (a b) -> p a b", b=DM), x_in.rearrange("(a p) b -> p a b", p=128))
    dma_in(bi_sb, bi_h)
    dma_in(cw_sb, cw_h)
    dma_in(cb_sb, cb_h)
    dma_in(dtb_sb, dtb_h)
    dma_in(a_sb, alog_h)
    dma_in(d_sb, dd_h)
    dma_in(nw_sb, nw_h)
    dma_in(bo_sb, bo_h)
    dma_in(oh_sb, oh_h.partition_broadcast(128))
    dma_in(identf[:], idm_h)
    dma_in(identb[:], idmb_h)
    setup_in = p.val(SEM_IN)

    p.wait("gpsimd", SEM_IN, setup_in)
    p.wait("vector", SEM_IN, setup_in)
    v_op(lambda e: e.memset(onesr[:], 1.0))
    p.wait("scalar", SEM_IN, setup_in)
    s_op(lambda e: e.activation(out=bigA[:, 0:NLAYER * NDT * NST], in_=a_sb[:],
                                func=AF.Exp))
    p.wait("vector", SEM_ACT, p.val(SEM_ACT))
    v_op(lambda e: e.tensor_single_scalar(out=a_sb[:],
                                          in_=bigA[:, 0:NLAYER * NDT * NST],
                                          scalar=-1.0, op=OP.mult))
    p.wait("tensor", SEM_IN, setup_in)

    def quant(dst, src, F, aoff, boff):
        nb = F // 16
        xn = bigA[:, aoff:aoff + F]
        qa = bigA[:, aoff + F:aoff + 2 * F]
        xsq = bigA[:, aoff + 2 * F:aoff + 3 * F]
        bm = bigA[:, aoff + 3 * F:aoff + 3 * F + nb]
        scl = bigA[:, aoff + 3 * F + nb:aoff + 3 * F + 2 * nb]
        rcp = bigA[:, aoff + 3 * F + 2 * nb:aoff + 3 * F + 3 * nb]
        m0 = bigB[:, boff:boff + F]
        m1 = bigB[:, boff + F:boff + 2 * F]
        m2 = bigB[:, boff + 2 * F:boff + 3 * F]
        src3 = src.rearrange("p (b k) -> p b k", k=16)
        v_op(lambda e: e.tensor_reduce(out=bm, in_=src3, axis=AX.X, op=OP.max,
                                       apply_absolute_value=True))
        vpad()
        v_op(lambda e: e.tensor_scalar(out=scl, in0=bm, scalar1=1.0 / 6.0,
                                       scalar2=1e-12, op0=OP.mult, op1=OP.max))
        vpad()
        v_op(lambda e: e.reciprocal(out=rcp, in_=scl))
        vpad()
        v_op(lambda e: e.tensor_tensor(
            out=xn.rearrange("p (b k) -> p b k", k=16), in0=src3,
            in1=rcp.rearrange("p (b o) -> p b o", o=1).broadcast_to([128, nb, 16]),
            op=OP.mult))
        dv = p.val(SEM_DVE)
        p.wait("scalar", SEM_DVE, dv)
        avq = s_op(lambda e: e.activation(out=xsq, in_=xn, func=AF.Square))

        def mk(dst_t, th):
            return lambda e: e.tensor_single_scalar(out=dst_t, in_=xsq,
                                                    scalar=th * th, op=OP.is_ge)
        p.wait("vector", SEM_ACT, avq)
        v_op(mk(m0, MIDS[0]))
        v_op(mk(m1, MIDS[1]))
        v_op(lambda e: e.tensor_tensor(out=m0, in0=m0, in1=m1, op=OP.add))
        v_op(mk(m1, MIDS[2]))
        v_op(lambda e: e.tensor_tensor(out=m0, in0=m0, in1=m1, op=OP.add))
        v_op(mk(m1, MIDS[3]))
        v_op(lambda e: e.tensor_tensor(out=m0, in0=m0, in1=m1, op=OP.add))
        v_op(mk(m2, MIDS[4]))
        v_op(mk(m1, MIDS[5]))
        v_op(lambda e: e.tensor_tensor(out=m2, in0=m2, in1=m1, op=OP.add))
        v_op(mk(m1, MIDS[6]))
        v_op(lambda e: e.scalar_tensor_tensor(out=qa, in0=m0, scalar=0.5, in1=m2,
                                              op0=OP.mult, op1=OP.add))
        v_op(lambda e: e.scalar_tensor_tensor(out=qa, in0=m1, scalar=2.0, in1=qa,
                                              op0=OP.mult, op1=OP.add))
        v_op(lambda e: e.tensor_scalar(out=m1, in0=xn, scalar1=0.0, scalar2=2.0,
                                       op0=OP.is_ge, op1=OP.mult))
        v_op(lambda e: e.tensor_tensor(out=xn, in0=qa, in1=m1, op=OP.mult))
        v_op(lambda e: e.tensor_tensor(out=qa, in0=xn, in1=qa, op=OP.subtract))
        v_op(lambda e: e.tensor_tensor(
            out=dst.rearrange("p (b k) -> p b k", k=16),
            in0=qa.rearrange("p (b k) -> p b k", k=16),
            in1=scl.rearrange("p (b o) -> p b o", o=1).broadcast_to([128, nb, 16]),
            op=OP.mult))

    def transpose_tiles(src, ntile, dst_ap_of_t):
        for t in range(ntile):
            pb = pst[t % 4]
            bank_wait("tensor", pb)
            t_op(lambda e, pb=pb, t=t: e.transpose(
                out=pb[:], in_=src[:, t * 128:(t + 1) * 128], identity=identb[:]))
            p.wait("scalar", SEM_MM, p.val(SEM_MM))
            s_op(lambda e, pb=pb, d=dst_ap_of_t(t): e.copy(out=d, in_=pb[:]))
            bank_mark(pb, SEM_ACT)

    def quant_weights():
        bigBf0 = bigB.bitcast(F32)
        wtiles = [bigA[:, 9000:9000 + 2816], bigBf0[:, 8192:8192 + 2816]]
        qtiles = [bigB[:, 0:2816], bigB[:, 2816:5632]]
        jobs = []
        for whbm, dst, rows, cols in ((wg_h, wgT_q, NIT, DM), (wu_h, wuT_q, NIT, DM),
                                      (wd_h, wdT_q, 8, II)):
            for r in range(rows):
                jobs.append((whbm, dst, r, cols))
        lds = {}
        wfree = {0: [], 1: []}
        qfree = {0: [], 1: []}
        for i in range(len(jobs) + 1):
            if i < len(jobs):
                whbm, dst, r, cols = jobs[i]
                par = i % 2
                for sem, v in wfree[par]:
                    p.wait("sync", sem, v)
                wfree[par] = []
                lds[i] = dma_in(wtiles[par][:, 0:cols], whbm[r * 128:(r + 1) * 128, :])
            if i == 0:
                continue
            whbm, dst, r, cols = jobs[i - 1]
            par = (i - 1) % 2
            wt_, qt_ = wtiles[par], qtiles[par]
            p.wait("vector", SEM_IN, lds[i - 1])
            for sem, v in qfree[par]:
                p.wait("vector", sem, v)
            qfree[par] = []
            quant(qt_[:, 0:cols], wt_[:, 0:cols], cols, 0, 22016)
            wfree[par] = [(SEM_DVE, p.val(SEM_DVE))]
            dv = p.val(SEM_DVE)
            p.wait("tensor", SEM_DVE, dv)
            for t in range(cols // 128):
                pb = pst[t % 4]
                ev = wb_bf16(2048 + (t % 4) * 64, 128)
                bank_wait("tensor", pb)
                t_op(lambda e, pb=pb, t=t, qt_=qt_: e.transpose(
                    out=pb[:], in_=qt_[:, t * 128:(t + 1) * 128],
                    identity=identb[:]))
                p.wait("scalar", SEM_MM, p.val(SEM_MM))
                p.wait("scalar", SEM_OUT, p.val(SEM_OUT))
                av = s_op(lambda e, ev=ev, pb=pb: e.copy(out=ev, in_=pb[:]))
                bank_mark(pb, SEM_ACT)
                p.wait("sync", SEM_ACT, av)
                dma_out(dst[t * 128:(t + 1) * 128, r * 128:(r + 1) * 128], ev)
            qfree[par] = [(SEM_MM, p.val(SEM_MM))]

    quant_weights()

    def ln_tile(src_t, dst):
        s1, s2, s3 = stat_sb[:, 0:1], stat_sb[:, 1:2], stat_sb[:, 2:3]
        t2 = bigA[:, 4096:5120]
        v_op(lambda e: e.tensor_reduce(out=s1, in_=src_t, axis=AX.X, op=OP.add))
        dv = p.val(SEM_DVE)
        p.wait("scalar", SEM_DVE, dv)
        av = s_op(lambda e: e.activation(out=t2, in_=src_t, func=AF.Square))
        p.wait("vector", SEM_ACT, av)
        v_op(lambda e: e.tensor_reduce(out=s2, in_=t2, axis=AX.X, op=OP.add))
        v_op(lambda e: e.tensor_single_scalar(out=s1, in_=s1, scalar=1.0 / DM, op=OP.mult))
        v_op(lambda e: e.tensor_single_scalar(out=s2, in_=s2, scalar=1.0 / DM, op=OP.mult))
        v_op(lambda e: e.tensor_tensor(out=s3, in0=s1, in1=s1, op=OP.mult))
        vpad()
        v_op(lambda e: e.tensor_tensor(out=s2, in0=s2, in1=s3, op=OP.subtract))
        vpad()
        v_op(lambda e: e.tensor_single_scalar(out=s2, in_=s2, scalar=EPS, op=OP.add))
        dv = p.val(SEM_DVE)
        p.wait("scalar", SEM_DVE, dv)
        av = s_op(lambda e: e.activation(out=s3, in_=s2, func=AF.Sqrt))
        p.wait("vector", SEM_ACT, av)
        v_op(lambda e: e.reciprocal(out=s2, in_=s3))
        vpad()
        v_op(lambda e: e.tensor_scalar(out=dst, in0=src_t, scalar1=s1, scalar2=s2,
                                       op0=OP.subtract, op1=OP.mult))

    def rms_rstd(src_t, rstd_col):
        s2, s3 = stat_sb[:, 4:5], stat_sb[:, 5:6]
        t2 = bigA[:, 4096:5120]
        av = s_op(lambda e: e.activation(out=t2, in_=src_t, func=AF.Square))
        p.wait("vector", SEM_ACT, av)
        v_op(lambda e: e.tensor_reduce(out=s2, in_=t2, axis=AX.X, op=OP.add))
        vpad()
        v_op(lambda e: e.tensor_scalar(out=s2, in0=s2, scalar1=1.0 / DM, scalar2=EPS,
                                       op0=OP.mult, op1=OP.add))
        dv = p.val(SEM_DVE)
        p.wait("scalar", SEM_DVE, dv)
        av = s_op(lambda e: e.activation(out=s3, in_=s2, func=AF.Sqrt))
        p.wait("vector", SEM_ACT, av)
        v_op(lambda e, r=rstd_col: e.reciprocal(out=r, in_=s3))
        vpad()

    NCH = [512, 512, 512, 512, 512, 256]

    def ffn_call(dump=False):
        p.wait("sync", SEM_OUT, p.val(SEM_OUT))
        for th in range(2):
            for tt in range(8):
                ti = th * 8 + tt
                src_t = stream[:, ti * DM:(ti + 1) * DM]
                xln = bigA[:, 5120:6144]
                ln_tile(src_t, xln)
                xq = bigB[:, 8192:9216]
                p.wait("vector", SEM_MM, p.val(SEM_MM))
                if dump:
                    dvx = p.val(SEM_DVE)
                    p.wait("sync", SEM_DVE, dvx)
                    dma_out(xln_dbg[ti], xln)
                quant(xq, xln, DM, 0, 9216)
                dv = p.val(SEM_DVE)
                p.wait("tensor", SEM_DVE, dv)
                if dump:
                    p.wait("sync", SEM_DVE, dv)
                    dma_out(xq_dbg[ti], xq)
                transpose_tiles(xq, 8,
                                lambda t, tt=tt: bigB[:, t * 1024 + tt * 128:
                                                      t * 1024 + tt * 128 + 128])
            xqT = bigB[:, 0:8192].rearrange("p (a b) -> p a b", b=1024)
            avq = p.val(SEM_ACT)

            hq_base = 8192
            for ci, ch in enumerate(NCH):
                c0 = sum(NCH[:ci])
                wgc = bigA[:, 7168:7168 + 2048].bitcast(BF16)[:, 0:8 * ch] \
                    .rearrange("p (a b) -> p a b", b=ch)
                wuc = bigA[:, 9216:9216 + 2048].bitcast(BF16)[:, 0:8 * ch] \
                    .rearrange("p (a b) -> p a b", b=ch)
                p.wait("sync", SEM_MM, p.val(SEM_MM))
                dma_in(wgc, wgT_q[:, c0:c0 + ch].rearrange("(a p) b -> p a b", p=128))
                l2 = dma_in(wuc, wuT_q[:, c0:c0 + ch].rearrange("(a p) b -> p a b", p=128))
                p.wait("tensor", SEM_IN, l2)
                p.wait("tensor", SEM_ACT, avq)
                for tt in range(8):
                    pg, pu = ps[2 * (tt % 2)], ps[2 * (tt % 2) + 1]
                    bank_wait("tensor", pg)
                    for k in range(8):
                        t_op(lambda e, k=k, tt=tt, ch=ch, pg=pg, wgc=wgc: e.matmul(
                            pg[:, 0:ch], xqT[:, k, tt * 128:(tt + 1) * 128],
                            wgc[:, k, :], start=(k == 0), stop=(k == 7)))
                    bank_wait("tensor", pu)
                    for k in range(8):
                        t_op(lambda e, k=k, tt=tt, ch=ch, pu=pu, wuc=wuc: e.matmul(
                            pu[:, 0:ch], xqT[:, k, tt * 128:(tt + 1) * 128],
                            wuc[:, k, :], start=(k == 0), stop=(k == 7)))
                    mmv = p.val(SEM_MM)
                    p.wait("scalar", SEM_MM, mmv)
                    hsil = bigA[:, 6144:6656]
                    hch = bigA[:, 6656:7168]
                    p.wait("scalar", SEM_DVE, p.val(SEM_DVE))
                    av = s_op(lambda e, pg=pg, ch=ch, hsil=hsil: e.activation(
                        out=hsil[:, 0:ch], in_=pg[:, 0:ch], func=AF.Silu))
                    bank_mark(pg, SEM_ACT)
                    p.wait("vector", SEM_ACT, av)
                    v_op(lambda e, pu=pu, ch=ch, hsil=hsil, hch=hch: e.tensor_tensor(
                        out=hch[:, 0:ch], in0=hsil[:, 0:ch], in1=pu[:, 0:ch],
                        op=OP.mult))
                    bank_mark(pu, SEM_DVE)
                    if dump:
                        dvh = p.val(SEM_DVE)
                        p.wait("sync", SEM_DVE, dvh)
                        dma_out(h_dbg[th * 8 + tt][:, c0:c0 + ch], hch[:, 0:ch])
                    hqc = bigB[:, 30720:30720 + ch]
                    p.wait("vector", SEM_MM, p.val(SEM_MM))
                    quant(hqc, hch[:, 0:ch], ch, 0, 31232)
                    dv = p.val(SEM_DVE)
                    p.wait("tensor", SEM_DVE, dv)
                    transpose_tiles(hqc, ch // 128,
                                    lambda t, c0=c0, tt=tt: bigB[
                                        :, hq_base + (c0 // 128 + t) * 1024 + tt * 128:
                                        hq_base + (c0 // 128 + t) * 1024 + tt * 128 + 128])
            hqT = bigB[:, hq_base:hq_base + 22528].rearrange("p (a b) -> p a b", b=1024)
            avh = p.val(SEM_ACT)

            for dc in range(4):
                wdc = wb_bf16(0, 22 * 256).rearrange("p (a b) -> p a b", b=256)
                p.wait("sync", SEM_MM, p.val(SEM_MM))
                ld = dma_in(wdc, wdT_q[:, dc * 256:(dc + 1) * 256]
                            .rearrange("(a p) b -> p a b", p=128))
                p.wait("tensor", SEM_IN, ld)
                p.wait("tensor", SEM_ACT, avh)
                for tt in range(8):
                    ti = th * 8 + tt
                    pb = ps[2 + (tt % 2)]
                    bank_wait("tensor", pb)
                    for k in range(22):
                        t_op(lambda e, k=k, tt=tt, pb=pb, wdc=wdc: e.matmul(
                            pb[:, 0:256], hqT[:, k, tt * 128:(tt + 1) * 128],
                            wdc[:, k, :], start=(k == 0), stop=(k == 21)))
                    mmv = p.val(SEM_MM)
                    p.wait("vector", SEM_MM, mmv)
                    dst = stream[:, ti * DM + dc * 256:ti * DM + (dc + 1) * 256]
                    v_op(lambda e, pb=pb, dst=dst: e.scalar_tensor_tensor(
                        out=dst, in0=pb[:, 0:256], scalar=0.5, in1=dst,
                        op0=OP.mult, op1=OP.add))
                    bank_mark(pb, SEM_DVE)

    def mamba_layer(li):
        xn = bigB[:, 0:16384].rearrange("p (a b) -> p a b", b=DM)
        rstd = stat_sb[:, 6:7]
        p.wait("scalar", SEM_DVE, p.val(SEM_DVE))
        p.wait("sync", SEM_DVE, p.val(SEM_DVE))
        for i in range(NT):
            src_t = stream[:, i * DM:(i + 1) * DM]
            rms_rstd(src_t, rstd)
            v_op(lambda e, s=src_t, i=i: e.tensor_single_scalar(
                out=xn[:, i, :], in_=s, scalar=rstd, op=OP.mult))
            dma_out(h_sp[i], src_t)
        hspv = p.val(SEM_OUT)
        dv = p.val(SEM_DVE)
        p.wait("tensor", SEM_DVE, dv)
        p.wait("scalar", SEM_OUT, hspv)

        xnT = stream_b[:, 0:16384].rearrange("p (a b) -> p a b", b=2048)
        for i in range(NT):
            transpose_tiles(xn[:, i, :], 8,
                            lambda t, i=i: xnT[:, t, i * 128:(i + 1) * 128])
        avT = p.val(SEM_ACT)
        p.wait("tensor", SEM_ACT, avT)

        xs = bigB[:, 0:16 * 2048].rearrange("p (a b) -> p a b", b=2048)
        wic_f = bigA[:, 0:1024].rearrange("p (a b) -> p a b", b=128)
        wic = wb_bf16(2048, 8 * 128).rearrange("p (a b) -> p a b", b=128)
        rtmp = bigA[:, 1024:1536]
        for mt in range(32):
            p.wait("sync", SEM_MM, p.val(SEM_MM))
            ld = dma_in(wic_f, wiT_h[li][:, mt * 128:(mt + 1) * 128]
                        .rearrange("(a p) b -> p a b", p=128))
            p.wait("vector", SEM_IN, ld)
            v_op(lambda e: e.tensor_tensor(
                out=wic, in0=wic_f,
                in1=nw_sb[:, li * 8:(li + 1) * 8].rearrange("p (a o) -> p a o", o=1)
                    .broadcast_to([128, 8, 128]),
                op=OP.mult))
            cv = p.val(SEM_DVE)
            p.wait("tensor", SEM_DVE, cv)
            for tb in range(NTB):
                pb = ps[tb % 2]
                bank_wait("tensor", pb)
                for k in range(8):
                    t_op(lambda e, k=k, tb=tb, pb=pb: e.matmul(
                        pb[:], wic[:, k, :], xnT[:, k, tb * 512:(tb + 1) * 512],
                        start=(k == 0), stop=(k == 7)))
                p.wait("scalar", SEM_MM, p.val(SEM_MM))
                bcol = bi_sb[:, li * 32 + mt:li * 32 + mt + 1]
                if mt < 16:
                    s_op(lambda e, pb=pb, mt=mt, tb=tb, bcol=bcol: e.add(
                        out=xs[:, mt, tb * 512:(tb + 1) * 512], in_=pb[:], add=bcol))
                else:
                    p.wait("scalar", SEM_OUT, p.val(SEM_OUT))
                    av = s_op(lambda e, pb=pb, bcol=bcol: e.add(
                        out=rtmp, in_=pb[:], add=bcol))
                    p.wait("sync", SEM_ACT, av)
                    dma_out(res_sp[mt - 16][:, tb * 512:(tb + 1) * 512], rtmp)
                bank_mark(pb, SEM_ACT)

        hpf = bigA[:, 2048:2048 + 64]
        xsv = p.val(SEM_ACT)
        p.wait("vector", SEM_ACT, xsv)
        for dt in range(NDT):
            v_op(lambda e, dt=dt: e.tensor_copy(
                out=hpf[:, dt * 4:dt * 4 + 2], in_=xs[:, dt, 0:2]))
            v_op(lambda e, dt=dt: e.tensor_copy(
                out=hpf[:, dt * 4 + 2:dt * 4 + 4], in_=xs[:, dt, 2046:2048]))
        hv = p.val(SEM_DVE)
        p.wait("sync", SEM_DVE, hv)
        gv = dma_out(gin_h, hpf)
        p.wait("gpsimd", SEM_OUT, gv)
        p.op("gpsimd", lambda e: e.collective_compute(
            "AllGather", OP.bypass, replica_groups=[list(range(NCORES))],
            ins=[gin_h], outs=[gout_h]), SEM_CC)
        p.wait("sync", SEM_CC, p.val(SEM_CC))
        gl = bigA[:, 2112:2112 + 512].rearrange("p (a b) -> p a b", b=64)
        lg = dma_in(gl, gout_h.rearrange("c p f -> p c f"))
        p.wait("vector", SEM_IN, lg)
        v_op(lambda e: e.memset(halo_sb[:], 0.0))
        vpad()
        vpad()
        hl4 = halo_sb.rearrange("p (a b) -> p a b", b=4)
        for s in range(NCORES):
            v_op(lambda e, s=s: e.scalar_tensor_tensor(
                out=hl4[:, :, 0:2],
                in0=gl[:, s, :].rearrange("p (a b) -> p a b", b=4)[:, :, 2:4],
                scalar=oh_sb[:, 8 + s:9 + s],
                in1=hl4[:, :, 0:2],
                op0=OP.mult, op1=OP.add))
            v_op(lambda e, s=s: e.scalar_tensor_tensor(
                out=hl4[:, :, 2:4],
                in0=gl[:, s, :].rearrange("p (a b) -> p a b", b=4)[:, :, 0:2],
                scalar=oh_sb[:, 16 + s:17 + s],
                in1=hl4[:, :, 2:4],
                op0=OP.mult, op1=OP.add))
        halo_b = wb_bf16(1280, NDT * 4).rearrange("p (a b) -> p a b", b=4)
        vpad()
        vpad()
        v_op(lambda e: e.tensor_copy(out=halo_b.rearrange("p a b -> p (a b)"),
                                     in_=halo_sb[:]))
        hv2 = p.val(SEM_DVE)
        p.wait("tensor", SEM_DVE, hv2)

        ub = bigA[:, 0:4096].bitcast(BF16).rearrange("p (a b) -> p a b", b=512)
        uf = bigA[:, 4096:4608]
        r_t = bigA[:, 4608:5120]
        m_t = bigA[:, 5120:5632]
        e_t = bigA[:, 5632:6144]
        d_t = bigA[:, 6144:6656]
        w_t = bigA[:, 6656:7168]
        y_t = bigA[:, 7168:7680]
        ur_t = bigA[:, 7680:8192]
        diag = wb_bf16(0, 5 * 128).rearrange("p (a b) -> p a b", b=128)
        wxc_f = wbuf[:, 1536:1536 + 1536].rearrange("p (a b) -> p a b", b=96)
        wxc = wb_bf16(512, 16 * 96).rearrange("p (a b) -> p a b", b=96)
        ldx = dma_in(wxc_f, wxT_h[li].rearrange("(a p) b -> p a b", p=128))
        p.wait("vector", SEM_IN, ldx)
        v_op(lambda e: e.tensor_copy(out=wxc.rearrange("p a b -> p (a b)"),
                                     in_=wxc_f.rearrange("p a b -> p (a b)")))
        wxv = p.val(SEM_DVE)
        p.wait("tensor", SEM_DVE, wxv)
        xdbl = stream[0:96, 8192:10240]
        dtw = wbuf[0:64, 1536:1536 + 128]
        cdT = scol[:, 768:784]
        for tb in range(NTB):
            for dt in range(NDT):
                p.wait("vector", SEM_MM, p.val(SEM_MM))
                for j in range(5):
                    v_op(lambda e, j=j, dt=dt: e.tensor_single_scalar(
                        out=diag[:, j, :], in_=identf[:],
                        scalar=cw_sb[:, (li * NDT + dt) * 5 + j:
                                     (li * NDT + dt) * 5 + j + 1], op=OP.mult))
                dv = p.val(SEM_DVE)
                p.wait("tensor", SEM_DVE, dv)
                pb = ps[dt % 2]
                bank_wait("tensor", pb)
                segs = []
                for j in range(5):
                    lo = tb * 512 + j - 2
                    hi = lo + 512
                    if lo < 0:
                        nh = -lo
                        segs.append((j, pb[:, 0:nh], halo_b[:, dt, 2 - nh:2]))
                        segs.append((j, pb[:, nh:512], xs[:, dt, 0:512 - nh]))
                    elif hi > 2048:
                        nh = hi - 2048
                        segs.append((j, pb[:, 0:512 - nh], xs[:, dt, lo:2048]))
                        segs.append((j, pb[:, 512 - nh:512], halo_b[:, dt, 2:2 + nh]))
                    else:
                        segs.append((j, pb[:], xs[:, dt, lo:lo + 512]))
                for si, (j, pslice, mov) in enumerate(segs):
                    t_op(lambda e, j=j, pslice=pslice, mov=mov, si=si,
                         last=(si == len(segs) - 1): e.matmul(
                        pslice, diag[:, j, :], mov, start=(si == 0), stop=last))
                p.wait("scalar", SEM_MM, p.val(SEM_MM))
                ccol = cb_sb[:, li * NDT + dt:li * NDT + dt + 1]
                p.wait("scalar", SEM_OUT, p.val(SEM_OUT))
                p.wait("scalar", SEM_DVE, p.val(SEM_DVE))
                av = s_op(lambda e, pb=pb, ccol=ccol: e.activation(
                    out=uf, in_=pb[:], func=AF.Silu, bias=ccol))
                bank_mark(pb, SEM_ACT)
                p.wait("vector", SEM_ACT, av)
                v_op(lambda e, dt=dt: e.tensor_copy(out=ub[:, dt, :], in_=uf))
                p.wait("sync", SEM_ACT, av)
                dma_out(u_sp[dt][:, tb * 512:(tb + 1) * 512], uf)
            p.wait("sync", SEM_OUT, p.val(SEM_OUT))
            dv = p.val(SEM_DVE)
            p.wait("tensor", SEM_DVE, dv)
            pbx = ps[2]
            bank_wait("tensor", pbx)
            for k in range(NDT):
                t_op(lambda e, k=k, pbx=pbx: e.matmul(
                    pbx[0:96, :], wxc[:, k, :], ub[:, k, :],
                    start=(k == 0), stop=(k == NDT - 1)))
            mv = p.val(SEM_MM)
            p.wait("scalar", SEM_MM, mv)
            av = s_op(lambda e, pbx=pbx, tb=tb: e.copy(
                out=xdbl[:, tb * 512:(tb + 1) * 512], in_=pbx[0:96, :]))
            bank_mark(pbx, SEM_ACT)
            p.wait("tensor", SEM_ACT, av)
            for mt in range(NDT):
                p.wait("sync", SEM_MM, p.val(SEM_MM))
                p.wait("sync", SEM_DVE, p.val(SEM_DVE))
                ldt = dma_in(dtw, wdtT_h[li][:, mt * 128:(mt + 1) * 128])
                dma_in(ur_t, u_sp[mt][:, tb * 512:(tb + 1) * 512])
                p.wait("tensor", SEM_IN, ldt)
                pb2 = ps[3]
                bank_wait("tensor", pb2)
                t_op(lambda e, tb=tb, pb2=pb2: e.matmul(
                    pb2[:], dtw, xdbl[0:64, tb * 512:(tb + 1) * 512],
                    start=True, stop=True))
                mv = p.val(SEM_MM)
                p.wait("scalar", SEM_MM, mv)
                p.wait("vector", SEM_MM, mv)
                bcol = dtb_sb[:, li * NDT + mt:li * NDT + mt + 1]
                p.wait("scalar", SEM_OUT, p.val(SEM_OUT))
                p.wait("vector", SEM_OUT, p.val(SEM_OUT))
                av = s_op(lambda e, pb2=pb2, bcol=bcol: e.activation(
                    out=r_t, in_=pb2[:], func=AF.Relu, bias=bcol))
                p.wait("vector", SEM_ACT, p.val(SEM_ACT))
                v_op(lambda e, pb2=pb2: e.scalar_tensor_tensor(
                    out=m_t, in0=r_t, scalar=2.0, in1=pb2[:],
                    op0=OP.mult, op1=OP.subtract))
                v_op(lambda e, bcol=bcol: e.tensor_single_scalar(
                    out=m_t, in_=m_t, scalar=bcol, op=OP.subtract))
                bank_mark(pb2, SEM_DVE)
                dv = p.val(SEM_DVE)
                p.wait("scalar", SEM_DVE, dv)
                av = s_op(lambda e: e.activation(out=e_t, in_=m_t, func=AF.Exp,
                                                 scale=-1.0))
                p.wait("vector", SEM_ACT, av)
                v_op(lambda e: e.tensor_single_scalar(out=e_t, in_=e_t, scalar=1.0,
                                                      op=OP.add))
                dv = p.val(SEM_DVE)
                p.wait("scalar", SEM_DVE, dv)
                av = s_op(lambda e: e.activation(out=m_t, in_=e_t, func=AF.Ln))
                p.wait("vector", SEM_ACT, av)
                p.wait("vector", SEM_IN, p.val(SEM_IN))
                v_op(lambda e: e.tensor_tensor(out=d_t, in0=r_t, in1=m_t, op=OP.add))
                v_op(lambda e: e.tensor_reduce(out=stat_sb[:, 8:9], in_=d_t,
                                               axis=AX.X, op=OP.add))
                vpad()
                if tb == 0:
                    v_op(lambda e, mt=mt: e.tensor_copy(out=cdT[:, mt:mt + 1],
                                                        in_=stat_sb[:, 8:9]))
                else:
                    v_op(lambda e, mt=mt: e.tensor_tensor(
                        out=cdT[:, mt:mt + 1], in0=cdT[:, mt:mt + 1],
                        in1=stat_sb[:, 8:9], op=OP.add))
                v_op(lambda e: e.tensor_tensor(out=w_t, in0=d_t,
                                                   in1=ur_t, op=OP.mult))
                dcol = d_sb[:, li * NDT + mt:li * NDT + mt + 1]
                v_op(lambda e, dcol=dcol: e.tensor_single_scalar(
                    out=y_t, in_=ur_t, scalar=dcol, op=OP.mult))
                dv = p.val(SEM_DVE)
                p.wait("sync", SEM_DVE, dv)
                dma_out(del_sp[mt][:, tb * 512:(tb + 1) * 512], d_t)
                dma_out(w_sp[mt][:, tb * 512:(tb + 1) * 512], w_t)
                dma_out(yd_sp[mt][:, tb * 512:(tb + 1) * 512], y_t)
        p.wait("sync", SEM_ACT, p.val(SEM_ACT))
        bcv = dma_out(bc_sp, xdbl[64:96, :])
        p.wait("sync", SEM_OUT, p.val(SEM_OUT))

        reps = stream
        bigBf = bigB.bitcast(F32)
        dltb = [bigA[:, 0:2048], bigA[:, 2048:4096]]
        wtb = [bigA[:, 4096:6144], bigA[:, 6144:8192]]
        ytb = [bigA[:, 8192:10240], bigA[:, 10240:12288]]
        ab = [bigBf[:, 0:2048], bigBf[:, 2048:4096]]
        zb = [bigBf[:, 4096:6144], bigBf[:, 6144:8192]]
        hb = [bigBf[:, 8192:10240], bigBf[:, 10240:12288]]
        tb_ = [bigBf[:, 12288:14336], bigBf[:, 14336:16384]]
        Mv = scol[:, 0:256]
        hend = scol[:, 256:512]
        hin = scol[:, 512:768]
        tc = scol[:, 800:1056]
        ld_free = {0: [], 1: []}
        ab_free = {0: [], 1: []}
        zb_free = {0: [], 1: []}
        for ng in range(4):
            p.wait("sync", SEM_GP, p.val(SEM_GP))
            p.wait("sync", SEM_DVE, p.val(SEM_DVE))
            p.wait("sync", SEM_MM, p.val(SEM_MM))
            p.wait("sync", SEM_ACT, p.val(SEM_ACT))
            p.wait("sync", SEM_OUT, p.val(SEM_OUT))
            rl = None
            for j in range(4):
                n = ng * 4 + j
                dma_in(reps[:, (2 * j) * 2048:(2 * j + 1) * 2048],
                       bc_sp[n:n + 1, :].partition_broadcast(128))
                rl = dma_in(reps[:, (2 * j + 1) * 2048:(2 * j + 2) * 2048],
                            bc_sp[16 + n:17 + n, :].partition_broadcast(128))
            p.wait("vector", SEM_IN, rl)
            p.wait("gpsimd", SEM_IN, rl)
            ld_done = {}
            for dt in range(NDT + 1):
                if dt < NDT:
                    par = dt % 2
                    for sem, v in ld_free[par]:
                        p.wait("sync", sem, v)
                    ld_free[par] = []
                    dma_in(dltb[par], del_sp[dt])
                    dma_in(wtb[par], w_sp[dt])
                    ysrc = yd_sp[dt] if ng == 0 else y_sp[dt]
                    ld_done[dt] = dma_in(ytb[par], ysrc)
                if dt == 0:
                    continue
                cdt = dt - 1
                par = cdt % 2
                dlt_c, wt_c, yt_c = dltb[par], wtb[par], ytb[par]
                p.wait("scalar", SEM_IN, ld_done[cdt])
                p.wait("gpsimd", SEM_IN, ld_done[cdt])
                p.wait("vector", SEM_IN, ld_done[cdt])
                for j in range(4):
                    n = ng * 4 + j
                    bpar = j % 2
                    acol = a_sb[:, (li * NDT + cdt) * NST + n:
                                (li * NDT + cdt) * NST + n + 1]
                    for sem, v in ab_free[bpar]:
                        p.wait("scalar", sem, v)
                    ab_free[bpar] = []
                    av = s_op(lambda e, acol=acol, A=ab[bpar], D=dlt_c: e.activation(
                        out=A, in_=D, func=AF.Exp, scale=acol))
                    for sem, v in zb_free[bpar]:
                        p.wait("vector", sem, v)
                    zb_free[bpar] = []
                    v_op(lambda e, j=j, Z=zb[bpar], W=wt_c: e.tensor_tensor(
                        out=Z, in0=W,
                        in1=reps[:, (2 * j) * 2048:(2 * j + 1) * 2048], op=OP.mult))
                    p.wait("vector", SEM_ACT, av)
                    v_op(lambda e, A=ab[bpar], Z=zb[bpar], H=hb[bpar]:
                         e.tensor_tensor_scan(out=H, data0=A, data1=Z, initial=0.0,
                                              op0=OP.mult, op1=OP.add))
                    ab_free[bpar].append((SEM_DVE, p.val(SEM_DVE)))
                    zb_free[bpar].append((SEM_DVE, p.val(SEM_DVE)))
                    p.wait("gpsimd", SEM_DVE, p.val(SEM_DVE))
                    gc = g_op(lambda e, j=j, H=hb[bpar], T=tb_[bpar]: e.tensor_tensor(
                        out=T, in0=H,
                        in1=reps[:, (2 * j + 1) * 2048:(2 * j + 2) * 2048],
                        op=OP.mult))
                    p.wait("vector", SEM_GP, gc)
                    v_op(lambda e, T=tb_[bpar], Y=yt_c: e.tensor_tensor(
                        out=Y, in0=Y, in1=T, op=OP.add))
                    v_op(lambda e, cdt=cdt, n=n, H=hb[bpar]: e.tensor_copy(
                        out=hend[:, cdt * 16 + n:cdt * 16 + n + 1],
                        in_=H[:, 2047:2048]))
                    ab_free[bpar].append((SEM_GP, p.val(SEM_GP)))
                dv = p.val(SEM_DVE)
                p.wait("sync", SEM_DVE, dv)
                ow = dma_out(y_sp[cdt], yt_c)
                ld_free[par] = [(SEM_ACT, p.val(SEM_ACT)),
                                (SEM_GP, p.val(SEM_GP)),
                                (SEM_OUT, ow)]

        p.wait("scalar", SEM_DVE, p.val(SEM_DVE))
        cdT = scol[:, 768:784]
        for dt in range(NDT):
            s_op(lambda e, dt=dt: e.activation(
                out=Mv[:, dt * 16:(dt + 1) * 16],
                in_=a_sb[:, (li * NDT + dt) * NST:(li * NDT + dt + 1) * NST],
                func=AF.Exp, scale=cdT[:, dt:dt + 1]))
        sv = p.val(SEM_ACT)
        p.wait("sync", SEM_ACT, sv)
        p.wait("sync", SEM_DVE, p.val(SEM_DVE))
        dma_out(gin_s[:, 0:256], Mv)
        g2 = dma_out(gin_s[:, 256:512], hend)
        p.wait("gpsimd", SEM_OUT, g2)
        p.op("gpsimd", lambda e: e.collective_compute(
            "AllGather", OP.bypass, replica_groups=[list(range(NCORES))],
            ins=[gin_s], outs=[gout_s]), SEM_CC)
        p.wait("sync", SEM_CC, p.val(SEM_CC))
        p.wait("sync", SEM_GP, p.val(SEM_GP))
        gat = bigBf[:, 14336 - 4096 + 4096:16384]
        gat = bigA[:, 0:4096].rearrange("p (a b) -> p a b", b=512)
        lg = dma_in(gat, gout_s.rearrange("c p f -> p c f"))
        p.wait("vector", SEM_IN, lg)
        v_op(lambda e: e.memset(hin, 0.0))
        v_op(lambda e: e.memset(tc, 0.0))
        vpad()
        vpad()
        for sc_ in range(NCORES):
            if sc_ % 4 == 0:
                v_op(lambda e: e.memset(tc, 0.0))
                vpad()
                vpad()
            else:
                v_op(lambda e, sc_=sc_: e.tensor_tensor(
                    out=tc, in0=tc, in1=gat[:, sc_ - 1, 0:256], op=OP.mult))
                v_op(lambda e, sc_=sc_: e.tensor_tensor(
                    out=tc, in0=tc, in1=gat[:, sc_ - 1, 256:512], op=OP.add))
                v_op(lambda e, sc_=sc_: e.scalar_tensor_tensor(
                    out=hin, in0=tc, scalar=oh_sb[:, sc_:sc_ + 1], in1=hin,
                    op0=OP.mult, op1=OP.add))

        zeros_t = wtb[0]
        p.wait("vector", SEM_IN, p.val(SEM_IN))
        p.wait("vector", SEM_GP, p.val(SEM_GP))
        v_op(lambda e: e.memset(zeros_t, 0.0))
        ld_free = {0: [], 1: []}
        gb_free = {0: [], 1: []}
        for ng in range(4):
            p.wait("sync", SEM_DVE, p.val(SEM_DVE))
            p.wait("sync", SEM_OUT, p.val(SEM_OUT))
            rl = None
            for j in range(4):
                n = ng * 4 + j
                rl = dma_in(reps[:, (2 * j + 1) * 2048:(2 * j + 2) * 2048],
                            bc_sp[16 + n:17 + n, :].partition_broadcast(128))
            p.wait("vector", SEM_IN, rl)
            ld_done = {}
            for dt in range(NDT + 1):
                if dt < NDT:
                    par = dt % 2
                    for sem, v in ld_free[par]:
                        p.wait("sync", sem, v)
                    ld_free[par] = []
                    dma_in(dltb[par], del_sp[dt])
                    ld_done[dt] = dma_in(ytb[par], y_sp[dt])
                if dt == 0:
                    continue
                cdt = dt - 1
                par = cdt % 2
                dlt_c, yt_c = dltb[par], ytb[par]
                cd_c = hb[par]
                p.wait("vector", SEM_IN, ld_done[cdt])
                p.wait("vector", SEM_ACT, p.val(SEM_ACT))
                v_op(lambda e, D=dlt_c, CD=cd_c: e.tensor_tensor_scan(
                    out=CD, data0=D, data1=zeros_t, initial=0.0,
                    op0=OP.add, op1=OP.add))
                dvc = p.val(SEM_DVE)
                p.wait("scalar", SEM_DVE, dvc)
                for j in range(4):
                    n = ng * 4 + j
                    bpar = j % 2
                    acol = a_sb[:, (li * NDT + cdt) * NST + n:
                                (li * NDT + cdt) * NST + n + 1]
                    for sem, v in gb_free[bpar]:
                        p.wait("scalar", sem, v)
                    gb_free[bpar] = []
                    av = s_op(lambda e, acol=acol, G=ab[bpar], CD=cd_c: e.activation(
                        out=G, in_=CD, func=AF.Exp, scale=acol))
                    p.wait("vector", SEM_ACT, av)
                    hcol = hin[:, cdt * 16 + n:cdt * 16 + n + 1]
                    v_op(lambda e, j=j, hcol=hcol, G=ab[bpar], T=tb_[bpar]:
                         e.scalar_tensor_tensor(
                        out=T, in0=G, scalar=hcol,
                        in1=reps[:, (2 * j + 1) * 2048:(2 * j + 2) * 2048],
                        op0=OP.mult, op1=OP.mult))
                    gb_free[bpar].append((SEM_DVE, p.val(SEM_DVE)))
                    v_op(lambda e, T=tb_[bpar], Y=yt_c: e.tensor_tensor(
                        out=Y, in0=Y, in1=T, op=OP.add))
                dv = p.val(SEM_DVE)
                p.wait("sync", SEM_DVE, dv)
                ow = dma_out(y_sp[cdt], yt_c)
                ld_free[par] = [(SEM_DVE, dv), (SEM_OUT, ow)]

        p.wait("sync", SEM_OUT, p.val(SEM_OUT))
        p.wait("sync", SEM_ACT, p.val(SEM_ACT))
        p.wait("sync", SEM_DVE, p.val(SEM_DVE))
        p.wait("scalar", SEM_DVE, p.val(SEM_DVE))
        gld = {}
        gfree = {0: [], 1: []}
        for dt in range(NDT + 1):
            if dt < NDT:
                par = dt % 2
                for sem, v in gfree[par]:
                    p.wait("sync", sem, v)
                gfree[par] = []
                dma_in(ytb[par], y_sp[dt])
                gld[dt] = dma_in(dltb[par], res_sp[dt])
            if dt == 0:
                continue
            cdt = dt - 1
            par = cdt % 2
            p.wait("scalar", SEM_IN, gld[cdt])
            av = s_op(lambda e, R=dltb[par], A=wtb[par]: e.activation(
                out=A, in_=R, func=AF.Silu))
            p.wait("vector", SEM_ACT, av)
            p.wait("vector", SEM_IN, gld[cdt])
            v_op(lambda e, cdt=cdt, Y=ytb[par], A=wtb[par]: e.tensor_tensor(
                out=bigB[:, cdt * 2048:(cdt + 1) * 2048], in0=Y, in1=A,
                op=OP.mult))
            gfree[par] = [(SEM_DVE, p.val(SEM_DVE)), (SEM_ACT, p.val(SEM_ACT))]
        y_g = bigB[:, 0:16 * 2048].rearrange("p (a b) -> p a b", b=2048)
        ygv = p.val(SEM_DVE)
        p.wait("tensor", SEM_DVE, ygv)

        woc = bigA[:, 0:8192].bitcast(BF16).rearrange("p (a b) -> p a b", b=1024)
        wof = bigA[:, 8192:10240]
        for dt2 in range(8):
            p.wait("sync", SEM_DVE, p.val(SEM_DVE))
            ld = dma_in(wof.rearrange("p (a b) -> p a b", b=DM),
                        woT_h[li][dt2 * 256:(dt2 + 1) * 256, :]
                        .rearrange("(a p) b -> p a b", p=128))
            p.wait("vector", SEM_IN, ld)
            v_op(lambda e, dt2=dt2: e.tensor_copy(
                out=woc[:, 2 * dt2:2 * dt2 + 2, :].rearrange("p a b -> p (a b)"),
                in_=wof))
        wov = p.val(SEM_DVE)
        p.wait("tensor", SEM_DVE, wov)
        p.wait("sync", SEM_OUT, p.val(SEM_OUT))
        for i in range(NT):
            p.wait("sync", SEM_DVE, p.val(SEM_DVE))
            hld = dma_in(wof[:, 0:1024], h_sp[i])
            for c in range(2):
                pb = ps[c]
                bank_wait("tensor", pb)
                for k in range(NDT):
                    t_op(lambda e, k=k, i=i, c=c, pb=pb: e.matmul(
                        pb[:], y_g[:, k, i * 128:(i + 1) * 128],
                        woc[:, k, c * 512:(c + 1) * 512],
                        start=(k == 0), stop=False))
                t_op(lambda e, c=c, pb=pb: e.matmul(
                    pb[:], onesr[0:1, 0:128],
                    bo_sb[0:1, li * DM + c * 512:li * DM + (c + 1) * 512],
                    start=False, stop=True))
            mv = p.val(SEM_MM)
            p.wait("vector", SEM_MM, mv)
            p.wait("vector", SEM_IN, hld)
            for c in range(2):
                pb = ps[c]
                v_op(lambda e, i=i, c=c, pb=pb: e.tensor_tensor(
                    out=stream[:, i * DM + c * 512:i * DM + (c + 1) * 512],
                    in0=wof[:, c * 512:(c + 1) * 512], in1=pb[:], op=OP.add))
                bank_mark(pb, SEM_DVE)
        if dbg:
            p.wait("sync", SEM_DVE, p.val(SEM_DVE))
            for i2 in range(NT):
                dma_out(xln_dbg[i2], stream[:, i2 * DM:(i2 + 1) * DM])
            p.wait("gpsimd", SEM_OUT, p.val(SEM_OUT))
            p.wait("vector", SEM_OUT, p.val(SEM_OUT))
            p.wait("scalar", SEM_OUT, p.val(SEM_OUT))

    ffn_call(dump=dbg)
    p.wait("sync", SEM_DVE, p.val(SEM_DVE))
    for i in range(NT):
        src_t = stream[:, i * DM:(i + 1) * DM]
        xv = dma_out(x_sp[i], src_t)
        xln = bigA[:, 5120:6144]
        ln_tile(src_t, xln)
        p.wait("vector", SEM_OUT, xv)
        v_op(lambda e, s=src_t, xln=xln: e.tensor_copy(out=s, in_=xln))
    for li in range(nlay):
        mamba_layer(li)
    rstd2 = stat_sb[:, 7:8]
    xr_t = bigA[:, 6144:7168]
    nfw_sb = bigA[:, 7168:8192]
    p.wait("sync", SEM_DVE, p.val(SEM_DVE))
    p.wait("sync", SEM_ACT, p.val(SEM_ACT))
    p.wait("sync", SEM_OUT, p.val(SEM_OUT))
    nfl = dma_in(nfw_sb, nfw_h.partition_broadcast(128))
    p.wait("vector", SEM_IN, nfl)
    p.wait("scalar", SEM_DVE, p.val(SEM_DVE))
    for i in range(NT):
        src_t = stream[:, i * DM:(i + 1) * DM]
        rms_rstd(src_t, rstd2)
        t1 = bigA[:, 5120:6144]
        v_op(lambda e, s=src_t, t1=t1: e.scalar_tensor_tensor(
            out=t1, in0=s, scalar=rstd2, in1=nfw_sb, op0=OP.mult, op1=OP.mult))
        p.wait("sync", SEM_DVE, p.val(SEM_DVE))
        xld = dma_in(xr_t, x_sp[i])
        p.wait("vector", SEM_IN, xld)
        vpad()
        v_op(lambda e, s=src_t, t1=t1: e.tensor_tensor(
            out=s, in0=t1, in1=xr_t, op=OP.add))
    ffn_call()
    fv = p.val(SEM_DVE)
    p.wait("sync", SEM_DVE, fv)
    ov = dma_out(out_h.rearrange("(a p) b -> p a b", p=128), stream.rearrange("p (a b) -> p a b", b=DM))
    p.wait("sync", SEM_OUT, ov)

    with nc.Block() as block:
        p.replay(block)
    return nc


_NC_CACHE = [None]


def _stage_inputs(inputs):
    x = np.asarray(inputs["L0_attn_x"], np.float32)
    ipw = np.asarray(inputs["L0_attn_in_proj_w"], np.float32)
    ipb = np.asarray(inputs["L0_attn_in_proj_b"], np.float32)
    cw = np.asarray(inputs["L0_attn_conv_w"], np.float32)
    cb = np.asarray(inputs["L0_attn_conv_b"], np.float32)
    xpw = np.asarray(inputs["L0_attn_x_proj_w"], np.float32)
    dtw = np.asarray(inputs["L0_attn_dt_proj_w"], np.float32)
    dtb = np.asarray(inputs["L0_attn_dt_proj_b"], np.float32)
    alog = np.asarray(inputs["L0_attn_A_log"], np.float32)
    dd = np.asarray(inputs["L0_attn_D"], np.float32)
    opw = np.asarray(inputs["L0_attn_out_proj_w"], np.float32)
    opb = np.asarray(inputs["L0_attn_out_proj_b"], np.float32)
    nw = np.asarray(inputs["L0_attn_norm_w"], np.float32)
    nfw = np.asarray(inputs["L0_attn_norm_f_w"], np.float32)
    wg = np.asarray(inputs["L0_ffn_w_gate"], np.float32)
    wu = np.asarray(inputs["L0_ffn_w_up"], np.float32)
    wd = np.asarray(inputs["L0_ffn_w_down"], np.float32)

    def colmap(a):
        L = a.shape[0]
        G = a.shape[1] // 128
        return np.ascontiguousarray(a.reshape(L, G, 128).transpose(2, 0, 1)
                                    .reshape(128, L * G))

    shared = {
        "wiT": np.ascontiguousarray(ipw.transpose(0, 2, 1)),
        "wxT": np.ascontiguousarray(xpw.transpose(0, 2, 1)),
        "wdtT": np.ascontiguousarray(dtw.transpose(0, 2, 1)),
        "woT": np.ascontiguousarray(opw.transpose(0, 2, 1)),
        "bi_c": colmap(ipb),
        "cw_c": np.ascontiguousarray(cw.reshape(NLAYER, NDT, 128, 5)
                                     .transpose(2, 0, 1, 3).reshape(128, -1)),
        "cb_c": colmap(cb), "dtb_c": colmap(dtb),
        "alog_c": np.ascontiguousarray(alog.reshape(NLAYER, NDT, 128, NST)
                                       .transpose(2, 0, 1, 3).reshape(128, -1)),
        "d_c": colmap(dd), "nw_c": colmap(nw),
        "nfw_r": np.ascontiguousarray(nfw.reshape(1, DM)),
        "bo_r": np.ascontiguousarray(opb.reshape(1, NLAYER * DM)),
        "wg": wg, "wu": wu, "wd": wd,
        "idm": np.eye(128, dtype=np.float32),
        "idmb": np.eye(128, dtype=np.float32).astype(ml_dtypes.bfloat16),
    }
    in_maps = []
    for c in range(NCORES):
        b, q = c // 4, c % 4
        oh = np.zeros((1, 24), np.float32)
        oh[0, c] = 1.0
        if q > 0:
            oh[0, 8 + (c - 1)] = 1.0
        if q < 3:
            oh[0, 16 + (c + 1)] = 1.0
        m = dict(shared)
        m["x"] = np.ascontiguousarray(x[b, q * TAU:(q + 1) * TAU, :])
        m["oh"] = oh
        in_maps.append(m)
    return in_maps


def kernel(**inputs):
    if _NC_CACHE[0] is None:
        _NC_CACHE[0] = build_program()
    nc = _NC_CACHE[0]
    in_maps = _stage_inputs(inputs)
    res = run_bass_kernel_spmd(nc, in_maps, core_ids=list(range(NCORES)))
    out = np.empty((2, 8192, DM), np.float32)
    for c in range(NCORES):
        b, q = c // 4, c % 4
        out[b, q * TAU:(q + 1) * TAU, :] = res.results[c]["out"]
    return out


# version 10
# speedup vs baseline: 1.2106x; 1.0147x over previous
import numpy as np
import ml_dtypes
import concourse.bass as bass
import concourse.mybir as mybir
from concourse.bass_utils import run_bass_kernel_spmd

F32 = mybir.dt.float32
BF16 = mybir.dt.bfloat16
AF = mybir.ActivationFunctionType
OP = mybir.AluOpType
AX = mybir.AxisListType

NCORES = 8
TAU = 2048
NT = 16
DM = 1024
DI = 2048
NDT = 16
NST = 16
DTR = 64
II = 2816
NIT = 22
NLAYER = 2
NTB = 4
MIDS = [0.25, 0.75, 1.25, 1.75, 2.5, 3.5, 5.0]
EPS = 1e-5


class Prog:
    ENGINES = ("sync", "vector", "scalar", "tensor", "gpsimd")

    def __init__(self, nc):
        self.nc = nc
        self.q = {e: [] for e in self.ENGINES}
        self.sems = {}
        self.cnt = {}

    def sem(self, name):
        if name not in self.sems:
            self.sems[name] = self.nc.alloc_semaphore(name=name)
            self.cnt[name] = 0
        return self.sems[name]

    def op(self, eng, fn, sem=None, amt=1):
        s = self.sem(sem) if sem else None
        if sem:
            self.cnt[sem] += amt

        def run(e, fn=fn, s=s, amt=amt):
            ins = fn(e)
            if s is not None:
                ins.then_inc(s, amt)
        self.q[eng].append(run)
        return self.cnt[sem] if sem else None

    def wait(self, eng, sem, val=None):
        s = self.sem(sem)
        v = self.cnt[sem] if val is None else val

        def run(e, s=s, v=v):
            e.wait_ge(s, v)
        self.q[eng].append(run)

    def val(self, sem):
        return self.cnt.get(sem, 0)

    def replay(self, block):
        for name in self.ENGINES:
            dec = getattr(block, name)
            ops = self.q[name]

            def body(e, ops=ops):
                for f in ops:
                    f(e)
            dec(body)


SEM_IN, SEM_MM, SEM_ACT, SEM_DVE, SEM_GP, SEM_OUT, SEM_CC = (
    "s_in", "s_mm", "s_act", "s_dve", "s_gp", "s_out", "s_cc")


def build_program(dbg=False, nlay=NLAYER):
    nc = bass.Bass("TRN2", detect_race_conditions=False, target_bir_lowering=False, debug=False)
    nc.num_devices = NCORES
    p = Prog(nc)

    def din(name, shape, dtype=F32):
        return nc.dram_tensor(name, shape, dtype, kind="ExternalInput").ap()

    x_in = din("x", [TAU, DM])
    wiT_h = din("wiT", [NLAYER, DM, 2 * DI])
    wxT_h = din("wxT", [NLAYER, DI, 96])
    wdtT_h = din("wdtT", [NLAYER, DTR, DI])
    woT_h = din("woT", [NLAYER, DI, DM])
    bi_h = din("bi_c", [128, NLAYER * 32])
    cw_h = din("cw_c", [128, NLAYER * NDT * 5])
    cb_h = din("cb_c", [128, NLAYER * NDT])
    dtb_h = din("dtb_c", [128, NLAYER * NDT])
    alog_h = din("alog_c", [128, NLAYER * NDT * NST])
    dd_h = din("d_c", [128, NLAYER * NDT])
    nw_h = din("nw_c", [128, NLAYER * 8])
    nfw_h = din("nfw_r", [1, DM])
    bo_h = din("bo_r", [1, NLAYER * DM])
    wg_h = din("wg", [II, DM])
    wu_h = din("wu", [II, DM])
    wd_h = din("wd", [DM, II])
    oh_h = din("oh", [1, 24])
    idm_h = din("idm", [128, 128])
    idmb_h = din("idmb", [128, 128], BF16)

    out_h = nc.dram_tensor("out", [TAU, DM], F32, kind="ExternalOutput").ap()

    def scr(name, shape, dtype=F32, force_internal=False):
        kind = "ExternalOutput" if (dbg and not force_internal) else "Internal"
        return nc.dram_tensor(name, shape, dtype, kind=kind).ap()

    wgT_q = scr("wgT_q", [DM, II], BF16)
    wuT_q = scr("wuT_q", [DM, II], BF16)
    wdT_q = scr("wdT_q", [II, DM], BF16)
    x_sp = scr("x_sp", [NT, 128, DM])
    h_sp = scr("h_sp", [NT, 128, DM])
    res_sp = scr("res_sp", [NDT, 128, TAU])
    del_sp = scr("del_sp", [NDT, 128, TAU])
    w_sp = scr("w_sp", [NDT, 128, TAU])
    yd_sp = scr("yd_sp", [NDT, 128, TAU])
    y_sp = scr("y_sp", [NDT, 128, TAU])
    u_sp = scr("u_sp", [NDT, 128, TAU])
    bc_sp = scr("bc_sp", [32, TAU])
    scan_dbg = scr("scan_dbg", [4, 128, TAU])
    xln_dbg = scr("xln_dbg", [NT, 128, DM])
    xq_dbg = scr("xq_dbg", [NT, 128, DM], BF16)
    h_dbg = scr("h_dbg", [NT, 128, II])
    gin_h = scr("gin_h", [128, NDT * 4], force_internal=True)
    gout_h = scr("gout_h", [NCORES, 128, NDT * 4], force_internal=True)
    gin_s = scr("gin_s", [128, 512], force_internal=True)
    gout_s = scr("gout_s", [NCORES, 128, 512], force_internal=True)

    def sball(name, shape, dtype=F32):
        return nc.alloc_sbuf_tensor(name, shape, dtype).ap()

    identf = sball("identf", [128, 128], F32)
    identb = sball("identb", [128, 128], BF16)
    onesr = sball("onesr", [1, 128], F32)
    bi_sb = sball("bi_sb", [128, NLAYER * 32], F32)
    cw_sb = sball("cw_sb", [128, NLAYER * NDT * 5], F32)
    cb_sb = sball("cb_sb", [128, NLAYER * NDT], F32)
    dtb_sb = sball("dtb_sb", [128, NLAYER * NDT], F32)
    a_sb = sball("a_sb", [128, NLAYER * NDT * NST], F32)
    d_sb = sball("d_sb", [128, NLAYER * NDT], F32)
    nw_sb = sball("nw_sb", [128, NLAYER * 8], F32)
    bo_sb = sball("bo_sb", [1, NLAYER * DM], F32)
    oh_sb = sball("oh_sb", [128, 24], F32)
    stat_sb = sball("stat_sb", [128, 16], F32)
    scol = sball("scol", [128, 1088], F32)
    halo_sb = sball("halo_sb", [128, NDT * 4], F32)

    stream = sball("stream", [128, NT * DM], F32)
    bigA = sball("bigA", [128, 12 * 1024], F32)
    bigB = sball("bigB", [128, 32 * 1024], BF16)
    wbuf = sball("wbuf", [128, 3 * 1024], F32)

    ps = [nc.alloc_psum_tensor(f"ps{i}", [128, 512], F32).ap() for i in range(4)]
    pst = [nc.alloc_psum_tensor(f"pst{i}", [128, 1024], BF16).ap()[:, 0:128]
           for i in range(4)]

    def wb_bf16(off_f32, n):
        return wbuf[:, off_f32:off_f32 + (n + 1) // 2].bitcast(BF16)[:, 0:n]

    stream_b = stream.bitcast(BF16)

    bank_free = {}

    def bank_wait(eng, bank):
        if id(bank) in bank_free:
            sname, v = bank_free[id(bank)]
            p.wait(eng, sname, v)

    def bank_mark(bank, sname):
        bank_free[id(bank)] = (sname, p.val(sname))

    def gp_slack(k=2):
        for _ in range(k):
            p.op("gpsimd", lambda e: e.tensor_copy(out=stat_sb[:, 13:14],
                                                   in_=identf[:, 0:1]), SEM_GP)

    def dma_slack(k=2):
        for _ in range(k):
            dma_in(identf[:], idm_h)

    def dma_in(dst, src):
        return p.op("sync", lambda e, d=dst, s=src: e.dma_start(out=d, in_=s), SEM_IN, 16)

    def dma_out(dst, src):
        return p.op("sync", lambda e, d=dst, s=src: e.dma_start(out=d, in_=s), SEM_OUT, 16)

    def v_op(fn):
        return p.op("vector", fn, SEM_DVE)

    def g_op(fn):
        return p.op("gpsimd", fn, SEM_GP)

    def s_op(fn):
        return p.op("scalar", fn, SEM_ACT)

    def t_op(fn):
        return p.op("tensor", fn, SEM_MM)

    def vpad():
        v_op(lambda e: e.tensor_copy(out=stat_sb[:, 12:13], in_=identf[:, 0:1]))

    dma_in(stream.rearrange("p (a b) -> p a b", b=DM), x_in.rearrange("(a p) b -> p a b", p=128))
    dma_in(bi_sb, bi_h)
    dma_in(cw_sb, cw_h)
    dma_in(cb_sb, cb_h)
    dma_in(dtb_sb, dtb_h)
    dma_in(a_sb, alog_h)
    dma_in(d_sb, dd_h)
    dma_in(nw_sb, nw_h)
    dma_in(bo_sb, bo_h)
    dma_in(oh_sb, oh_h.partition_broadcast(128))
    dma_in(identf[:], idm_h)
    dma_in(identb[:], idmb_h)
    setup_in = p.val(SEM_IN)

    p.wait("gpsimd", SEM_IN, setup_in)
    p.wait("vector", SEM_IN, setup_in)
    v_op(lambda e: e.memset(onesr[:], 1.0))
    p.wait("scalar", SEM_IN, setup_in)
    s_op(lambda e: e.activation(out=bigA[:, 0:NLAYER * NDT * NST], in_=a_sb[:],
                                func=AF.Exp))
    p.wait("vector", SEM_ACT, p.val(SEM_ACT))
    v_op(lambda e: e.tensor_single_scalar(out=a_sb[:],
                                          in_=bigA[:, 0:NLAYER * NDT * NST],
                                          scalar=-1.0, op=OP.mult))
    p.wait("tensor", SEM_IN, setup_in)

    def quant(dst, src, F, aoff, boff):
        nb = F // 16
        xn = bigA[:, aoff:aoff + F]
        qa = bigA[:, aoff + F:aoff + 2 * F]
        xsq = bigA[:, aoff + 2 * F:aoff + 3 * F]
        bm = bigA[:, aoff + 3 * F:aoff + 3 * F + nb]
        scl = bigA[:, aoff + 3 * F + nb:aoff + 3 * F + 2 * nb]
        rcp = bigA[:, aoff + 3 * F + 2 * nb:aoff + 3 * F + 3 * nb]
        m0 = bigB[:, boff:boff + F]
        m1 = bigB[:, boff + F:boff + 2 * F]
        m2 = bigB[:, boff + 2 * F:boff + 3 * F]
        src3 = src.rearrange("p (b k) -> p b k", k=16)
        v_op(lambda e: e.tensor_reduce(out=bm, in_=src3, axis=AX.X, op=OP.max,
                                       apply_absolute_value=True))
        vpad()
        v_op(lambda e: e.tensor_scalar(out=scl, in0=bm, scalar1=1.0 / 6.0,
                                       scalar2=1e-12, op0=OP.mult, op1=OP.max))
        vpad()
        v_op(lambda e: e.reciprocal(out=rcp, in_=scl))
        vpad()
        v_op(lambda e: e.tensor_tensor(
            out=xn.rearrange("p (b k) -> p b k", k=16), in0=src3,
            in1=rcp.rearrange("p (b o) -> p b o", o=1).broadcast_to([128, nb, 16]),
            op=OP.mult))
        dv = p.val(SEM_DVE)
        p.wait("scalar", SEM_DVE, dv)
        avq = s_op(lambda e: e.activation(out=xsq, in_=xn, func=AF.Square))

        def mk(dst_t, th):
            return lambda e: e.tensor_single_scalar(out=dst_t, in_=xsq,
                                                    scalar=th * th, op=OP.is_ge)
        p.wait("vector", SEM_ACT, avq)
        v_op(mk(m0, MIDS[0]))
        v_op(mk(m1, MIDS[1]))
        v_op(lambda e: e.tensor_tensor(out=m0, in0=m0, in1=m1, op=OP.add))
        v_op(mk(m1, MIDS[2]))
        v_op(lambda e: e.tensor_tensor(out=m0, in0=m0, in1=m1, op=OP.add))
        v_op(mk(m1, MIDS[3]))
        v_op(lambda e: e.tensor_tensor(out=m0, in0=m0, in1=m1, op=OP.add))
        v_op(mk(m2, MIDS[4]))
        v_op(mk(m1, MIDS[5]))
        v_op(lambda e: e.tensor_tensor(out=m2, in0=m2, in1=m1, op=OP.add))
        v_op(mk(m1, MIDS[6]))
        v_op(lambda e: e.scalar_tensor_tensor(out=qa, in0=m0, scalar=0.5, in1=m2,
                                              op0=OP.mult, op1=OP.add))
        v_op(lambda e: e.scalar_tensor_tensor(out=qa, in0=m1, scalar=2.0, in1=qa,
                                              op0=OP.mult, op1=OP.add))
        v_op(lambda e: e.tensor_scalar(out=m1, in0=xn, scalar1=0.0, scalar2=2.0,
                                       op0=OP.is_ge, op1=OP.mult))
        v_op(lambda e: e.tensor_tensor(out=xn, in0=qa, in1=m1, op=OP.mult))
        v_op(lambda e: e.tensor_tensor(out=qa, in0=xn, in1=qa, op=OP.subtract))
        v_op(lambda e: e.tensor_tensor(
            out=dst.rearrange("p (b k) -> p b k", k=16),
            in0=qa.rearrange("p (b k) -> p b k", k=16),
            in1=scl.rearrange("p (b o) -> p b o", o=1).broadcast_to([128, nb, 16]),
            op=OP.mult))

    def transpose_tiles(src, ntile, dst_ap_of_t):
        for t in range(ntile):
            pb = pst[t % 4]
            bank_wait("tensor", pb)
            t_op(lambda e, pb=pb, t=t: e.transpose(
                out=pb[:], in_=src[:, t * 128:(t + 1) * 128], identity=identb[:]))
            p.wait("scalar", SEM_MM, p.val(SEM_MM))
            s_op(lambda e, pb=pb, d=dst_ap_of_t(t): e.copy(out=d, in_=pb[:]))
            bank_mark(pb, SEM_ACT)

    def quant_weights():
        bigBf0 = bigB.bitcast(F32)
        wtiles = [bigA[:, 9000:9000 + 2816], bigBf0[:, 8192:8192 + 2816]]
        qtiles = [bigB[:, 0:2816], bigB[:, 2816:5632]]
        jobs = []
        for whbm, dst, rows, cols in ((wg_h, wgT_q, NIT, DM), (wu_h, wuT_q, NIT, DM),
                                      (wd_h, wdT_q, 8, II)):
            for r in range(rows):
                jobs.append((whbm, dst, r, cols))
        lds = {}
        wfree = {0: [], 1: []}
        qfree = {0: [], 1: []}
        for i in range(len(jobs) + 1):
            if i < len(jobs):
                whbm, dst, r, cols = jobs[i]
                par = i % 2
                for sem, v in wfree[par]:
                    p.wait("sync", sem, v)
                wfree[par] = []
                lds[i] = dma_in(wtiles[par][:, 0:cols], whbm[r * 128:(r + 1) * 128, :])
            if i == 0:
                continue
            whbm, dst, r, cols = jobs[i - 1]
            par = (i - 1) % 2
            wt_, qt_ = wtiles[par], qtiles[par]
            p.wait("vector", SEM_IN, lds[i - 1])
            for sem, v in qfree[par]:
                p.wait("vector", sem, v)
            qfree[par] = []
            quant(qt_[:, 0:cols], wt_[:, 0:cols], cols, 0, 22016)
            wfree[par] = [(SEM_DVE, p.val(SEM_DVE))]
            dv = p.val(SEM_DVE)
            p.wait("tensor", SEM_DVE, dv)
            for t in range(cols // 128):
                pb = pst[t % 4]
                ev = wb_bf16(2048 + (t % 4) * 64, 128)
                bank_wait("tensor", pb)
                t_op(lambda e, pb=pb, t=t, qt_=qt_: e.transpose(
                    out=pb[:], in_=qt_[:, t * 128:(t + 1) * 128],
                    identity=identb[:]))
                p.wait("scalar", SEM_MM, p.val(SEM_MM))
                p.wait("scalar", SEM_OUT, p.val(SEM_OUT))
                av = s_op(lambda e, ev=ev, pb=pb: e.copy(out=ev, in_=pb[:]))
                bank_mark(pb, SEM_ACT)
                p.wait("sync", SEM_ACT, av)
                dma_out(dst[t * 128:(t + 1) * 128, r * 128:(r + 1) * 128], ev)
            qfree[par] = [(SEM_MM, p.val(SEM_MM))]

    quant_weights()

    def ln_tile(src_t, dst):
        s1, s2, s3 = stat_sb[:, 0:1], stat_sb[:, 1:2], stat_sb[:, 2:3]
        t2 = bigA[:, 4096:5120]
        v_op(lambda e: e.tensor_reduce(out=s1, in_=src_t, axis=AX.X, op=OP.add))
        dv = p.val(SEM_DVE)
        p.wait("scalar", SEM_DVE, dv)
        av = s_op(lambda e: e.activation(out=t2, in_=src_t, func=AF.Square))
        p.wait("vector", SEM_ACT, av)
        v_op(lambda e: e.tensor_reduce(out=s2, in_=t2, axis=AX.X, op=OP.add))
        v_op(lambda e: e.tensor_single_scalar(out=s1, in_=s1, scalar=1.0 / DM, op=OP.mult))
        v_op(lambda e: e.tensor_single_scalar(out=s2, in_=s2, scalar=1.0 / DM, op=OP.mult))
        v_op(lambda e: e.tensor_tensor(out=s3, in0=s1, in1=s1, op=OP.mult))
        vpad()
        v_op(lambda e: e.tensor_tensor(out=s2, in0=s2, in1=s3, op=OP.subtract))
        vpad()
        v_op(lambda e: e.tensor_single_scalar(out=s2, in_=s2, scalar=EPS, op=OP.add))
        dv = p.val(SEM_DVE)
        p.wait("scalar", SEM_DVE, dv)
        av = s_op(lambda e: e.activation(out=s3, in_=s2, func=AF.Sqrt))
        p.wait("vector", SEM_ACT, av)
        v_op(lambda e: e.reciprocal(out=s2, in_=s3))
        vpad()
        v_op(lambda e: e.tensor_scalar(out=dst, in0=src_t, scalar1=s1, scalar2=s2,
                                       op0=OP.subtract, op1=OP.mult))

    def rms_rstd(src_t, rstd_col):
        s2, s3 = stat_sb[:, 4:5], stat_sb[:, 5:6]
        t2 = bigA[:, 4096:5120]
        av = s_op(lambda e: e.activation(out=t2, in_=src_t, func=AF.Square))
        p.wait("vector", SEM_ACT, av)
        v_op(lambda e: e.tensor_reduce(out=s2, in_=t2, axis=AX.X, op=OP.add))
        vpad()
        v_op(lambda e: e.tensor_scalar(out=s2, in0=s2, scalar1=1.0 / DM, scalar2=EPS,
                                       op0=OP.mult, op1=OP.add))
        dv = p.val(SEM_DVE)
        p.wait("scalar", SEM_DVE, dv)
        av = s_op(lambda e: e.activation(out=s3, in_=s2, func=AF.Sqrt))
        p.wait("vector", SEM_ACT, av)
        v_op(lambda e, r=rstd_col: e.reciprocal(out=r, in_=s3))
        vpad()

    NCH = [512, 512, 512, 512, 512, 256]

    def ffn_call(dump=False):
        p.wait("sync", SEM_OUT, p.val(SEM_OUT))
        for th in range(2):
            for tt in range(8):
                ti = th * 8 + tt
                src_t = stream[:, ti * DM:(ti + 1) * DM]
                xln = bigA[:, 5120:6144]
                ln_tile(src_t, xln)
                xq = bigB[:, 8192:9216]
                p.wait("vector", SEM_MM, p.val(SEM_MM))
                if dump:
                    dvx = p.val(SEM_DVE)
                    p.wait("sync", SEM_DVE, dvx)
                    dma_out(xln_dbg[ti], xln)
                quant(xq, xln, DM, 0, 9216)
                dv = p.val(SEM_DVE)
                p.wait("tensor", SEM_DVE, dv)
                if dump:
                    p.wait("sync", SEM_DVE, dv)
                    dma_out(xq_dbg[ti], xq)
                transpose_tiles(xq, 8,
                                lambda t, tt=tt: bigB[:, t * 1024 + tt * 128:
                                                      t * 1024 + tt * 128 + 128])
            xqT = bigB[:, 0:8192].rearrange("p (a b) -> p a b", b=1024)
            avq = p.val(SEM_ACT)

            hq_base = 8192
            for ci, ch in enumerate(NCH):
                c0 = sum(NCH[:ci])
                wgc = bigA[:, 7168:7168 + 2048].bitcast(BF16)[:, 0:8 * ch] \
                    .rearrange("p (a b) -> p a b", b=ch)
                wuc = bigA[:, 9216:9216 + 2048].bitcast(BF16)[:, 0:8 * ch] \
                    .rearrange("p (a b) -> p a b", b=ch)
                p.wait("sync", SEM_MM, p.val(SEM_MM))
                dma_in(wgc, wgT_q[:, c0:c0 + ch].rearrange("(a p) b -> p a b", p=128))
                l2 = dma_in(wuc, wuT_q[:, c0:c0 + ch].rearrange("(a p) b -> p a b", p=128))
                p.wait("tensor", SEM_IN, l2)
                p.wait("tensor", SEM_ACT, avq)
                for tt in range(8):
                    pg, pu = ps[2 * (tt % 2)], ps[2 * (tt % 2) + 1]
                    bank_wait("tensor", pg)
                    for k in range(8):
                        t_op(lambda e, k=k, tt=tt, ch=ch, pg=pg, wgc=wgc: e.matmul(
                            pg[:, 0:ch], xqT[:, k, tt * 128:(tt + 1) * 128],
                            wgc[:, k, :], start=(k == 0), stop=(k == 7)))
                    bank_wait("tensor", pu)
                    for k in range(8):
                        t_op(lambda e, k=k, tt=tt, ch=ch, pu=pu, wuc=wuc: e.matmul(
                            pu[:, 0:ch], xqT[:, k, tt * 128:(tt + 1) * 128],
                            wuc[:, k, :], start=(k == 0), stop=(k == 7)))
                    mmv = p.val(SEM_MM)
                    p.wait("scalar", SEM_MM, mmv)
                    hsil = bigA[:, 6144:6656]
                    hch = bigA[:, 6656:7168]
                    p.wait("scalar", SEM_DVE, p.val(SEM_DVE))
                    av = s_op(lambda e, pg=pg, ch=ch, hsil=hsil: e.activation(
                        out=hsil[:, 0:ch], in_=pg[:, 0:ch], func=AF.Silu))
                    bank_mark(pg, SEM_ACT)
                    p.wait("vector", SEM_ACT, av)
                    v_op(lambda e, pu=pu, ch=ch, hsil=hsil, hch=hch: e.tensor_tensor(
                        out=hch[:, 0:ch], in0=hsil[:, 0:ch], in1=pu[:, 0:ch],
                        op=OP.mult))
                    bank_mark(pu, SEM_DVE)
                    if dump:
                        dvh = p.val(SEM_DVE)
                        p.wait("sync", SEM_DVE, dvh)
                        dma_out(h_dbg[th * 8 + tt][:, c0:c0 + ch], hch[:, 0:ch])
                    hqc = bigB[:, 30720:30720 + ch]
                    p.wait("vector", SEM_MM, p.val(SEM_MM))
                    quant(hqc, hch[:, 0:ch], ch, 0, 31232)
                    dv = p.val(SEM_DVE)
                    p.wait("tensor", SEM_DVE, dv)
                    transpose_tiles(hqc, ch // 128,
                                    lambda t, c0=c0, tt=tt: bigB[
                                        :, hq_base + (c0 // 128 + t) * 1024 + tt * 128:
                                        hq_base + (c0 // 128 + t) * 1024 + tt * 128 + 128])
            hqT = bigB[:, hq_base:hq_base + 22528].rearrange("p (a b) -> p a b", b=1024)
            avh = p.val(SEM_ACT)

            for dc in range(4):
                wdc = wb_bf16(0, 22 * 256).rearrange("p (a b) -> p a b", b=256)
                p.wait("sync", SEM_MM, p.val(SEM_MM))
                ld = dma_in(wdc, wdT_q[:, dc * 256:(dc + 1) * 256]
                            .rearrange("(a p) b -> p a b", p=128))
                p.wait("tensor", SEM_IN, ld)
                p.wait("tensor", SEM_ACT, avh)
                for tt in range(8):
                    ti = th * 8 + tt
                    pb = ps[2 + (tt % 2)]
                    bank_wait("tensor", pb)
                    for k in range(22):
                        t_op(lambda e, k=k, tt=tt, pb=pb, wdc=wdc: e.matmul(
                            pb[:, 0:256], hqT[:, k, tt * 128:(tt + 1) * 128],
                            wdc[:, k, :], start=(k == 0), stop=(k == 21)))
                    mmv = p.val(SEM_MM)
                    p.wait("vector", SEM_MM, mmv)
                    dst = stream[:, ti * DM + dc * 256:ti * DM + (dc + 1) * 256]
                    v_op(lambda e, pb=pb, dst=dst: e.scalar_tensor_tensor(
                        out=dst, in0=pb[:, 0:256], scalar=0.5, in1=dst,
                        op0=OP.mult, op1=OP.add))
                    bank_mark(pb, SEM_DVE)

    def mamba_layer(li):
        xn = bigB[:, 0:16384].rearrange("p (a b) -> p a b", b=DM)
        rstd = stat_sb[:, 6:7]
        p.wait("scalar", SEM_DVE, p.val(SEM_DVE))
        p.wait("sync", SEM_DVE, p.val(SEM_DVE))
        for i in range(NT):
            src_t = stream[:, i * DM:(i + 1) * DM]
            rms_rstd(src_t, rstd)
            v_op(lambda e, s=src_t, i=i: e.tensor_single_scalar(
                out=xn[:, i, :], in_=s, scalar=rstd, op=OP.mult))
            dma_out(h_sp[i], src_t)
        hspv = p.val(SEM_OUT)
        dv = p.val(SEM_DVE)
        p.wait("tensor", SEM_DVE, dv)
        p.wait("scalar", SEM_OUT, hspv)

        xnT = stream_b[:, 0:16384].rearrange("p (a b) -> p a b", b=2048)
        for i in range(NT):
            transpose_tiles(xn[:, i, :], 8,
                            lambda t, i=i: xnT[:, t, i * 128:(i + 1) * 128])
        avT = p.val(SEM_ACT)
        p.wait("tensor", SEM_ACT, avT)

        xs = bigB[:, 0:16 * 2048].rearrange("p (a b) -> p a b", b=2048)
        wic_f = bigA[:, 0:1024].rearrange("p (a b) -> p a b", b=128)
        wic = wb_bf16(2048, 8 * 128).rearrange("p (a b) -> p a b", b=128)
        rtmp = bigA[:, 1024:1536]
        for mt in range(32):
            p.wait("sync", SEM_MM, p.val(SEM_MM))
            ld = dma_in(wic_f, wiT_h[li][:, mt * 128:(mt + 1) * 128]
                        .rearrange("(a p) b -> p a b", p=128))
            p.wait("vector", SEM_IN, ld)
            v_op(lambda e: e.tensor_tensor(
                out=wic, in0=wic_f,
                in1=nw_sb[:, li * 8:(li + 1) * 8].rearrange("p (a o) -> p a o", o=1)
                    .broadcast_to([128, 8, 128]),
                op=OP.mult))
            cv = p.val(SEM_DVE)
            p.wait("tensor", SEM_DVE, cv)
            for tb in range(NTB):
                pb = ps[tb % 2]
                bank_wait("tensor", pb)
                for k in range(8):
                    t_op(lambda e, k=k, tb=tb, pb=pb: e.matmul(
                        pb[:], wic[:, k, :], xnT[:, k, tb * 512:(tb + 1) * 512],
                        start=(k == 0), stop=(k == 7)))
                p.wait("scalar", SEM_MM, p.val(SEM_MM))
                bcol = bi_sb[:, li * 32 + mt:li * 32 + mt + 1]
                if mt < 16:
                    s_op(lambda e, pb=pb, mt=mt, tb=tb, bcol=bcol: e.add(
                        out=xs[:, mt, tb * 512:(tb + 1) * 512], in_=pb[:], add=bcol))
                else:
                    p.wait("scalar", SEM_OUT, p.val(SEM_OUT))
                    av = s_op(lambda e, pb=pb, bcol=bcol: e.add(
                        out=rtmp, in_=pb[:], add=bcol))
                    p.wait("sync", SEM_ACT, av)
                    dma_out(res_sp[mt - 16][:, tb * 512:(tb + 1) * 512], rtmp)
                bank_mark(pb, SEM_ACT)

        hpf = bigA[:, 2048:2048 + 64]
        xsv = p.val(SEM_ACT)
        p.wait("vector", SEM_ACT, xsv)
        for dt in range(NDT):
            v_op(lambda e, dt=dt: e.tensor_copy(
                out=hpf[:, dt * 4:dt * 4 + 2], in_=xs[:, dt, 0:2]))
            v_op(lambda e, dt=dt: e.tensor_copy(
                out=hpf[:, dt * 4 + 2:dt * 4 + 4], in_=xs[:, dt, 2046:2048]))
        hv = p.val(SEM_DVE)
        p.wait("sync", SEM_DVE, hv)
        gv = dma_out(gin_h, hpf)
        p.wait("gpsimd", SEM_OUT, gv)
        gp_slack(2)
        p.op("gpsimd", lambda e: e.collective_compute(
            "AllGather", OP.bypass, replica_groups=[list(range(NCORES))],
            ins=[gin_h], outs=[gout_h]), SEM_CC)
        p.wait("sync", SEM_CC, p.val(SEM_CC))
        dma_slack(2)
        gl = bigA[:, 2112:2112 + 512].rearrange("p (a b) -> p a b", b=64)
        lg = dma_in(gl, gout_h.rearrange("c p f -> p c f"))
        p.wait("vector", SEM_IN, lg)
        v_op(lambda e: e.memset(halo_sb[:], 0.0))
        vpad()
        vpad()
        hl4 = halo_sb.rearrange("p (a b) -> p a b", b=4)
        for s in range(NCORES):
            v_op(lambda e, s=s: e.scalar_tensor_tensor(
                out=hl4[:, :, 0:2],
                in0=gl[:, s, :].rearrange("p (a b) -> p a b", b=4)[:, :, 2:4],
                scalar=oh_sb[:, 8 + s:9 + s],
                in1=hl4[:, :, 0:2],
                op0=OP.mult, op1=OP.add))
            v_op(lambda e, s=s: e.scalar_tensor_tensor(
                out=hl4[:, :, 2:4],
                in0=gl[:, s, :].rearrange("p (a b) -> p a b", b=4)[:, :, 0:2],
                scalar=oh_sb[:, 16 + s:17 + s],
                in1=hl4[:, :, 2:4],
                op0=OP.mult, op1=OP.add))
        halo_b = wb_bf16(1280, NDT * 4).rearrange("p (a b) -> p a b", b=4)
        vpad()
        vpad()
        v_op(lambda e: e.tensor_copy(out=halo_b.rearrange("p a b -> p (a b)"),
                                     in_=halo_sb[:]))
        hv2 = p.val(SEM_DVE)
        p.wait("tensor", SEM_DVE, hv2)

        ub = bigA[:, 0:4096].bitcast(BF16).rearrange("p (a b) -> p a b", b=512)
        uf = bigA[:, 4096:4608]
        r_t = bigA[:, 4608:5120]
        m_t = bigA[:, 5120:5632]
        e_t = bigA[:, 5632:6144]
        d_t = bigA[:, 6144:6656]
        w_t = bigA[:, 6656:7168]
        y_t = bigA[:, 7168:7680]
        ur_t = bigA[:, 7680:8192]
        diag = wb_bf16(0, 5 * 128).rearrange("p (a b) -> p a b", b=128)
        wxc_f = wbuf[:, 1536:1536 + 1536].rearrange("p (a b) -> p a b", b=96)
        wxc = wb_bf16(512, 16 * 96).rearrange("p (a b) -> p a b", b=96)
        ldx = dma_in(wxc_f, wxT_h[li].rearrange("(a p) b -> p a b", p=128))
        p.wait("vector", SEM_IN, ldx)
        v_op(lambda e: e.tensor_copy(out=wxc.rearrange("p a b -> p (a b)"),
                                     in_=wxc_f.rearrange("p a b -> p (a b)")))
        wxv = p.val(SEM_DVE)
        p.wait("tensor", SEM_DVE, wxv)
        xdbl = stream[0:96, 8192:10240]
        dtwb = [wbuf[0:64, 1536:1536 + 128], wbuf[0:64, 1664:1664 + 128]]
        dtw_ld = [0, 0]
        dtw_mm = [0, 0]
        cdT = scol[:, 768:784]
        for tb in range(NTB):
            for dt in range(NDT):
                p.wait("vector", SEM_MM, p.val(SEM_MM))
                for j in range(5):
                    v_op(lambda e, j=j, dt=dt: e.tensor_single_scalar(
                        out=diag[:, j, :], in_=identf[:],
                        scalar=cw_sb[:, (li * NDT + dt) * 5 + j:
                                     (li * NDT + dt) * 5 + j + 1], op=OP.mult))
                dv = p.val(SEM_DVE)
                p.wait("tensor", SEM_DVE, dv)
                pb = ps[dt % 2]
                bank_wait("tensor", pb)
                segs = []
                for j in range(5):
                    lo = tb * 512 + j - 2
                    hi = lo + 512
                    if lo < 0:
                        nh = -lo
                        segs.append((j, pb[:, 0:nh], halo_b[:, dt, 2 - nh:2]))
                        segs.append((j, pb[:, nh:512], xs[:, dt, 0:512 - nh]))
                    elif hi > 2048:
                        nh = hi - 2048
                        segs.append((j, pb[:, 0:512 - nh], xs[:, dt, lo:2048]))
                        segs.append((j, pb[:, 512 - nh:512], halo_b[:, dt, 2:2 + nh]))
                    else:
                        segs.append((j, pb[:], xs[:, dt, lo:lo + 512]))
                for si, (j, pslice, mov) in enumerate(segs):
                    t_op(lambda e, j=j, pslice=pslice, mov=mov, si=si,
                         last=(si == len(segs) - 1): e.matmul(
                        pslice, diag[:, j, :], mov, start=(si == 0), stop=last))
                p.wait("scalar", SEM_MM, p.val(SEM_MM))
                ccol = cb_sb[:, li * NDT + dt:li * NDT + dt + 1]
                p.wait("scalar", SEM_OUT, p.val(SEM_OUT))
                p.wait("scalar", SEM_DVE, p.val(SEM_DVE))
                av = s_op(lambda e, pb=pb, ccol=ccol: e.activation(
                    out=uf, in_=pb[:], func=AF.Silu, bias=ccol))
                bank_mark(pb, SEM_ACT)
                p.wait("vector", SEM_ACT, av)
                v_op(lambda e, dt=dt: e.tensor_copy(out=ub[:, dt, :], in_=uf))
                p.wait("sync", SEM_ACT, av)
                dma_out(u_sp[dt][:, tb * 512:(tb + 1) * 512], uf)
            p.wait("sync", SEM_OUT, p.val(SEM_OUT))
            dv = p.val(SEM_DVE)
            p.wait("tensor", SEM_DVE, dv)
            pbx = ps[2]
            bank_wait("tensor", pbx)
            for k in range(NDT):
                t_op(lambda e, k=k, pbx=pbx: e.matmul(
                    pbx[0:96, :], wxc[:, k, :], ub[:, k, :],
                    start=(k == 0), stop=(k == NDT - 1)))
            mv = p.val(SEM_MM)
            p.wait("scalar", SEM_MM, mv)
            av = s_op(lambda e, pbx=pbx, tb=tb: e.copy(
                out=xdbl[:, tb * 512:(tb + 1) * 512], in_=pbx[0:96, :]))
            bank_mark(pbx, SEM_ACT)
            p.wait("tensor", SEM_ACT, av)
            p.wait("sync", SEM_MM, p.val(SEM_MM))
            dtw_ld[0] = dma_in(dtwb[0], wdtT_h[li][:, 0:128])
            for mt in range(NDT):
                wpar = mt % 2
                if mt + 1 < NDT:
                    p.wait("sync", SEM_MM, dtw_mm[1 - wpar])
                    dtw_ld[1 - wpar] = dma_in(
                        dtwb[1 - wpar], wdtT_h[li][:, (mt + 1) * 128:(mt + 2) * 128])
                p.wait("sync", SEM_DVE, p.val(SEM_DVE))
                dma_in(ur_t, u_sp[mt][:, tb * 512:(tb + 1) * 512])
                p.wait("tensor", SEM_IN, dtw_ld[wpar])
                pb2 = ps[3]
                bank_wait("tensor", pb2)
                t_op(lambda e, tb=tb, pb2=pb2, wpar=wpar: e.matmul(
                    pb2[:], dtwb[wpar], xdbl[0:64, tb * 512:(tb + 1) * 512],
                    start=True, stop=True))
                dtw_mm[wpar] = p.val(SEM_MM)
                mv = p.val(SEM_MM)
                p.wait("scalar", SEM_MM, mv)
                p.wait("vector", SEM_MM, mv)
                bcol = dtb_sb[:, li * NDT + mt:li * NDT + mt + 1]
                p.wait("scalar", SEM_OUT, p.val(SEM_OUT))
                p.wait("vector", SEM_OUT, p.val(SEM_OUT))
                av = s_op(lambda e, pb2=pb2, bcol=bcol: e.activation(
                    out=r_t, in_=pb2[:], func=AF.Relu, bias=bcol))
                p.wait("vector", SEM_ACT, p.val(SEM_ACT))
                v_op(lambda e, pb2=pb2: e.scalar_tensor_tensor(
                    out=m_t, in0=r_t, scalar=2.0, in1=pb2[:],
                    op0=OP.mult, op1=OP.subtract))
                v_op(lambda e, bcol=bcol: e.tensor_single_scalar(
                    out=m_t, in_=m_t, scalar=bcol, op=OP.subtract))
                bank_mark(pb2, SEM_DVE)
                dv = p.val(SEM_DVE)
                p.wait("scalar", SEM_DVE, dv)
                av = s_op(lambda e: e.activation(out=e_t, in_=m_t, func=AF.Exp,
                                                 scale=-1.0))
                p.wait("vector", SEM_ACT, av)
                v_op(lambda e: e.tensor_single_scalar(out=e_t, in_=e_t, scalar=1.0,
                                                      op=OP.add))
                dv = p.val(SEM_DVE)
                p.wait("scalar", SEM_DVE, dv)
                av = s_op(lambda e: e.activation(out=m_t, in_=e_t, func=AF.Ln))
                p.wait("vector", SEM_ACT, av)
                p.wait("vector", SEM_IN, p.val(SEM_IN))
                v_op(lambda e: e.tensor_tensor(out=d_t, in0=r_t, in1=m_t, op=OP.add))
                v_op(lambda e: e.tensor_reduce(out=stat_sb[:, 8:9], in_=d_t,
                                               axis=AX.X, op=OP.add))
                vpad()
                if tb == 0:
                    v_op(lambda e, mt=mt: e.tensor_copy(out=cdT[:, mt:mt + 1],
                                                        in_=stat_sb[:, 8:9]))
                else:
                    v_op(lambda e, mt=mt: e.tensor_tensor(
                        out=cdT[:, mt:mt + 1], in0=cdT[:, mt:mt + 1],
                        in1=stat_sb[:, 8:9], op=OP.add))
                v_op(lambda e: e.tensor_tensor(out=w_t, in0=d_t,
                                                   in1=ur_t, op=OP.mult))
                dcol = d_sb[:, li * NDT + mt:li * NDT + mt + 1]
                v_op(lambda e, dcol=dcol: e.tensor_single_scalar(
                    out=y_t, in_=ur_t, scalar=dcol, op=OP.mult))
                dv = p.val(SEM_DVE)
                p.wait("sync", SEM_DVE, dv)
                dma_out(del_sp[mt][:, tb * 512:(tb + 1) * 512], d_t)
                dma_out(w_sp[mt][:, tb * 512:(tb + 1) * 512], w_t)
                dma_out(yd_sp[mt][:, tb * 512:(tb + 1) * 512], y_t)
        p.wait("sync", SEM_ACT, p.val(SEM_ACT))
        bcv = dma_out(bc_sp, xdbl[64:96, :])
        p.wait("sync", SEM_OUT, p.val(SEM_OUT))

        reps = stream
        bigBf = bigB.bitcast(F32)
        dltb = [bigA[:, 0:2048], bigA[:, 2048:4096]]
        wtb = [bigA[:, 4096:6144], bigA[:, 6144:8192]]
        ytb = [bigA[:, 8192:10240], bigA[:, 10240:12288]]
        ab = [bigBf[:, 0:2048], bigBf[:, 2048:4096]]
        zb = [bigBf[:, 4096:6144], bigBf[:, 6144:8192]]
        hb = [bigBf[:, 8192:10240], bigBf[:, 10240:12288]]
        tb_ = [bigBf[:, 12288:14336], bigBf[:, 14336:16384]]
        Mv = scol[:, 0:256]
        hend = scol[:, 256:512]
        hin = scol[:, 512:768]
        tc = scol[:, 800:1056]
        ld_free = {0: [], 1: []}
        ab_free = {0: [], 1: []}
        zb_free = {0: [], 1: []}
        for ng in range(4):
            p.wait("sync", SEM_GP, p.val(SEM_GP))
            p.wait("sync", SEM_DVE, p.val(SEM_DVE))
            p.wait("sync", SEM_MM, p.val(SEM_MM))
            p.wait("sync", SEM_ACT, p.val(SEM_ACT))
            p.wait("sync", SEM_OUT, p.val(SEM_OUT))
            if ng == 0:
                dma_slack(1)
            rl = None
            for j in range(4):
                n = ng * 4 + j
                dma_in(reps[:, (2 * j) * 2048:(2 * j + 1) * 2048],
                       bc_sp[n:n + 1, :].partition_broadcast(128))
                rl = dma_in(reps[:, (2 * j + 1) * 2048:(2 * j + 2) * 2048],
                            bc_sp[16 + n:17 + n, :].partition_broadcast(128))
            p.wait("vector", SEM_IN, rl)
            p.wait("gpsimd", SEM_IN, rl)
            ld_done = {}
            for dt in range(NDT + 1):
                if dt < NDT:
                    par = dt % 2
                    for sem, v in ld_free[par]:
                        p.wait("sync", sem, v)
                    ld_free[par] = []
                    dma_in(dltb[par], del_sp[dt])
                    dma_in(wtb[par], w_sp[dt])
                    ysrc = yd_sp[dt] if ng == 0 else y_sp[dt]
                    ld_done[dt] = dma_in(ytb[par], ysrc)
                if dt == 0:
                    continue
                cdt = dt - 1
                par = cdt % 2
                dlt_c, wt_c, yt_c = dltb[par], wtb[par], ytb[par]
                p.wait("scalar", SEM_IN, ld_done[cdt])
                p.wait("gpsimd", SEM_IN, ld_done[cdt])
                p.wait("vector", SEM_IN, ld_done[cdt])
                for j in range(4):
                    n = ng * 4 + j
                    bpar = j % 2
                    acol = a_sb[:, (li * NDT + cdt) * NST + n:
                                (li * NDT + cdt) * NST + n + 1]
                    for sem, v in ab_free[bpar]:
                        p.wait("scalar", sem, v)
                    ab_free[bpar] = []
                    av = s_op(lambda e, acol=acol, A=ab[bpar], D=dlt_c: e.activation(
                        out=A, in_=D, func=AF.Exp, scale=acol))
                    for sem, v in zb_free[bpar]:
                        p.wait("vector", sem, v)
                    zb_free[bpar] = []
                    v_op(lambda e, j=j, Z=zb[bpar], W=wt_c: e.tensor_tensor(
                        out=Z, in0=W,
                        in1=reps[:, (2 * j) * 2048:(2 * j + 1) * 2048], op=OP.mult))
                    p.wait("vector", SEM_ACT, av)
                    v_op(lambda e, A=ab[bpar], Z=zb[bpar], H=hb[bpar]:
                         e.tensor_tensor_scan(out=H, data0=A, data1=Z, initial=0.0,
                                              op0=OP.mult, op1=OP.add))
                    ab_free[bpar].append((SEM_DVE, p.val(SEM_DVE)))
                    zb_free[bpar].append((SEM_DVE, p.val(SEM_DVE)))
                    p.wait("gpsimd", SEM_DVE, p.val(SEM_DVE))
                    gc = g_op(lambda e, j=j, H=hb[bpar], T=tb_[bpar]: e.tensor_tensor(
                        out=T, in0=H,
                        in1=reps[:, (2 * j + 1) * 2048:(2 * j + 2) * 2048],
                        op=OP.mult))
                    p.wait("vector", SEM_GP, gc)
                    v_op(lambda e, T=tb_[bpar], Y=yt_c: e.tensor_tensor(
                        out=Y, in0=Y, in1=T, op=OP.add))
                    v_op(lambda e, cdt=cdt, n=n, H=hb[bpar]: e.tensor_copy(
                        out=hend[:, cdt * 16 + n:cdt * 16 + n + 1],
                        in_=H[:, 2047:2048]))
                    ab_free[bpar].append((SEM_GP, p.val(SEM_GP)))
                dv = p.val(SEM_DVE)
                p.wait("sync", SEM_DVE, dv)
                ow = dma_out(y_sp[cdt], yt_c)
                ld_free[par] = [(SEM_ACT, p.val(SEM_ACT)),
                                (SEM_GP, p.val(SEM_GP)),
                                (SEM_OUT, ow)]

        p.wait("scalar", SEM_DVE, p.val(SEM_DVE))
        cdT = scol[:, 768:784]
        for dt in range(NDT):
            s_op(lambda e, dt=dt: e.activation(
                out=Mv[:, dt * 16:(dt + 1) * 16],
                in_=a_sb[:, (li * NDT + dt) * NST:(li * NDT + dt + 1) * NST],
                func=AF.Exp, scale=cdT[:, dt:dt + 1]))
        sv = p.val(SEM_ACT)
        p.wait("sync", SEM_ACT, sv)
        p.wait("sync", SEM_DVE, p.val(SEM_DVE))
        dma_out(gin_s[:, 0:256], Mv)
        g2 = dma_out(gin_s[:, 256:512], hend)
        p.wait("gpsimd", SEM_OUT, g2)
        gp_slack(2)
        p.op("gpsimd", lambda e: e.collective_compute(
            "AllGather", OP.bypass, replica_groups=[list(range(NCORES))],
            ins=[gin_s], outs=[gout_s]), SEM_CC)
        p.wait("sync", SEM_CC, p.val(SEM_CC))
        p.wait("sync", SEM_GP, p.val(SEM_GP))
        dma_slack(2)
        gat = bigBf[:, 14336 - 4096 + 4096:16384]
        gat = bigA[:, 0:4096].rearrange("p (a b) -> p a b", b=512)
        lg = dma_in(gat, gout_s.rearrange("c p f -> p c f"))
        p.wait("vector", SEM_IN, lg)
        v_op(lambda e: e.memset(hin, 0.0))
        v_op(lambda e: e.memset(tc, 0.0))
        vpad()
        vpad()
        for sc_ in range(NCORES):
            if sc_ % 4 == 0:
                v_op(lambda e: e.memset(tc, 0.0))
                vpad()
                vpad()
            else:
                v_op(lambda e, sc_=sc_: e.tensor_tensor(
                    out=tc, in0=tc, in1=gat[:, sc_ - 1, 0:256], op=OP.mult))
                v_op(lambda e, sc_=sc_: e.tensor_tensor(
                    out=tc, in0=tc, in1=gat[:, sc_ - 1, 256:512], op=OP.add))
                v_op(lambda e, sc_=sc_: e.scalar_tensor_tensor(
                    out=hin, in0=tc, scalar=oh_sb[:, sc_:sc_ + 1], in1=hin,
                    op0=OP.mult, op1=OP.add))

        zeros_t = wtb[0]
        p.wait("vector", SEM_IN, p.val(SEM_IN))
        p.wait("vector", SEM_GP, p.val(SEM_GP))
        v_op(lambda e: e.memset(zeros_t, 0.0))
        ld_free = {0: [], 1: []}
        gb_free = {0: [], 1: []}
        for ng in range(4):
            p.wait("sync", SEM_DVE, p.val(SEM_DVE))
            p.wait("sync", SEM_OUT, p.val(SEM_OUT))
            if ng == 0:
                dma_slack(1)
            rl = None
            for j in range(4):
                n = ng * 4 + j
                rl = dma_in(reps[:, (2 * j + 1) * 2048:(2 * j + 2) * 2048],
                            bc_sp[16 + n:17 + n, :].partition_broadcast(128))
            p.wait("vector", SEM_IN, rl)
            ld_done = {}
            for dt in range(NDT + 1):
                if dt < NDT:
                    par = dt % 2
                    for sem, v in ld_free[par]:
                        p.wait("sync", sem, v)
                    ld_free[par] = []
                    dma_in(dltb[par], del_sp[dt])
                    ld_done[dt] = dma_in(ytb[par], y_sp[dt])
                if dt == 0:
                    continue
                cdt = dt - 1
                par = cdt % 2
                dlt_c, yt_c = dltb[par], ytb[par]
                cd_c = hb[par]
                p.wait("vector", SEM_IN, ld_done[cdt])
                p.wait("vector", SEM_ACT, p.val(SEM_ACT))
                v_op(lambda e, D=dlt_c, CD=cd_c: e.tensor_tensor_scan(
                    out=CD, data0=D, data1=zeros_t, initial=0.0,
                    op0=OP.add, op1=OP.add))
                dvc = p.val(SEM_DVE)
                p.wait("scalar", SEM_DVE, dvc)
                for j in range(4):
                    n = ng * 4 + j
                    bpar = j % 2
                    acol = a_sb[:, (li * NDT + cdt) * NST + n:
                                (li * NDT + cdt) * NST + n + 1]
                    for sem, v in gb_free[bpar]:
                        p.wait("scalar", sem, v)
                    gb_free[bpar] = []
                    av = s_op(lambda e, acol=acol, G=ab[bpar], CD=cd_c: e.activation(
                        out=G, in_=CD, func=AF.Exp, scale=acol))
                    p.wait("vector", SEM_ACT, av)
                    hcol = hin[:, cdt * 16 + n:cdt * 16 + n + 1]
                    v_op(lambda e, j=j, hcol=hcol, G=ab[bpar], T=tb_[bpar]:
                         e.scalar_tensor_tensor(
                        out=T, in0=G, scalar=hcol,
                        in1=reps[:, (2 * j + 1) * 2048:(2 * j + 2) * 2048],
                        op0=OP.mult, op1=OP.mult))
                    gb_free[bpar].append((SEM_DVE, p.val(SEM_DVE)))
                    v_op(lambda e, T=tb_[bpar], Y=yt_c: e.tensor_tensor(
                        out=Y, in0=Y, in1=T, op=OP.add))
                dv = p.val(SEM_DVE)
                p.wait("sync", SEM_DVE, dv)
                ow = dma_out(y_sp[cdt], yt_c)
                ld_free[par] = [(SEM_DVE, dv), (SEM_OUT, ow)]

        p.wait("sync", SEM_OUT, p.val(SEM_OUT))
        p.wait("sync", SEM_ACT, p.val(SEM_ACT))
        p.wait("sync", SEM_DVE, p.val(SEM_DVE))
        p.wait("scalar", SEM_DVE, p.val(SEM_DVE))
        gld = {}
        gfree = {0: [], 1: []}
        for dt in range(NDT + 1):
            if dt < NDT:
                par = dt % 2
                for sem, v in gfree[par]:
                    p.wait("sync", sem, v)
                gfree[par] = []
                dma_in(ytb[par], y_sp[dt])
                gld[dt] = dma_in(dltb[par], res_sp[dt])
            if dt == 0:
                continue
            cdt = dt - 1
            par = cdt % 2
            p.wait("scalar", SEM_IN, gld[cdt])
            av = s_op(lambda e, R=dltb[par], A=wtb[par]: e.activation(
                out=A, in_=R, func=AF.Silu))
            p.wait("vector", SEM_ACT, av)
            p.wait("vector", SEM_IN, gld[cdt])
            v_op(lambda e, cdt=cdt, Y=ytb[par], A=wtb[par]: e.tensor_tensor(
                out=bigB[:, cdt * 2048:(cdt + 1) * 2048], in0=Y, in1=A,
                op=OP.mult))
            gfree[par] = [(SEM_DVE, p.val(SEM_DVE)), (SEM_ACT, p.val(SEM_ACT))]
        y_g = bigB[:, 0:16 * 2048].rearrange("p (a b) -> p a b", b=2048)
        ygv = p.val(SEM_DVE)
        p.wait("tensor", SEM_DVE, ygv)

        woc = bigA[:, 0:8192].bitcast(BF16).rearrange("p (a b) -> p a b", b=1024)
        wof = bigA[:, 8192:10240]
        for dt2 in range(8):
            p.wait("sync", SEM_DVE, p.val(SEM_DVE))
            ld = dma_in(wof.rearrange("p (a b) -> p a b", b=DM),
                        woT_h[li][dt2 * 256:(dt2 + 1) * 256, :]
                        .rearrange("(a p) b -> p a b", p=128))
            p.wait("vector", SEM_IN, ld)
            v_op(lambda e, dt2=dt2: e.tensor_copy(
                out=woc[:, 2 * dt2:2 * dt2 + 2, :].rearrange("p a b -> p (a b)"),
                in_=wof))
        wov = p.val(SEM_DVE)
        p.wait("tensor", SEM_DVE, wov)
        p.wait("sync", SEM_OUT, p.val(SEM_OUT))
        for i in range(NT):
            p.wait("sync", SEM_DVE, p.val(SEM_DVE))
            hld = dma_in(wof[:, 0:1024], h_sp[i])
            for c in range(2):
                pb = ps[c]
                bank_wait("tensor", pb)
                for k in range(NDT):
                    t_op(lambda e, k=k, i=i, c=c, pb=pb: e.matmul(
                        pb[:], y_g[:, k, i * 128:(i + 1) * 128],
                        woc[:, k, c * 512:(c + 1) * 512],
                        start=(k == 0), stop=False))
                t_op(lambda e, c=c, pb=pb: e.matmul(
                    pb[:], onesr[0:1, 0:128],
                    bo_sb[0:1, li * DM + c * 512:li * DM + (c + 1) * 512],
                    start=False, stop=True))
            mv = p.val(SEM_MM)
            p.wait("vector", SEM_MM, mv)
            p.wait("vector", SEM_IN, hld)
            for c in range(2):
                pb = ps[c]
                v_op(lambda e, i=i, c=c, pb=pb: e.tensor_tensor(
                    out=stream[:, i * DM + c * 512:i * DM + (c + 1) * 512],
                    in0=wof[:, c * 512:(c + 1) * 512], in1=pb[:], op=OP.add))
                bank_mark(pb, SEM_DVE)
        if dbg:
            p.wait("sync", SEM_DVE, p.val(SEM_DVE))
            for i2 in range(NT):
                dma_out(xln_dbg[i2], stream[:, i2 * DM:(i2 + 1) * DM])
            p.wait("gpsimd", SEM_OUT, p.val(SEM_OUT))
            p.wait("vector", SEM_OUT, p.val(SEM_OUT))
            p.wait("scalar", SEM_OUT, p.val(SEM_OUT))

    ffn_call(dump=dbg)
    p.wait("sync", SEM_DVE, p.val(SEM_DVE))
    for i in range(NT):
        src_t = stream[:, i * DM:(i + 1) * DM]
        xv = dma_out(x_sp[i], src_t)
        xln = bigA[:, 5120:6144]
        ln_tile(src_t, xln)
        p.wait("vector", SEM_OUT, xv)
        v_op(lambda e, s=src_t, xln=xln: e.tensor_copy(out=s, in_=xln))
    for li in range(nlay):
        mamba_layer(li)
    rstd2 = stat_sb[:, 7:8]
    xr_t = bigA[:, 6144:7168]
    nfw_sb = bigA[:, 7168:8192]
    p.wait("sync", SEM_DVE, p.val(SEM_DVE))
    p.wait("sync", SEM_ACT, p.val(SEM_ACT))
    p.wait("sync", SEM_OUT, p.val(SEM_OUT))
    nfl = dma_in(nfw_sb, nfw_h.partition_broadcast(128))
    p.wait("vector", SEM_IN, nfl)
    p.wait("scalar", SEM_DVE, p.val(SEM_DVE))
    for i in range(NT):
        src_t = stream[:, i * DM:(i + 1) * DM]
        rms_rstd(src_t, rstd2)
        t1 = bigA[:, 5120:6144]
        v_op(lambda e, s=src_t, t1=t1: e.scalar_tensor_tensor(
            out=t1, in0=s, scalar=rstd2, in1=nfw_sb, op0=OP.mult, op1=OP.mult))
        p.wait("sync", SEM_DVE, p.val(SEM_DVE))
        xld = dma_in(xr_t, x_sp[i])
        p.wait("vector", SEM_IN, xld)
        vpad()
        v_op(lambda e, s=src_t, t1=t1: e.tensor_tensor(
            out=s, in0=t1, in1=xr_t, op=OP.add))
    ffn_call()
    fv = p.val(SEM_DVE)
    p.wait("sync", SEM_DVE, fv)
    ov = dma_out(out_h.rearrange("(a p) b -> p a b", p=128), stream.rearrange("p (a b) -> p a b", b=DM))
    p.wait("sync", SEM_OUT, ov)

    with nc.Block() as block:
        p.replay(block)
    return nc


_NC_CACHE = [None]


def _stage_inputs(inputs):
    x = np.asarray(inputs["L0_attn_x"], np.float32)
    ipw = np.asarray(inputs["L0_attn_in_proj_w"], np.float32)
    ipb = np.asarray(inputs["L0_attn_in_proj_b"], np.float32)
    cw = np.asarray(inputs["L0_attn_conv_w"], np.float32)
    cb = np.asarray(inputs["L0_attn_conv_b"], np.float32)
    xpw = np.asarray(inputs["L0_attn_x_proj_w"], np.float32)
    dtw = np.asarray(inputs["L0_attn_dt_proj_w"], np.float32)
    dtb = np.asarray(inputs["L0_attn_dt_proj_b"], np.float32)
    alog = np.asarray(inputs["L0_attn_A_log"], np.float32)
    dd = np.asarray(inputs["L0_attn_D"], np.float32)
    opw = np.asarray(inputs["L0_attn_out_proj_w"], np.float32)
    opb = np.asarray(inputs["L0_attn_out_proj_b"], np.float32)
    nw = np.asarray(inputs["L0_attn_norm_w"], np.float32)
    nfw = np.asarray(inputs["L0_attn_norm_f_w"], np.float32)
    wg = np.asarray(inputs["L0_ffn_w_gate"], np.float32)
    wu = np.asarray(inputs["L0_ffn_w_up"], np.float32)
    wd = np.asarray(inputs["L0_ffn_w_down"], np.float32)

    def colmap(a):
        L = a.shape[0]
        G = a.shape[1] // 128
        return np.ascontiguousarray(a.reshape(L, G, 128).transpose(2, 0, 1)
                                    .reshape(128, L * G))

    shared = {
        "wiT": np.ascontiguousarray(ipw.transpose(0, 2, 1)),
        "wxT": np.ascontiguousarray(xpw.transpose(0, 2, 1)),
        "wdtT": np.ascontiguousarray(dtw.transpose(0, 2, 1)),
        "woT": np.ascontiguousarray(opw.transpose(0, 2, 1)),
        "bi_c": colmap(ipb),
        "cw_c": np.ascontiguousarray(cw.reshape(NLAYER, NDT, 128, 5)
                                     .transpose(2, 0, 1, 3).reshape(128, -1)),
        "cb_c": colmap(cb), "dtb_c": colmap(dtb),
        "alog_c": np.ascontiguousarray(alog.reshape(NLAYER, NDT, 128, NST)
                                       .transpose(2, 0, 1, 3).reshape(128, -1)),
        "d_c": colmap(dd), "nw_c": colmap(nw),
        "nfw_r": np.ascontiguousarray(nfw.reshape(1, DM)),
        "bo_r": np.ascontiguousarray(opb.reshape(1, NLAYER * DM)),
        "wg": wg, "wu": wu, "wd": wd,
        "idm": np.eye(128, dtype=np.float32),
        "idmb": np.eye(128, dtype=np.float32).astype(ml_dtypes.bfloat16),
    }
    in_maps = []
    for c in range(NCORES):
        b, q = c // 4, c % 4
        oh = np.zeros((1, 24), np.float32)
        oh[0, c] = 1.0
        if q > 0:
            oh[0, 8 + (c - 1)] = 1.0
        if q < 3:
            oh[0, 16 + (c + 1)] = 1.0
        m = dict(shared)
        m["x"] = np.ascontiguousarray(x[b, q * TAU:(q + 1) * TAU, :])
        m["oh"] = oh
        in_maps.append(m)
    return in_maps


def kernel(**inputs):
    if _NC_CACHE[0] is None:
        _NC_CACHE[0] = build_program()
    nc = _NC_CACHE[0]
    in_maps = _stage_inputs(inputs)
    res = run_bass_kernel_spmd(nc, in_maps, core_ids=list(range(NCORES)))
    out = np.empty((2, 8192, DM), np.float32)
    for c in range(NCORES):
        b, q = c // 4, c % 4
        out[b, q * TAU:(q + 1) * TAU, :] = res.results[c]["out"]
    return out


# version 11
# speedup vs baseline: 1.2502x; 1.0327x over previous
import numpy as np
import ml_dtypes
import concourse.bass as bass
import concourse.mybir as mybir
from concourse.bass_utils import run_bass_kernel_spmd

F32 = mybir.dt.float32
BF16 = mybir.dt.bfloat16
AF = mybir.ActivationFunctionType
OP = mybir.AluOpType
AX = mybir.AxisListType

NCORES = 8
TAU = 2048
NT = 16
DM = 1024
DI = 2048
NDT = 16
NST = 16
DTR = 64
II = 2816
NIT = 22
NLAYER = 2
NTB = 4
MIDS = [0.25, 0.75, 1.25, 1.75, 2.5, 3.5, 5.0]
EPS = 1e-5


class Prog:
    ENGINES = ("sync", "vector", "scalar", "tensor", "gpsimd")

    def __init__(self, nc):
        self.nc = nc
        self.q = {e: [] for e in self.ENGINES}
        self.sems = {}
        self.cnt = {}

    def sem(self, name):
        if name not in self.sems:
            self.sems[name] = self.nc.alloc_semaphore(name=name)
            self.cnt[name] = 0
        return self.sems[name]

    def op(self, eng, fn, sem=None, amt=1):
        s = self.sem(sem) if sem else None
        if sem:
            self.cnt[sem] += amt

        def run(e, fn=fn, s=s, amt=amt):
            ins = fn(e)
            if s is not None:
                ins.then_inc(s, amt)
        self.q[eng].append(run)
        return self.cnt[sem] if sem else None

    def wait(self, eng, sem, val=None):
        s = self.sem(sem)
        v = self.cnt[sem] if val is None else val

        def run(e, s=s, v=v):
            e.wait_ge(s, v)
        self.q[eng].append(run)

    def val(self, sem):
        return self.cnt.get(sem, 0)

    def replay(self, block):
        for name in self.ENGINES:
            dec = getattr(block, name)
            ops = self.q[name]

            def body(e, ops=ops):
                for f in ops:
                    f(e)
            dec(body)


SEM_IN, SEM_MM, SEM_ACT, SEM_DVE, SEM_GP, SEM_OUT, SEM_CC = (
    "s_in", "s_mm", "s_act", "s_dve", "s_gp", "s_out", "s_cc")


def build_program(dbg=False, nlay=NLAYER):
    nc = bass.Bass("TRN2", detect_race_conditions=False, target_bir_lowering=False, debug=False)
    nc.num_devices = NCORES
    p = Prog(nc)

    def din(name, shape, dtype=F32):
        return nc.dram_tensor(name, shape, dtype, kind="ExternalInput").ap()

    x_in = din("x", [TAU, DM])
    wiT_h = din("wiT", [NLAYER, DM, 2 * DI])
    wxT_h = din("wxT", [NLAYER, DI, 96])
    wdtT_h = din("wdtT", [NLAYER, DTR, DI])
    woT_h = din("woT", [NLAYER, DI, DM])
    bi_h = din("bi_c", [128, NLAYER * 32])
    cw_h = din("cw_c", [128, NLAYER * NDT * 5])
    cb_h = din("cb_c", [128, NLAYER * NDT])
    dtb_h = din("dtb_c", [128, NLAYER * NDT])
    alog_h = din("alog_c", [128, NLAYER * NDT * NST])
    dd_h = din("d_c", [128, NLAYER * NDT])
    nw_h = din("nw_c", [128, NLAYER * 8])
    nfw_h = din("nfw_r", [1, DM])
    bo_h = din("bo_r", [1, NLAYER * DM])
    wg_h = din("wg", [II, DM])
    wu_h = din("wu", [II, DM])
    wd_h = din("wd", [DM, II])
    oh_h = din("oh", [1, 24])
    idm_h = din("idm", [128, 128])
    idmb_h = din("idmb", [128, 128], BF16)

    out_h = nc.dram_tensor("out", [TAU, DM], F32, kind="ExternalOutput").ap()

    def scr(name, shape, dtype=F32, force_internal=False):
        kind = "ExternalOutput" if (dbg and not force_internal) else "Internal"
        return nc.dram_tensor(name, shape, dtype, kind=kind).ap()

    wgT_q = scr("wgT_q", [DM, II], BF16)
    wuT_q = scr("wuT_q", [DM, II], BF16)
    wdT_q = scr("wdT_q", [II, DM], BF16)
    x_sp = scr("x_sp", [NT, 128, DM])
    h_sp = scr("h_sp", [NT, 128, DM])
    res_sp = scr("res_sp", [NDT, 128, TAU])
    del_sp = scr("del_sp", [NDT, 128, TAU])
    w_sp = scr("w_sp", [NDT, 128, TAU])
    yd_sp = scr("yd_sp", [NDT, 128, TAU])
    y_sp = scr("y_sp", [NDT, 128, TAU])
    u_sp = scr("u_sp", [NDT, 128, TAU])
    bc_sp = scr("bc_sp", [32, TAU])
    scan_dbg = scr("scan_dbg", [4, 128, TAU])
    xln_dbg = scr("xln_dbg", [NT, 128, DM])
    xq_dbg = scr("xq_dbg", [NT, 128, DM], BF16)
    h_dbg = scr("h_dbg", [NT, 128, II])
    gin_h = scr("gin_h", [128, NDT * 4], force_internal=True)
    gout_h = scr("gout_h", [NCORES, 128, NDT * 4], force_internal=True)
    gin_s = scr("gin_s", [128, 512], force_internal=True)
    gout_s = scr("gout_s", [NCORES, 128, 512], force_internal=True)

    def sball(name, shape, dtype=F32):
        return nc.alloc_sbuf_tensor(name, shape, dtype).ap()

    identf = sball("identf", [128, 128], F32)
    identb = sball("identb", [128, 128], BF16)
    onesr = sball("onesr", [1, 128], F32)
    bi_sb = sball("bi_sb", [128, NLAYER * 32], F32)
    cw_sb = sball("cw_sb", [128, NLAYER * NDT * 5], F32)
    cb_sb = sball("cb_sb", [128, NLAYER * NDT], F32)
    dtb_sb = sball("dtb_sb", [128, NLAYER * NDT], F32)
    a_sb = sball("a_sb", [128, NLAYER * NDT * NST], F32)
    d_sb = sball("d_sb", [128, NLAYER * NDT], F32)
    nw_sb = sball("nw_sb", [128, NLAYER * 8], F32)
    bo_sb = sball("bo_sb", [1, NLAYER * DM], F32)
    oh_sb = sball("oh_sb", [128, 24], F32)
    stat_sb = sball("stat_sb", [128, 16], F32)
    scol = sball("scol", [128, 1088], F32)
    halo_sb = sball("halo_sb", [128, NDT * 4], F32)

    stream = sball("stream", [128, NT * DM], F32)
    bigA = sball("bigA", [128, 12 * 1024], F32)
    bigB = sball("bigB", [128, 32 * 1024], BF16)
    wbuf = sball("wbuf", [128, 3 * 1024], F32)

    ps = [nc.alloc_psum_tensor(f"ps{i}", [128, 512], F32).ap() for i in range(4)]
    pst = [nc.alloc_psum_tensor(f"pst{i}", [128, 1024], BF16).ap()[:, 0:128]
           for i in range(4)]

    def wb_bf16(off_f32, n):
        return wbuf[:, off_f32:off_f32 + (n + 1) // 2].bitcast(BF16)[:, 0:n]

    stream_b = stream.bitcast(BF16)

    bank_free = {}

    def bank_wait(eng, bank):
        if id(bank) in bank_free:
            sname, v = bank_free[id(bank)]
            p.wait(eng, sname, v)

    def bank_mark(bank, sname):
        bank_free[id(bank)] = (sname, p.val(sname))

    def gp_slack(k=2):
        for _ in range(k):
            p.op("gpsimd", lambda e: e.tensor_copy(out=stat_sb[:, 13:14],
                                                   in_=identf[:, 0:1]), SEM_GP)

    def dma_slack(k=2):
        for _ in range(k):
            dma_in(identf[:], idm_h)

    def dma_in(dst, src):
        return p.op("sync", lambda e, d=dst, s=src: e.dma_start(out=d, in_=s), SEM_IN, 16)

    def dma_out(dst, src):
        return p.op("sync", lambda e, d=dst, s=src: e.dma_start(out=d, in_=s), SEM_OUT, 16)

    def v_op(fn):
        return p.op("vector", fn, SEM_DVE)

    def g_op(fn):
        return p.op("gpsimd", fn, SEM_GP)

    def s_op(fn):
        return p.op("scalar", fn, SEM_ACT)

    def t_op(fn):
        return p.op("tensor", fn, SEM_MM)

    def vpad():
        v_op(lambda e: e.tensor_copy(out=stat_sb[:, 12:13], in_=identf[:, 0:1]))

    dma_in(stream.rearrange("p (a b) -> p a b", b=DM), x_in.rearrange("(a p) b -> p a b", p=128))
    dma_in(bi_sb, bi_h)
    dma_in(cw_sb, cw_h)
    dma_in(cb_sb, cb_h)
    dma_in(dtb_sb, dtb_h)
    dma_in(a_sb, alog_h)
    dma_in(d_sb, dd_h)
    dma_in(nw_sb, nw_h)
    dma_in(bo_sb, bo_h)
    dma_in(oh_sb, oh_h.partition_broadcast(128))
    dma_in(identf[:], idm_h)
    dma_in(identb[:], idmb_h)
    setup_in = p.val(SEM_IN)

    p.wait("gpsimd", SEM_IN, setup_in)
    p.wait("vector", SEM_IN, setup_in)
    v_op(lambda e: e.memset(onesr[:], 1.0))
    p.wait("scalar", SEM_IN, setup_in)
    s_op(lambda e: e.activation(out=bigA[:, 0:NLAYER * NDT * NST], in_=a_sb[:],
                                func=AF.Exp))
    p.wait("vector", SEM_ACT, p.val(SEM_ACT))
    v_op(lambda e: e.tensor_single_scalar(out=a_sb[:],
                                          in_=bigA[:, 0:NLAYER * NDT * NST],
                                          scalar=-1.0, op=OP.mult))
    p.wait("tensor", SEM_IN, setup_in)

    def quant(dst, src, F, aoff, boff):
        nb = F // 16
        xn = bigA[:, aoff:aoff + F]
        qa = bigA[:, aoff + F:aoff + 2 * F]
        xsq = bigA[:, aoff + 2 * F:aoff + 3 * F]
        bm = bigA[:, aoff + 3 * F:aoff + 3 * F + nb]
        scl = bigA[:, aoff + 3 * F + nb:aoff + 3 * F + 2 * nb]
        rcp = bigA[:, aoff + 3 * F + 2 * nb:aoff + 3 * F + 3 * nb]
        m0 = bigB[:, boff:boff + F]
        m1 = bigB[:, boff + F:boff + 2 * F]
        m2 = bigB[:, boff + 2 * F:boff + 3 * F]
        src3 = src.rearrange("p (b k) -> p b k", k=16)
        v_op(lambda e: e.tensor_reduce(out=bm, in_=src3, axis=AX.X, op=OP.max,
                                       apply_absolute_value=True))
        vpad()
        v_op(lambda e: e.tensor_scalar(out=scl, in0=bm, scalar1=1.0 / 6.0,
                                       scalar2=1e-12, op0=OP.mult, op1=OP.max))
        vpad()
        v_op(lambda e: e.reciprocal(out=rcp, in_=scl))
        vpad()
        v_op(lambda e: e.tensor_tensor(
            out=xn.rearrange("p (b k) -> p b k", k=16), in0=src3,
            in1=rcp.rearrange("p (b o) -> p b o", o=1).broadcast_to([128, nb, 16]),
            op=OP.mult))
        dv = p.val(SEM_DVE)
        p.wait("scalar", SEM_DVE, dv)
        avq = s_op(lambda e: e.activation(out=xsq, in_=xn, func=AF.Square))

        def mk(dst_t, th):
            return lambda e: e.tensor_single_scalar(out=dst_t, in_=xsq,
                                                    scalar=th * th, op=OP.is_ge)
        p.wait("vector", SEM_ACT, avq)
        v_op(mk(m0, MIDS[0]))
        v_op(mk(m1, MIDS[1]))
        v_op(lambda e: e.tensor_tensor(out=m0, in0=m0, in1=m1, op=OP.add))
        v_op(mk(m1, MIDS[2]))
        v_op(lambda e: e.tensor_tensor(out=m0, in0=m0, in1=m1, op=OP.add))
        v_op(mk(m1, MIDS[3]))
        v_op(lambda e: e.tensor_tensor(out=m0, in0=m0, in1=m1, op=OP.add))
        v_op(mk(m2, MIDS[4]))
        v_op(mk(m1, MIDS[5]))
        v_op(lambda e: e.tensor_tensor(out=m2, in0=m2, in1=m1, op=OP.add))
        v_op(mk(m1, MIDS[6]))
        v_op(lambda e: e.scalar_tensor_tensor(out=qa, in0=m0, scalar=0.5, in1=m2,
                                              op0=OP.mult, op1=OP.add))
        v_op(lambda e: e.scalar_tensor_tensor(out=qa, in0=m1, scalar=2.0, in1=qa,
                                              op0=OP.mult, op1=OP.add))
        v_op(lambda e: e.tensor_scalar(out=m1, in0=xn, scalar1=0.0, scalar2=2.0,
                                       op0=OP.is_ge, op1=OP.mult))
        v_op(lambda e: e.tensor_tensor(out=xn, in0=qa, in1=m1, op=OP.mult))
        v_op(lambda e: e.tensor_tensor(out=qa, in0=xn, in1=qa, op=OP.subtract))
        v_op(lambda e: e.tensor_tensor(
            out=dst.rearrange("p (b k) -> p b k", k=16),
            in0=qa.rearrange("p (b k) -> p b k", k=16),
            in1=scl.rearrange("p (b o) -> p b o", o=1).broadcast_to([128, nb, 16]),
            op=OP.mult))

    def transpose_tiles(src, ntile, dst_ap_of_t):
        for t in range(ntile):
            pb = pst[t % 4]
            bank_wait("tensor", pb)
            t_op(lambda e, pb=pb, t=t: e.transpose(
                out=pb[:], in_=src[:, t * 128:(t + 1) * 128], identity=identb[:]))
            p.wait("scalar", SEM_MM, p.val(SEM_MM))
            s_op(lambda e, pb=pb, d=dst_ap_of_t(t): e.copy(out=d, in_=pb[:]))
            bank_mark(pb, SEM_ACT)

    def quant_weights():
        bigBf0 = bigB.bitcast(F32)
        wtiles = [bigA[:, 9000:9000 + 2816], bigBf0[:, 8192:8192 + 2816]]
        qtiles = [bigB[:, 0:2816], bigB[:, 2816:5632]]
        jobs = []
        for whbm, dst, rows, cols in ((wg_h, wgT_q, NIT, DM), (wu_h, wuT_q, NIT, DM),
                                      (wd_h, wdT_q, 8, II)):
            for r in range(rows):
                jobs.append((whbm, dst, r, cols))
        lds = {}
        wfree = {0: [], 1: []}
        qfree = {0: [], 1: []}
        for i in range(len(jobs) + 1):
            if i < len(jobs):
                whbm, dst, r, cols = jobs[i]
                par = i % 2
                for sem, v in wfree[par]:
                    p.wait("sync", sem, v)
                wfree[par] = []
                lds[i] = dma_in(wtiles[par][:, 0:cols], whbm[r * 128:(r + 1) * 128, :])
            if i == 0:
                continue
            whbm, dst, r, cols = jobs[i - 1]
            par = (i - 1) % 2
            wt_, qt_ = wtiles[par], qtiles[par]
            p.wait("vector", SEM_IN, lds[i - 1])
            for sem, v in qfree[par]:
                p.wait("vector", sem, v)
            qfree[par] = []
            quant(qt_[:, 0:cols], wt_[:, 0:cols], cols, 0, 22016)
            wfree[par] = [(SEM_DVE, p.val(SEM_DVE))]
            dv = p.val(SEM_DVE)
            p.wait("tensor", SEM_DVE, dv)
            for t in range(cols // 128):
                pb = pst[t % 4]
                ev = wb_bf16(2048 + (t % 4) * 64, 128)
                bank_wait("tensor", pb)
                t_op(lambda e, pb=pb, t=t, qt_=qt_: e.transpose(
                    out=pb[:], in_=qt_[:, t * 128:(t + 1) * 128],
                    identity=identb[:]))
                p.wait("scalar", SEM_MM, p.val(SEM_MM))
                p.wait("scalar", SEM_OUT, p.val(SEM_OUT))
                av = s_op(lambda e, ev=ev, pb=pb: e.copy(out=ev, in_=pb[:]))
                bank_mark(pb, SEM_ACT)
                p.wait("sync", SEM_ACT, av)
                dma_out(dst[t * 128:(t + 1) * 128, r * 128:(r + 1) * 128], ev)
            qfree[par] = [(SEM_MM, p.val(SEM_MM))]

    quant_weights()

    def ln_tile(src_t, dst):
        s1, s2, s3 = stat_sb[:, 0:1], stat_sb[:, 1:2], stat_sb[:, 2:3]
        t2 = bigA[:, 4096:5120]
        v_op(lambda e: e.tensor_reduce(out=s1, in_=src_t, axis=AX.X, op=OP.add))
        dv = p.val(SEM_DVE)
        p.wait("scalar", SEM_DVE, dv)
        av = s_op(lambda e: e.activation(out=t2, in_=src_t, func=AF.Square))
        p.wait("vector", SEM_ACT, av)
        v_op(lambda e: e.tensor_reduce(out=s2, in_=t2, axis=AX.X, op=OP.add))
        v_op(lambda e: e.tensor_single_scalar(out=s1, in_=s1, scalar=1.0 / DM, op=OP.mult))
        v_op(lambda e: e.tensor_single_scalar(out=s2, in_=s2, scalar=1.0 / DM, op=OP.mult))
        v_op(lambda e: e.tensor_tensor(out=s3, in0=s1, in1=s1, op=OP.mult))
        vpad()
        v_op(lambda e: e.tensor_tensor(out=s2, in0=s2, in1=s3, op=OP.subtract))
        vpad()
        v_op(lambda e: e.tensor_single_scalar(out=s2, in_=s2, scalar=EPS, op=OP.add))
        dv = p.val(SEM_DVE)
        p.wait("scalar", SEM_DVE, dv)
        av = s_op(lambda e: e.activation(out=s3, in_=s2, func=AF.Sqrt))
        p.wait("vector", SEM_ACT, av)
        v_op(lambda e: e.reciprocal(out=s2, in_=s3))
        vpad()
        v_op(lambda e: e.tensor_scalar(out=dst, in0=src_t, scalar1=s1, scalar2=s2,
                                       op0=OP.subtract, op1=OP.mult))

    def rms_rstd(src_t, rstd_col):
        s2, s3 = stat_sb[:, 4:5], stat_sb[:, 5:6]
        t2 = bigA[:, 4096:5120]
        av = s_op(lambda e: e.activation(out=t2, in_=src_t, func=AF.Square))
        p.wait("vector", SEM_ACT, av)
        v_op(lambda e: e.tensor_reduce(out=s2, in_=t2, axis=AX.X, op=OP.add))
        vpad()
        v_op(lambda e: e.tensor_scalar(out=s2, in0=s2, scalar1=1.0 / DM, scalar2=EPS,
                                       op0=OP.mult, op1=OP.add))
        dv = p.val(SEM_DVE)
        p.wait("scalar", SEM_DVE, dv)
        av = s_op(lambda e: e.activation(out=s3, in_=s2, func=AF.Sqrt))
        p.wait("vector", SEM_ACT, av)
        v_op(lambda e, r=rstd_col: e.reciprocal(out=r, in_=s3))
        vpad()

    NCH = [512, 512, 512, 512, 512, 256]

    def ffn_call(dump=False):
        p.wait("sync", SEM_OUT, p.val(SEM_OUT))
        for th in range(2):
            for tt in range(8):
                ti = th * 8 + tt
                src_t = stream[:, ti * DM:(ti + 1) * DM]
                xln = bigA[:, 5120:6144]
                ln_tile(src_t, xln)
                xq = bigB[:, 8192:9216]
                p.wait("vector", SEM_MM, p.val(SEM_MM))
                if dump:
                    dvx = p.val(SEM_DVE)
                    p.wait("sync", SEM_DVE, dvx)
                    dma_out(xln_dbg[ti], xln)
                quant(xq, xln, DM, 0, 9216)
                dv = p.val(SEM_DVE)
                p.wait("tensor", SEM_DVE, dv)
                if dump:
                    p.wait("sync", SEM_DVE, dv)
                    dma_out(xq_dbg[ti], xq)
                transpose_tiles(xq, 8,
                                lambda t, tt=tt: bigB[:, t * 1024 + tt * 128:
                                                      t * 1024 + tt * 128 + 128])
            xqT = bigB[:, 0:8192].rearrange("p (a b) -> p a b", b=1024)
            avq = p.val(SEM_ACT)

            hq_base = 8192
            for ci, ch in enumerate(NCH):
                c0 = sum(NCH[:ci])
                wgc = bigA[:, 7168:7168 + 2048].bitcast(BF16)[:, 0:8 * ch] \
                    .rearrange("p (a b) -> p a b", b=ch)
                wuc = bigA[:, 9216:9216 + 2048].bitcast(BF16)[:, 0:8 * ch] \
                    .rearrange("p (a b) -> p a b", b=ch)
                p.wait("sync", SEM_MM, p.val(SEM_MM))
                dma_in(wgc, wgT_q[:, c0:c0 + ch].rearrange("(a p) b -> p a b", p=128))
                l2 = dma_in(wuc, wuT_q[:, c0:c0 + ch].rearrange("(a p) b -> p a b", p=128))
                p.wait("tensor", SEM_IN, l2)
                p.wait("tensor", SEM_ACT, avq)
                for tt in range(8):
                    pg, pu = ps[2 * (tt % 2)], ps[2 * (tt % 2) + 1]
                    bank_wait("tensor", pg)
                    for k in range(8):
                        t_op(lambda e, k=k, tt=tt, ch=ch, pg=pg, wgc=wgc: e.matmul(
                            pg[:, 0:ch], xqT[:, k, tt * 128:(tt + 1) * 128],
                            wgc[:, k, :], start=(k == 0), stop=(k == 7)))
                    bank_wait("tensor", pu)
                    for k in range(8):
                        t_op(lambda e, k=k, tt=tt, ch=ch, pu=pu, wuc=wuc: e.matmul(
                            pu[:, 0:ch], xqT[:, k, tt * 128:(tt + 1) * 128],
                            wuc[:, k, :], start=(k == 0), stop=(k == 7)))
                    mmv = p.val(SEM_MM)
                    p.wait("scalar", SEM_MM, mmv)
                    hsil = bigA[:, 6144:6656]
                    hch = bigA[:, 6656:7168]
                    p.wait("scalar", SEM_DVE, p.val(SEM_DVE))
                    av = s_op(lambda e, pg=pg, ch=ch, hsil=hsil: e.activation(
                        out=hsil[:, 0:ch], in_=pg[:, 0:ch], func=AF.Silu))
                    bank_mark(pg, SEM_ACT)
                    p.wait("vector", SEM_ACT, av)
                    v_op(lambda e, pu=pu, ch=ch, hsil=hsil, hch=hch: e.tensor_tensor(
                        out=hch[:, 0:ch], in0=hsil[:, 0:ch], in1=pu[:, 0:ch],
                        op=OP.mult))
                    bank_mark(pu, SEM_DVE)
                    if dump:
                        dvh = p.val(SEM_DVE)
                        p.wait("sync", SEM_DVE, dvh)
                        dma_out(h_dbg[th * 8 + tt][:, c0:c0 + ch], hch[:, 0:ch])
                    hqc = bigB[:, 30720:30720 + ch]
                    p.wait("vector", SEM_MM, p.val(SEM_MM))
                    quant(hqc, hch[:, 0:ch], ch, 0, 31232)
                    dv = p.val(SEM_DVE)
                    p.wait("tensor", SEM_DVE, dv)
                    transpose_tiles(hqc, ch // 128,
                                    lambda t, c0=c0, tt=tt: bigB[
                                        :, hq_base + (c0 // 128 + t) * 1024 + tt * 128:
                                        hq_base + (c0 // 128 + t) * 1024 + tt * 128 + 128])
            hqT = bigB[:, hq_base:hq_base + 22528].rearrange("p (a b) -> p a b", b=1024)
            avh = p.val(SEM_ACT)

            for dc in range(4):
                wdc = wb_bf16(0, 22 * 256).rearrange("p (a b) -> p a b", b=256)
                p.wait("sync", SEM_MM, p.val(SEM_MM))
                ld = dma_in(wdc, wdT_q[:, dc * 256:(dc + 1) * 256]
                            .rearrange("(a p) b -> p a b", p=128))
                p.wait("tensor", SEM_IN, ld)
                p.wait("tensor", SEM_ACT, avh)
                for tt in range(8):
                    ti = th * 8 + tt
                    pb = ps[2 + (tt % 2)]
                    bank_wait("tensor", pb)
                    for k in range(22):
                        t_op(lambda e, k=k, tt=tt, pb=pb, wdc=wdc: e.matmul(
                            pb[:, 0:256], hqT[:, k, tt * 128:(tt + 1) * 128],
                            wdc[:, k, :], start=(k == 0), stop=(k == 21)))
                    mmv = p.val(SEM_MM)
                    p.wait("vector", SEM_MM, mmv)
                    dst = stream[:, ti * DM + dc * 256:ti * DM + (dc + 1) * 256]
                    v_op(lambda e, pb=pb, dst=dst: e.scalar_tensor_tensor(
                        out=dst, in0=pb[:, 0:256], scalar=0.5, in1=dst,
                        op0=OP.mult, op1=OP.add))
                    bank_mark(pb, SEM_DVE)

    def mamba_layer(li):
        xn = bigB[:, 0:16384].rearrange("p (a b) -> p a b", b=DM)
        rstd = stat_sb[:, 6:7]
        p.wait("scalar", SEM_DVE, p.val(SEM_DVE))
        p.wait("sync", SEM_DVE, p.val(SEM_DVE))
        for i in range(NT):
            src_t = stream[:, i * DM:(i + 1) * DM]
            rms_rstd(src_t, rstd)
            v_op(lambda e, s=src_t, i=i: e.tensor_single_scalar(
                out=xn[:, i, :], in_=s, scalar=rstd, op=OP.mult))
            dma_out(h_sp[i], src_t)
        hspv = p.val(SEM_OUT)
        dv = p.val(SEM_DVE)
        p.wait("tensor", SEM_DVE, dv)
        p.wait("scalar", SEM_OUT, hspv)

        xnT = stream_b[:, 0:16384].rearrange("p (a b) -> p a b", b=2048)
        for i in range(NT):
            transpose_tiles(xn[:, i, :], 8,
                            lambda t, i=i: xnT[:, t, i * 128:(i + 1) * 128])
        avT = p.val(SEM_ACT)
        p.wait("tensor", SEM_ACT, avT)

        xs = bigB[:, 0:16 * 2048].rearrange("p (a b) -> p a b", b=2048)
        wic_f = bigA[:, 0:1024].rearrange("p (a b) -> p a b", b=128)
        wic = wb_bf16(2048, 8 * 128).rearrange("p (a b) -> p a b", b=128)
        rtmp = bigA[:, 1024:1536]
        for mt in range(32):
            p.wait("sync", SEM_MM, p.val(SEM_MM))
            ld = dma_in(wic_f, wiT_h[li][:, mt * 128:(mt + 1) * 128]
                        .rearrange("(a p) b -> p a b", p=128))
            p.wait("vector", SEM_IN, ld)
            v_op(lambda e: e.tensor_tensor(
                out=wic, in0=wic_f,
                in1=nw_sb[:, li * 8:(li + 1) * 8].rearrange("p (a o) -> p a o", o=1)
                    .broadcast_to([128, 8, 128]),
                op=OP.mult))
            cv = p.val(SEM_DVE)
            p.wait("tensor", SEM_DVE, cv)
            for tb in range(NTB):
                pb = ps[tb % 2]
                bank_wait("tensor", pb)
                for k in range(8):
                    t_op(lambda e, k=k, tb=tb, pb=pb: e.matmul(
                        pb[:], wic[:, k, :], xnT[:, k, tb * 512:(tb + 1) * 512],
                        start=(k == 0), stop=(k == 7)))
                p.wait("scalar", SEM_MM, p.val(SEM_MM))
                bcol = bi_sb[:, li * 32 + mt:li * 32 + mt + 1]
                if mt < 16:
                    s_op(lambda e, pb=pb, mt=mt, tb=tb, bcol=bcol: e.add(
                        out=xs[:, mt, tb * 512:(tb + 1) * 512], in_=pb[:], add=bcol))
                else:
                    p.wait("scalar", SEM_OUT, p.val(SEM_OUT))
                    av = s_op(lambda e, pb=pb, bcol=bcol: e.add(
                        out=rtmp, in_=pb[:], add=bcol))
                    p.wait("sync", SEM_ACT, av)
                    dma_out(res_sp[mt - 16][:, tb * 512:(tb + 1) * 512], rtmp)
                bank_mark(pb, SEM_ACT)

        hpf = bigA[:, 2048:2048 + 64]
        xsv = p.val(SEM_ACT)
        p.wait("vector", SEM_ACT, xsv)
        for dt in range(NDT):
            v_op(lambda e, dt=dt: e.tensor_copy(
                out=hpf[:, dt * 4:dt * 4 + 2], in_=xs[:, dt, 0:2]))
            v_op(lambda e, dt=dt: e.tensor_copy(
                out=hpf[:, dt * 4 + 2:dt * 4 + 4], in_=xs[:, dt, 2046:2048]))
        hv = p.val(SEM_DVE)
        p.wait("sync", SEM_DVE, hv)
        gv = dma_out(gin_h, hpf)
        p.wait("gpsimd", SEM_OUT, gv)
        gp_slack(2)
        p.op("gpsimd", lambda e: e.collective_compute(
            "AllGather", OP.bypass, replica_groups=[list(range(NCORES))],
            ins=[gin_h], outs=[gout_h]), SEM_CC)
        p.wait("sync", SEM_CC, p.val(SEM_CC))
        dma_slack(2)
        gl = bigA[:, 2112:2112 + 512].rearrange("p (a b) -> p a b", b=64)
        lg = dma_in(gl, gout_h.rearrange("c p f -> p c f"))
        p.wait("vector", SEM_IN, lg)
        v_op(lambda e: e.memset(halo_sb[:], 0.0))
        vpad()
        vpad()
        hl4 = halo_sb.rearrange("p (a b) -> p a b", b=4)
        for s in range(NCORES):
            v_op(lambda e, s=s: e.scalar_tensor_tensor(
                out=hl4[:, :, 0:2],
                in0=gl[:, s, :].rearrange("p (a b) -> p a b", b=4)[:, :, 2:4],
                scalar=oh_sb[:, 8 + s:9 + s],
                in1=hl4[:, :, 0:2],
                op0=OP.mult, op1=OP.add))
            v_op(lambda e, s=s: e.scalar_tensor_tensor(
                out=hl4[:, :, 2:4],
                in0=gl[:, s, :].rearrange("p (a b) -> p a b", b=4)[:, :, 0:2],
                scalar=oh_sb[:, 16 + s:17 + s],
                in1=hl4[:, :, 2:4],
                op0=OP.mult, op1=OP.add))
        halo_b = wb_bf16(1280, NDT * 4).rearrange("p (a b) -> p a b", b=4)
        vpad()
        vpad()
        v_op(lambda e: e.tensor_copy(out=halo_b.rearrange("p a b -> p (a b)"),
                                     in_=halo_sb[:]))
        hv2 = p.val(SEM_DVE)
        p.wait("tensor", SEM_DVE, hv2)

        ub = bigA[:, 0:4096].bitcast(BF16).rearrange("p (a b) -> p a b", b=512)
        uf = bigA[:, 4096:4608]
        r_t = bigA[:, 4608:5120]
        m_t = bigA[:, 5120:5632]
        e_t = bigA[:, 5632:6144]
        d_t = bigA[:, 6144:6656]
        w_t = bigA[:, 6656:7168]
        y_t = bigA[:, 7168:7680]
        ur_t = bigA[:, 7680:8192]
        diag = wb_bf16(0, 5 * 128).rearrange("p (a b) -> p a b", b=128)
        wxc_f = wbuf[:, 1536:1536 + 1536].rearrange("p (a b) -> p a b", b=96)
        wxc = wb_bf16(512, 16 * 96).rearrange("p (a b) -> p a b", b=96)
        ldx = dma_in(wxc_f, wxT_h[li].rearrange("(a p) b -> p a b", p=128))
        p.wait("vector", SEM_IN, ldx)
        v_op(lambda e: e.tensor_copy(out=wxc.rearrange("p a b -> p (a b)"),
                                     in_=wxc_f.rearrange("p a b -> p (a b)")))
        wxv = p.val(SEM_DVE)
        p.wait("tensor", SEM_DVE, wxv)
        xdbl = stream[0:96, 8192:10240]
        dtwb = [wbuf[0:64, 1536:1536 + 128], wbuf[0:64, 1664:1664 + 128]]
        dtw_ld = [0, 0]
        dtw_mm = [0, 0]
        cdT = scol[:, 768:784]
        for tb in range(NTB):
            for dt in range(NDT):
                p.wait("vector", SEM_MM, p.val(SEM_MM))
                for j in range(5):
                    v_op(lambda e, j=j, dt=dt: e.tensor_single_scalar(
                        out=diag[:, j, :], in_=identf[:],
                        scalar=cw_sb[:, (li * NDT + dt) * 5 + j:
                                     (li * NDT + dt) * 5 + j + 1], op=OP.mult))
                dv = p.val(SEM_DVE)
                p.wait("tensor", SEM_DVE, dv)
                pb = ps[dt % 2]
                bank_wait("tensor", pb)
                segs = []
                for j in range(5):
                    lo = tb * 512 + j - 2
                    hi = lo + 512
                    if lo < 0:
                        nh = -lo
                        segs.append((j, pb[:, 0:nh], halo_b[:, dt, 2 - nh:2]))
                        segs.append((j, pb[:, nh:512], xs[:, dt, 0:512 - nh]))
                    elif hi > 2048:
                        nh = hi - 2048
                        segs.append((j, pb[:, 0:512 - nh], xs[:, dt, lo:2048]))
                        segs.append((j, pb[:, 512 - nh:512], halo_b[:, dt, 2:2 + nh]))
                    else:
                        segs.append((j, pb[:], xs[:, dt, lo:lo + 512]))
                for si, (j, pslice, mov) in enumerate(segs):
                    t_op(lambda e, j=j, pslice=pslice, mov=mov, si=si,
                         last=(si == len(segs) - 1): e.matmul(
                        pslice, diag[:, j, :], mov, start=(si == 0), stop=last))
                p.wait("scalar", SEM_MM, p.val(SEM_MM))
                ccol = cb_sb[:, li * NDT + dt:li * NDT + dt + 1]
                p.wait("scalar", SEM_OUT, p.val(SEM_OUT))
                p.wait("scalar", SEM_DVE, p.val(SEM_DVE))
                av = s_op(lambda e, pb=pb, ccol=ccol: e.activation(
                    out=uf, in_=pb[:], func=AF.Silu, bias=ccol))
                bank_mark(pb, SEM_ACT)
                p.wait("vector", SEM_ACT, av)
                v_op(lambda e, dt=dt: e.tensor_copy(out=ub[:, dt, :], in_=uf))
                p.wait("sync", SEM_ACT, av)
                dma_out(u_sp[dt][:, tb * 512:(tb + 1) * 512], uf)
            p.wait("sync", SEM_OUT, p.val(SEM_OUT))
            dv = p.val(SEM_DVE)
            p.wait("tensor", SEM_DVE, dv)
            pbx = ps[2]
            bank_wait("tensor", pbx)
            for k in range(NDT):
                t_op(lambda e, k=k, pbx=pbx: e.matmul(
                    pbx[0:96, :], wxc[:, k, :], ub[:, k, :],
                    start=(k == 0), stop=(k == NDT - 1)))
            mv = p.val(SEM_MM)
            p.wait("scalar", SEM_MM, mv)
            av = s_op(lambda e, pbx=pbx, tb=tb: e.copy(
                out=xdbl[:, tb * 512:(tb + 1) * 512], in_=pbx[0:96, :]))
            bank_mark(pbx, SEM_ACT)
            p.wait("tensor", SEM_ACT, av)
            p.wait("sync", SEM_MM, p.val(SEM_MM))
            dtw_ld[0] = dma_in(dtwb[0], wdtT_h[li][:, 0:128])
            for mt in range(NDT):
                wpar = mt % 2
                if mt + 1 < NDT:
                    p.wait("sync", SEM_MM, dtw_mm[1 - wpar])
                    dtw_ld[1 - wpar] = dma_in(
                        dtwb[1 - wpar], wdtT_h[li][:, (mt + 1) * 128:(mt + 2) * 128])
                p.wait("sync", SEM_DVE, p.val(SEM_DVE))
                dma_in(ur_t, u_sp[mt][:, tb * 512:(tb + 1) * 512])
                p.wait("tensor", SEM_IN, dtw_ld[wpar])
                pb2 = ps[3]
                bank_wait("tensor", pb2)
                t_op(lambda e, tb=tb, pb2=pb2, wpar=wpar: e.matmul(
                    pb2[:], dtwb[wpar], xdbl[0:64, tb * 512:(tb + 1) * 512],
                    start=True, stop=True))
                dtw_mm[wpar] = p.val(SEM_MM)
                mv = p.val(SEM_MM)
                p.wait("scalar", SEM_MM, mv)
                p.wait("vector", SEM_MM, mv)
                bcol = dtb_sb[:, li * NDT + mt:li * NDT + mt + 1]
                p.wait("scalar", SEM_OUT, p.val(SEM_OUT))
                p.wait("vector", SEM_OUT, p.val(SEM_OUT))
                av = s_op(lambda e, pb2=pb2, bcol=bcol: e.activation(
                    out=r_t, in_=pb2[:], func=AF.Relu, bias=bcol))
                p.wait("vector", SEM_ACT, p.val(SEM_ACT))
                v_op(lambda e, pb2=pb2: e.scalar_tensor_tensor(
                    out=m_t, in0=r_t, scalar=2.0, in1=pb2[:],
                    op0=OP.mult, op1=OP.subtract))
                v_op(lambda e, bcol=bcol: e.tensor_single_scalar(
                    out=m_t, in_=m_t, scalar=bcol, op=OP.subtract))
                bank_mark(pb2, SEM_DVE)
                dv = p.val(SEM_DVE)
                p.wait("scalar", SEM_DVE, dv)
                av = s_op(lambda e: e.activation(out=e_t, in_=m_t, func=AF.Exp,
                                                 scale=-1.0))
                p.wait("vector", SEM_ACT, av)
                v_op(lambda e: e.tensor_single_scalar(out=e_t, in_=e_t, scalar=1.0,
                                                      op=OP.add))
                dv = p.val(SEM_DVE)
                p.wait("scalar", SEM_DVE, dv)
                av = s_op(lambda e: e.activation(out=m_t, in_=e_t, func=AF.Ln))
                p.wait("vector", SEM_ACT, av)
                p.wait("vector", SEM_IN, p.val(SEM_IN))
                v_op(lambda e: e.tensor_tensor(out=d_t, in0=r_t, in1=m_t, op=OP.add))
                v_op(lambda e: e.tensor_reduce(out=stat_sb[:, 8:9], in_=d_t,
                                               axis=AX.X, op=OP.add))
                vpad()
                if tb == 0:
                    v_op(lambda e, mt=mt: e.tensor_copy(out=cdT[:, mt:mt + 1],
                                                        in_=stat_sb[:, 8:9]))
                else:
                    v_op(lambda e, mt=mt: e.tensor_tensor(
                        out=cdT[:, mt:mt + 1], in0=cdT[:, mt:mt + 1],
                        in1=stat_sb[:, 8:9], op=OP.add))
                v_op(lambda e: e.tensor_tensor(out=w_t, in0=d_t,
                                                   in1=ur_t, op=OP.mult))
                dcol = d_sb[:, li * NDT + mt:li * NDT + mt + 1]
                v_op(lambda e, dcol=dcol: e.tensor_single_scalar(
                    out=y_t, in_=ur_t, scalar=dcol, op=OP.mult))
                dv = p.val(SEM_DVE)
                p.wait("sync", SEM_DVE, dv)
                dma_out(del_sp[mt][:, tb * 512:(tb + 1) * 512], d_t)
                dma_out(w_sp[mt][:, tb * 512:(tb + 1) * 512], w_t)
                dma_out(yd_sp[mt][:, tb * 512:(tb + 1) * 512], y_t)
        p.wait("sync", SEM_ACT, p.val(SEM_ACT))
        bcv = dma_out(bc_sp, xdbl[64:96, :])
        p.wait("sync", SEM_OUT, p.val(SEM_OUT))

        reps = stream
        bigBf = bigB.bitcast(F32)
        dltb = [bigA[:, 0:2048], bigA[:, 2048:4096]]
        wtb = [bigA[:, 4096:6144], bigA[:, 6144:8192]]
        ytb = [bigA[:, 8192:10240], bigA[:, 10240:12288]]
        ab = [bigBf[:, 0:2048], bigBf[:, 2048:4096]]
        zb = [bigBf[:, 4096:6144], bigBf[:, 6144:8192]]
        hb = [bigBf[:, 8192:10240], bigBf[:, 10240:12288]]
        tb_ = [bigBf[:, 12288:14336], bigBf[:, 14336:16384]]
        Mv = scol[:, 0:256]
        hend = scol[:, 256:512]
        hin = scol[:, 512:768]
        tc = scol[:, 800:1056]
        ld_free = {0: [], 1: []}
        ab_free = {0: [], 1: []}
        zb_free = {0: [], 1: []}
        for ng in range(4):
            p.wait("sync", SEM_GP, p.val(SEM_GP))
            p.wait("sync", SEM_DVE, p.val(SEM_DVE))
            p.wait("sync", SEM_MM, p.val(SEM_MM))
            p.wait("sync", SEM_ACT, p.val(SEM_ACT))
            p.wait("sync", SEM_OUT, p.val(SEM_OUT))
            if ng == 0:
                dma_slack(1)
            rl = None
            for j in range(4):
                n = ng * 4 + j
                dma_in(reps[:, (2 * j) * 2048:(2 * j + 1) * 2048],
                       bc_sp[n:n + 1, :].partition_broadcast(128))
                rl = dma_in(reps[:, (2 * j + 1) * 2048:(2 * j + 2) * 2048],
                            bc_sp[16 + n:17 + n, :].partition_broadcast(128))
            p.wait("vector", SEM_IN, rl)
            p.wait("gpsimd", SEM_IN, rl)
            ld_done = {}
            for dt in range(NDT + 1):
                if dt < NDT:
                    par = dt % 2
                    for sem, v in ld_free[par]:
                        p.wait("sync", sem, v)
                    ld_free[par] = []
                    dma_in(dltb[par], del_sp[dt])
                    dma_in(wtb[par], w_sp[dt])
                    ysrc = yd_sp[dt] if ng == 0 else y_sp[dt]
                    ld_done[dt] = dma_in(ytb[par], ysrc)
                if dt == 0:
                    continue
                cdt = dt - 1
                par = cdt % 2
                dlt_c, wt_c, yt_c = dltb[par], wtb[par], ytb[par]
                p.wait("scalar", SEM_IN, ld_done[cdt])
                p.wait("gpsimd", SEM_IN, ld_done[cdt])
                p.wait("vector", SEM_IN, ld_done[cdt])
                for j in range(4):
                    n = ng * 4 + j
                    bpar = j % 2
                    acol = a_sb[:, (li * NDT + cdt) * NST + n:
                                (li * NDT + cdt) * NST + n + 1]
                    for sem, v in ab_free[bpar]:
                        p.wait("scalar", sem, v)
                    ab_free[bpar] = []
                    av = s_op(lambda e, acol=acol, A=ab[bpar], D=dlt_c: e.activation(
                        out=A, in_=D, func=AF.Exp, scale=acol))
                    for sem, v in zb_free[bpar]:
                        p.wait("vector", sem, v)
                    zb_free[bpar] = []
                    v_op(lambda e, j=j, Z=zb[bpar], W=wt_c: e.tensor_tensor(
                        out=Z, in0=W,
                        in1=reps[:, (2 * j) * 2048:(2 * j + 1) * 2048], op=OP.mult))
                    p.wait("vector", SEM_ACT, av)
                    v_op(lambda e, A=ab[bpar], Z=zb[bpar], H=hb[bpar]:
                         e.tensor_tensor_scan(out=H, data0=A, data1=Z, initial=0.0,
                                              op0=OP.mult, op1=OP.add))
                    ab_free[bpar].append((SEM_DVE, p.val(SEM_DVE)))
                    zb_free[bpar].append((SEM_DVE, p.val(SEM_DVE)))
                    p.wait("gpsimd", SEM_DVE, p.val(SEM_DVE))
                    gc = g_op(lambda e, j=j, H=hb[bpar], T=tb_[bpar]: e.tensor_tensor(
                        out=T[:, 1024:2048], in0=H[:, 1024:2048],
                        in1=reps[:, (2 * j + 1) * 2048 + 1024:(2 * j + 2) * 2048],
                        op=OP.mult))
                    v_op(lambda e, j=j, H=hb[bpar], T=tb_[bpar]: e.tensor_tensor(
                        out=T[:, 0:1024], in0=H[:, 0:1024],
                        in1=reps[:, (2 * j + 1) * 2048:(2 * j + 1) * 2048 + 1024],
                        op=OP.mult))
                    p.wait("vector", SEM_GP, gc)
                    v_op(lambda e, T=tb_[bpar], Y=yt_c: e.tensor_tensor(
                        out=Y, in0=Y, in1=T, op=OP.add))
                    v_op(lambda e, cdt=cdt, n=n, H=hb[bpar]: e.tensor_copy(
                        out=hend[:, cdt * 16 + n:cdt * 16 + n + 1],
                        in_=H[:, 2047:2048]))
                    ab_free[bpar].append((SEM_GP, p.val(SEM_GP)))
                dv = p.val(SEM_DVE)
                p.wait("sync", SEM_DVE, dv)
                ow = dma_out(y_sp[cdt], yt_c)
                ld_free[par] = [(SEM_ACT, p.val(SEM_ACT)),
                                (SEM_GP, p.val(SEM_GP)),
                                (SEM_OUT, ow)]

        p.wait("scalar", SEM_DVE, p.val(SEM_DVE))
        cdT = scol[:, 768:784]
        for dt in range(NDT):
            s_op(lambda e, dt=dt: e.activation(
                out=Mv[:, dt * 16:(dt + 1) * 16],
                in_=a_sb[:, (li * NDT + dt) * NST:(li * NDT + dt + 1) * NST],
                func=AF.Exp, scale=cdT[:, dt:dt + 1]))
        sv = p.val(SEM_ACT)
        p.wait("sync", SEM_ACT, sv)
        p.wait("sync", SEM_DVE, p.val(SEM_DVE))
        dma_out(gin_s[:, 0:256], Mv)
        g2 = dma_out(gin_s[:, 256:512], hend)
        p.wait("gpsimd", SEM_OUT, g2)
        gp_slack(2)
        p.op("gpsimd", lambda e: e.collective_compute(
            "AllGather", OP.bypass, replica_groups=[list(range(NCORES))],
            ins=[gin_s], outs=[gout_s]), SEM_CC)
        p.wait("sync", SEM_CC, p.val(SEM_CC))
        p.wait("sync", SEM_GP, p.val(SEM_GP))
        dma_slack(2)
        gat = bigBf[:, 14336 - 4096 + 4096:16384]
        gat = bigA[:, 0:4096].rearrange("p (a b) -> p a b", b=512)
        lg = dma_in(gat, gout_s.rearrange("c p f -> p c f"))
        p.wait("vector", SEM_IN, lg)
        v_op(lambda e: e.memset(hin, 0.0))
        v_op(lambda e: e.memset(tc, 0.0))
        vpad()
        vpad()
        for sc_ in range(NCORES):
            if sc_ % 4 == 0:
                v_op(lambda e: e.memset(tc, 0.0))
                vpad()
                vpad()
            else:
                v_op(lambda e, sc_=sc_: e.tensor_tensor(
                    out=tc, in0=tc, in1=gat[:, sc_ - 1, 0:256], op=OP.mult))
                v_op(lambda e, sc_=sc_: e.tensor_tensor(
                    out=tc, in0=tc, in1=gat[:, sc_ - 1, 256:512], op=OP.add))
                v_op(lambda e, sc_=sc_: e.scalar_tensor_tensor(
                    out=hin, in0=tc, scalar=oh_sb[:, sc_:sc_ + 1], in1=hin,
                    op0=OP.mult, op1=OP.add))

        zeros_t = wtb[0]
        p.wait("vector", SEM_IN, p.val(SEM_IN))
        p.wait("vector", SEM_GP, p.val(SEM_GP))
        v_op(lambda e: e.memset(zeros_t, 0.0))
        ld_free = {0: [], 1: []}
        gb_free = {0: [], 1: []}
        for ng in range(4):
            p.wait("sync", SEM_DVE, p.val(SEM_DVE))
            p.wait("sync", SEM_OUT, p.val(SEM_OUT))
            if ng == 0:
                dma_slack(1)
            rl = None
            for j in range(4):
                n = ng * 4 + j
                rl = dma_in(reps[:, (2 * j + 1) * 2048:(2 * j + 2) * 2048],
                            bc_sp[16 + n:17 + n, :].partition_broadcast(128))
            p.wait("vector", SEM_IN, rl)
            ld_done = {}
            for dt in range(NDT + 1):
                if dt < NDT:
                    par = dt % 2
                    for sem, v in ld_free[par]:
                        p.wait("sync", sem, v)
                    ld_free[par] = []
                    dma_in(dltb[par], del_sp[dt])
                    ld_done[dt] = dma_in(ytb[par], y_sp[dt])
                if dt == 0:
                    continue
                cdt = dt - 1
                par = cdt % 2
                dlt_c, yt_c = dltb[par], ytb[par]
                cd_c = hb[par]
                p.wait("vector", SEM_IN, ld_done[cdt])
                p.wait("vector", SEM_ACT, p.val(SEM_ACT))
                v_op(lambda e, D=dlt_c, CD=cd_c: e.tensor_tensor_scan(
                    out=CD, data0=D, data1=zeros_t, initial=0.0,
                    op0=OP.add, op1=OP.add))
                dvc = p.val(SEM_DVE)
                p.wait("scalar", SEM_DVE, dvc)
                for j in range(4):
                    n = ng * 4 + j
                    bpar = j % 2
                    acol = a_sb[:, (li * NDT + cdt) * NST + n:
                                (li * NDT + cdt) * NST + n + 1]
                    for sem, v in gb_free[bpar]:
                        p.wait("scalar", sem, v)
                    gb_free[bpar] = []
                    av = s_op(lambda e, acol=acol, G=ab[bpar], CD=cd_c: e.activation(
                        out=G, in_=CD, func=AF.Exp, scale=acol))
                    p.wait("vector", SEM_ACT, av)
                    hcol = hin[:, cdt * 16 + n:cdt * 16 + n + 1]
                    v_op(lambda e, j=j, hcol=hcol, G=ab[bpar], T=tb_[bpar]:
                         e.scalar_tensor_tensor(
                        out=T, in0=G, scalar=hcol,
                        in1=reps[:, (2 * j + 1) * 2048:(2 * j + 2) * 2048],
                        op0=OP.mult, op1=OP.mult))
                    gb_free[bpar].append((SEM_DVE, p.val(SEM_DVE)))
                    v_op(lambda e, T=tb_[bpar], Y=yt_c: e.tensor_tensor(
                        out=Y, in0=Y, in1=T, op=OP.add))
                dv = p.val(SEM_DVE)
                p.wait("sync", SEM_DVE, dv)
                ow = dma_out(y_sp[cdt], yt_c)
                ld_free[par] = [(SEM_DVE, dv), (SEM_OUT, ow)]

        p.wait("sync", SEM_OUT, p.val(SEM_OUT))
        p.wait("sync", SEM_ACT, p.val(SEM_ACT))
        p.wait("sync", SEM_DVE, p.val(SEM_DVE))
        p.wait("scalar", SEM_DVE, p.val(SEM_DVE))
        gld = {}
        gfree = {0: [], 1: []}
        for dt in range(NDT + 1):
            if dt < NDT:
                par = dt % 2
                for sem, v in gfree[par]:
                    p.wait("sync", sem, v)
                gfree[par] = []
                dma_in(ytb[par], y_sp[dt])
                gld[dt] = dma_in(dltb[par], res_sp[dt])
            if dt == 0:
                continue
            cdt = dt - 1
            par = cdt % 2
            p.wait("scalar", SEM_IN, gld[cdt])
            av = s_op(lambda e, R=dltb[par], A=wtb[par]: e.activation(
                out=A, in_=R, func=AF.Silu))
            p.wait("vector", SEM_ACT, av)
            p.wait("vector", SEM_IN, gld[cdt])
            v_op(lambda e, cdt=cdt, Y=ytb[par], A=wtb[par]: e.tensor_tensor(
                out=bigB[:, cdt * 2048:(cdt + 1) * 2048], in0=Y, in1=A,
                op=OP.mult))
            gfree[par] = [(SEM_DVE, p.val(SEM_DVE)), (SEM_ACT, p.val(SEM_ACT))]
        y_g = bigB[:, 0:16 * 2048].rearrange("p (a b) -> p a b", b=2048)
        ygv = p.val(SEM_DVE)
        p.wait("tensor", SEM_DVE, ygv)

        woc = bigA[:, 0:8192].bitcast(BF16).rearrange("p (a b) -> p a b", b=1024)
        wof = bigA[:, 8192:10240]
        for dt2 in range(8):
            p.wait("sync", SEM_DVE, p.val(SEM_DVE))
            ld = dma_in(wof.rearrange("p (a b) -> p a b", b=DM),
                        woT_h[li][dt2 * 256:(dt2 + 1) * 256, :]
                        .rearrange("(a p) b -> p a b", p=128))
            p.wait("vector", SEM_IN, ld)
            v_op(lambda e, dt2=dt2: e.tensor_copy(
                out=woc[:, 2 * dt2:2 * dt2 + 2, :].rearrange("p a b -> p (a b)"),
                in_=wof))
        wov = p.val(SEM_DVE)
        p.wait("tensor", SEM_DVE, wov)
        p.wait("sync", SEM_OUT, p.val(SEM_OUT))
        for i in range(NT):
            p.wait("sync", SEM_DVE, p.val(SEM_DVE))
            hld = dma_in(wof[:, 0:1024], h_sp[i])
            for c in range(2):
                pb = ps[c]
                bank_wait("tensor", pb)
                for k in range(NDT):
                    t_op(lambda e, k=k, i=i, c=c, pb=pb: e.matmul(
                        pb[:], y_g[:, k, i * 128:(i + 1) * 128],
                        woc[:, k, c * 512:(c + 1) * 512],
                        start=(k == 0), stop=False))
                t_op(lambda e, c=c, pb=pb: e.matmul(
                    pb[:], onesr[0:1, 0:128],
                    bo_sb[0:1, li * DM + c * 512:li * DM + (c + 1) * 512],
                    start=False, stop=True))
            mv = p.val(SEM_MM)
            p.wait("vector", SEM_MM, mv)
            p.wait("vector", SEM_IN, hld)
            for c in range(2):
                pb = ps[c]
                v_op(lambda e, i=i, c=c, pb=pb: e.tensor_tensor(
                    out=stream[:, i * DM + c * 512:i * DM + (c + 1) * 512],
                    in0=wof[:, c * 512:(c + 1) * 512], in1=pb[:], op=OP.add))
                bank_mark(pb, SEM_DVE)
        if dbg:
            p.wait("sync", SEM_DVE, p.val(SEM_DVE))
            for i2 in range(NT):
                dma_out(xln_dbg[i2], stream[:, i2 * DM:(i2 + 1) * DM])
            p.wait("gpsimd", SEM_OUT, p.val(SEM_OUT))
            p.wait("vector", SEM_OUT, p.val(SEM_OUT))
            p.wait("scalar", SEM_OUT, p.val(SEM_OUT))

    ffn_call(dump=dbg)
    p.wait("sync", SEM_DVE, p.val(SEM_DVE))
    for i in range(NT):
        src_t = stream[:, i * DM:(i + 1) * DM]
        xv = dma_out(x_sp[i], src_t)
        xln = bigA[:, 5120:6144]
        ln_tile(src_t, xln)
        p.wait("vector", SEM_OUT, xv)
        v_op(lambda e, s=src_t, xln=xln: e.tensor_copy(out=s, in_=xln))
    for li in range(nlay):
        mamba_layer(li)
    rstd2 = stat_sb[:, 7:8]
    xr_t = bigA[:, 6144:7168]
    nfw_sb = bigA[:, 7168:8192]
    p.wait("sync", SEM_DVE, p.val(SEM_DVE))
    p.wait("sync", SEM_ACT, p.val(SEM_ACT))
    p.wait("sync", SEM_OUT, p.val(SEM_OUT))
    nfl = dma_in(nfw_sb, nfw_h.partition_broadcast(128))
    p.wait("vector", SEM_IN, nfl)
    p.wait("scalar", SEM_DVE, p.val(SEM_DVE))
    for i in range(NT):
        src_t = stream[:, i * DM:(i + 1) * DM]
        rms_rstd(src_t, rstd2)
        t1 = bigA[:, 5120:6144]
        v_op(lambda e, s=src_t, t1=t1: e.scalar_tensor_tensor(
            out=t1, in0=s, scalar=rstd2, in1=nfw_sb, op0=OP.mult, op1=OP.mult))
        p.wait("sync", SEM_DVE, p.val(SEM_DVE))
        xld = dma_in(xr_t, x_sp[i])
        p.wait("vector", SEM_IN, xld)
        vpad()
        v_op(lambda e, s=src_t, t1=t1: e.tensor_tensor(
            out=s, in0=t1, in1=xr_t, op=OP.add))
    ffn_call()
    fv = p.val(SEM_DVE)
    p.wait("sync", SEM_DVE, fv)
    ov = dma_out(out_h.rearrange("(a p) b -> p a b", p=128), stream.rearrange("p (a b) -> p a b", b=DM))
    p.wait("sync", SEM_OUT, ov)

    with nc.Block() as block:
        p.replay(block)
    return nc


_NC_CACHE = [None]


def _stage_inputs(inputs):
    x = np.asarray(inputs["L0_attn_x"], np.float32)
    ipw = np.asarray(inputs["L0_attn_in_proj_w"], np.float32)
    ipb = np.asarray(inputs["L0_attn_in_proj_b"], np.float32)
    cw = np.asarray(inputs["L0_attn_conv_w"], np.float32)
    cb = np.asarray(inputs["L0_attn_conv_b"], np.float32)
    xpw = np.asarray(inputs["L0_attn_x_proj_w"], np.float32)
    dtw = np.asarray(inputs["L0_attn_dt_proj_w"], np.float32)
    dtb = np.asarray(inputs["L0_attn_dt_proj_b"], np.float32)
    alog = np.asarray(inputs["L0_attn_A_log"], np.float32)
    dd = np.asarray(inputs["L0_attn_D"], np.float32)
    opw = np.asarray(inputs["L0_attn_out_proj_w"], np.float32)
    opb = np.asarray(inputs["L0_attn_out_proj_b"], np.float32)
    nw = np.asarray(inputs["L0_attn_norm_w"], np.float32)
    nfw = np.asarray(inputs["L0_attn_norm_f_w"], np.float32)
    wg = np.asarray(inputs["L0_ffn_w_gate"], np.float32)
    wu = np.asarray(inputs["L0_ffn_w_up"], np.float32)
    wd = np.asarray(inputs["L0_ffn_w_down"], np.float32)

    def colmap(a):
        L = a.shape[0]
        G = a.shape[1] // 128
        return np.ascontiguousarray(a.reshape(L, G, 128).transpose(2, 0, 1)
                                    .reshape(128, L * G))

    shared = {
        "wiT": np.ascontiguousarray(ipw.transpose(0, 2, 1)),
        "wxT": np.ascontiguousarray(xpw.transpose(0, 2, 1)),
        "wdtT": np.ascontiguousarray(dtw.transpose(0, 2, 1)),
        "woT": np.ascontiguousarray(opw.transpose(0, 2, 1)),
        "bi_c": colmap(ipb),
        "cw_c": np.ascontiguousarray(cw.reshape(NLAYER, NDT, 128, 5)
                                     .transpose(2, 0, 1, 3).reshape(128, -1)),
        "cb_c": colmap(cb), "dtb_c": colmap(dtb),
        "alog_c": np.ascontiguousarray(alog.reshape(NLAYER, NDT, 128, NST)
                                       .transpose(2, 0, 1, 3).reshape(128, -1)),
        "d_c": colmap(dd), "nw_c": colmap(nw),
        "nfw_r": np.ascontiguousarray(nfw.reshape(1, DM)),
        "bo_r": np.ascontiguousarray(opb.reshape(1, NLAYER * DM)),
        "wg": wg, "wu": wu, "wd": wd,
        "idm": np.eye(128, dtype=np.float32),
        "idmb": np.eye(128, dtype=np.float32).astype(ml_dtypes.bfloat16),
    }
    in_maps = []
    for c in range(NCORES):
        b, q = c // 4, c % 4
        oh = np.zeros((1, 24), np.float32)
        oh[0, c] = 1.0
        if q > 0:
            oh[0, 8 + (c - 1)] = 1.0
        if q < 3:
            oh[0, 16 + (c + 1)] = 1.0
        m = dict(shared)
        m["x"] = np.ascontiguousarray(x[b, q * TAU:(q + 1) * TAU, :])
        m["oh"] = oh
        in_maps.append(m)
    return in_maps


def kernel(**inputs):
    if _NC_CACHE[0] is None:
        _NC_CACHE[0] = build_program()
    nc = _NC_CACHE[0]
    in_maps = _stage_inputs(inputs)
    res = run_bass_kernel_spmd(nc, in_maps, core_ids=list(range(NCORES)))
    out = np.empty((2, 8192, DM), np.float32)
    for c in range(NCORES):
        b, q = c // 4, c % 4
        out[b, q * TAU:(q + 1) * TAU, :] = res.results[c]["out"]
    return out


# version 13 (final)
# speedup vs baseline: 1.2613x; 1.0089x over previous
import numpy as np
import ml_dtypes
import concourse.bass as bass
import concourse.mybir as mybir
from concourse.bass_utils import run_bass_kernel_spmd

F32 = mybir.dt.float32
BF16 = mybir.dt.bfloat16
AF = mybir.ActivationFunctionType
OP = mybir.AluOpType
AX = mybir.AxisListType

NCORES = 8
TAU = 2048
NT = 16
DM = 1024
DI = 2048
NDT = 16
NST = 16
DTR = 64
II = 2816
NIT = 22
NLAYER = 2
NTB = 4
MIDS = [0.25, 0.75, 1.25, 1.75, 2.5, 3.5, 5.0]
EPS = 1e-5


class Prog:
    ENGINES = ("sync", "vector", "scalar", "tensor", "gpsimd")

    def __init__(self, nc):
        self.nc = nc
        self.q = {e: [] for e in self.ENGINES}
        self.sems = {}
        self.cnt = {}

    def sem(self, name):
        if name not in self.sems:
            self.sems[name] = self.nc.alloc_semaphore(name=name)
            self.cnt[name] = 0
        return self.sems[name]

    def op(self, eng, fn, sem=None, amt=1):
        s = self.sem(sem) if sem else None
        if sem:
            self.cnt[sem] += amt

        def run(e, fn=fn, s=s, amt=amt):
            ins = fn(e)
            if s is not None:
                ins.then_inc(s, amt)
        self.q[eng].append(run)
        return self.cnt[sem] if sem else None

    def wait(self, eng, sem, val=None):
        s = self.sem(sem)
        v = self.cnt[sem] if val is None else val

        def run(e, s=s, v=v):
            e.wait_ge(s, v)
        self.q[eng].append(run)

    def val(self, sem):
        return self.cnt.get(sem, 0)

    def replay(self, block):
        for name in self.ENGINES:
            dec = getattr(block, name)
            ops = self.q[name]

            def body(e, ops=ops):
                for f in ops:
                    f(e)
            dec(body)


SEM_IN, SEM_MM, SEM_ACT, SEM_DVE, SEM_GP, SEM_OUT, SEM_CC = (
    "s_in", "s_mm", "s_act", "s_dve", "s_gp", "s_out", "s_cc")


def build_program(dbg=False, nlay=NLAYER):
    nc = bass.Bass("TRN2", detect_race_conditions=False, target_bir_lowering=False, debug=False)
    nc.num_devices = NCORES
    p = Prog(nc)

    def din(name, shape, dtype=F32):
        return nc.dram_tensor(name, shape, dtype, kind="ExternalInput").ap()

    x_in = din("x", [TAU, DM])
    wiT_h = din("wiT", [NLAYER, DM, 2 * DI])
    wxT_h = din("wxT", [NLAYER, DI, 96])
    wdtT_h = din("wdtT", [NLAYER, DTR, DI])
    woT_h = din("woT", [NLAYER, DI, DM])
    bi_h = din("bi_c", [128, NLAYER * 32])
    cw_h = din("cw_c", [128, NLAYER * NDT * 5])
    cb_h = din("cb_c", [128, NLAYER * NDT])
    dtb_h = din("dtb_c", [128, NLAYER * NDT])
    alog_h = din("alog_c", [128, NLAYER * NDT * NST])
    dd_h = din("d_c", [128, NLAYER * NDT])
    nw_h = din("nw_c", [128, NLAYER * 8])
    nfw_h = din("nfw_r", [1, DM])
    bo_h = din("bo_r", [1, NLAYER * DM])
    wg_h = din("wg", [II, DM])
    wu_h = din("wu", [II, DM])
    wd_h = din("wd", [DM, II])
    oh_h = din("oh", [1, 24])
    idm_h = din("idm", [128, 128])
    idmb_h = din("idmb", [128, 128], BF16)

    out_h = nc.dram_tensor("out", [TAU, DM], F32, kind="ExternalOutput").ap()

    def scr(name, shape, dtype=F32, force_internal=False):
        kind = "ExternalOutput" if (dbg and not force_internal) else "Internal"
        return nc.dram_tensor(name, shape, dtype, kind=kind).ap()

    wgT_q = scr("wgT_q", [DM, II], BF16)
    wuT_q = scr("wuT_q", [DM, II], BF16)
    wdT_q = scr("wdT_q", [II, DM], BF16)
    x_sp = scr("x_sp", [NT, 128, DM])
    h_sp = scr("h_sp", [NT, 128, DM])
    res_sp = scr("res_sp", [NDT, 128, TAU])
    del_sp = scr("del_sp", [NDT, 128, TAU])
    w_sp = scr("w_sp", [NDT, 128, TAU])
    yd_sp = scr("yd_sp", [NDT, 128, TAU])
    y_sp = scr("y_sp", [NDT, 128, TAU])
    u_sp = scr("u_sp", [NDT, 128, TAU])
    bc_sp = scr("bc_sp", [32, TAU])
    scan_dbg = scr("scan_dbg", [4, 128, TAU])
    xln_dbg = scr("xln_dbg", [NT, 128, DM])
    xq_dbg = scr("xq_dbg", [NT, 128, DM], BF16)
    h_dbg = scr("h_dbg", [NT, 128, II])
    gin_h = scr("gin_h", [128, NDT * 4], force_internal=True)
    gout_h = scr("gout_h", [NCORES, 128, NDT * 4], force_internal=True)
    gin_s = scr("gin_s", [128, 512], force_internal=True)
    gout_s = scr("gout_s", [NCORES, 128, 512], force_internal=True)

    def sball(name, shape, dtype=F32):
        return nc.alloc_sbuf_tensor(name, shape, dtype).ap()

    identf = sball("identf", [128, 128], F32)
    identb = sball("identb", [128, 128], BF16)
    onesr = sball("onesr", [1, 128], F32)
    bi_sb = sball("bi_sb", [128, NLAYER * 32], F32)
    cw_sb = sball("cw_sb", [128, NLAYER * NDT * 5], F32)
    cb_sb = sball("cb_sb", [128, NLAYER * NDT], F32)
    dtb_sb = sball("dtb_sb", [128, NLAYER * NDT], F32)
    a_sb = sball("a_sb", [128, NLAYER * NDT * NST], F32)
    d_sb = sball("d_sb", [128, NLAYER * NDT], F32)
    nw_sb = sball("nw_sb", [128, NLAYER * 8], F32)
    bo_sb = sball("bo_sb", [1, NLAYER * DM], F32)
    oh_sb = sball("oh_sb", [128, 24], F32)
    stat_sb = sball("stat_sb", [128, 16], F32)
    scol = sball("scol", [128, 1088], F32)
    halo_sb = sball("halo_sb", [128, NDT * 4], F32)

    stream = sball("stream", [128, NT * DM], F32)
    bigA = sball("bigA", [128, 12 * 1024], F32)
    bigB = sball("bigB", [128, 32 * 1024], BF16)
    wbuf = sball("wbuf", [128, 3 * 1024], F32)

    ps = [nc.alloc_psum_tensor(f"ps{i}", [128, 512], F32).ap() for i in range(4)]
    pst = [nc.alloc_psum_tensor(f"pst{i}", [128, 1024], BF16).ap()[:, 0:128]
           for i in range(4)]

    def wb_bf16(off_f32, n):
        return wbuf[:, off_f32:off_f32 + (n + 1) // 2].bitcast(BF16)[:, 0:n]

    stream_b = stream.bitcast(BF16)

    bank_free = {}

    def bank_wait(eng, bank):
        if id(bank) in bank_free:
            sname, v = bank_free[id(bank)]
            p.wait(eng, sname, v)

    def bank_mark(bank, sname):
        bank_free[id(bank)] = (sname, p.val(sname))

    def gp_slack(k=2):
        for _ in range(k):
            p.op("gpsimd", lambda e: e.tensor_copy(out=stat_sb[:, 13:14],
                                                   in_=identf[:, 0:1]), SEM_GP)

    def dma_slack(k=2):
        for _ in range(k):
            dma_in(identf[:], idm_h)

    def dma_in(dst, src):
        return p.op("sync", lambda e, d=dst, s=src: e.dma_start(out=d, in_=s), SEM_IN, 16)

    def dma_out(dst, src):
        return p.op("sync", lambda e, d=dst, s=src: e.dma_start(out=d, in_=s), SEM_OUT, 16)

    def v_op(fn):
        return p.op("vector", fn, SEM_DVE)

    def g_op(fn):
        return p.op("gpsimd", fn, SEM_GP)

    def s_op(fn):
        return p.op("scalar", fn, SEM_ACT)

    def t_op(fn):
        return p.op("tensor", fn, SEM_MM)

    def vpad():
        v_op(lambda e: e.tensor_copy(out=stat_sb[:, 12:13], in_=identf[:, 0:1]))

    dma_in(stream.rearrange("p (a b) -> p a b", b=DM), x_in.rearrange("(a p) b -> p a b", p=128))
    dma_in(bi_sb, bi_h)
    dma_in(cw_sb, cw_h)
    dma_in(cb_sb, cb_h)
    dma_in(dtb_sb, dtb_h)
    dma_in(a_sb, alog_h)
    dma_in(d_sb, dd_h)
    dma_in(nw_sb, nw_h)
    dma_in(bo_sb, bo_h)
    dma_in(oh_sb, oh_h.partition_broadcast(128))
    dma_in(identf[:], idm_h)
    dma_in(identb[:], idmb_h)
    setup_in = p.val(SEM_IN)

    p.wait("gpsimd", SEM_IN, setup_in)
    p.wait("vector", SEM_IN, setup_in)
    v_op(lambda e: e.memset(onesr[:], 1.0))
    p.wait("scalar", SEM_IN, setup_in)
    s_op(lambda e: e.activation(out=bigA[:, 0:NLAYER * NDT * NST], in_=a_sb[:],
                                func=AF.Exp))
    p.wait("vector", SEM_ACT, p.val(SEM_ACT))
    v_op(lambda e: e.tensor_single_scalar(out=a_sb[:],
                                          in_=bigA[:, 0:NLAYER * NDT * NST],
                                          scalar=-1.0, op=OP.mult))
    p.wait("tensor", SEM_IN, setup_in)

    def quant(dst, src, F, aoff, boff):
        nb = F // 16
        xn = bigA[:, aoff:aoff + F]
        qa = bigA[:, aoff + F:aoff + 2 * F]
        xsq = bigA[:, aoff + 2 * F:aoff + 3 * F]
        bm = bigA[:, aoff + 3 * F:aoff + 3 * F + nb]
        scl = bigA[:, aoff + 3 * F + nb:aoff + 3 * F + 2 * nb]
        rcp = bigA[:, aoff + 3 * F + 2 * nb:aoff + 3 * F + 3 * nb]
        m0 = bigB[:, boff:boff + F]
        m1 = bigB[:, boff + F:boff + 2 * F]
        m2 = bigB[:, boff + 2 * F:boff + 3 * F]
        src3 = src.rearrange("p (b k) -> p b k", k=16)
        v_op(lambda e: e.tensor_reduce(out=bm, in_=src3, axis=AX.X, op=OP.max,
                                       apply_absolute_value=True))
        vpad()
        v_op(lambda e: e.tensor_scalar(out=scl, in0=bm, scalar1=1.0 / 6.0,
                                       scalar2=1e-12, op0=OP.mult, op1=OP.max))
        vpad()
        v_op(lambda e: e.reciprocal(out=rcp, in_=scl))
        vpad()
        v_op(lambda e: e.tensor_tensor(
            out=xn.rearrange("p (b k) -> p b k", k=16), in0=src3,
            in1=rcp.rearrange("p (b o) -> p b o", o=1).broadcast_to([128, nb, 16]),
            op=OP.mult))
        dv = p.val(SEM_DVE)
        p.wait("scalar", SEM_DVE, dv)
        avq = s_op(lambda e: e.activation(out=xsq, in_=xn, func=AF.Square))

        def mk(dst_t, th):
            return lambda e: e.tensor_single_scalar(out=dst_t, in_=xsq,
                                                    scalar=th * th, op=OP.is_ge)
        p.wait("vector", SEM_ACT, avq)
        v_op(mk(m0, MIDS[0]))
        v_op(mk(m1, MIDS[1]))
        v_op(lambda e: e.tensor_tensor(out=m0, in0=m0, in1=m1, op=OP.add))
        v_op(mk(m1, MIDS[2]))
        v_op(lambda e: e.tensor_tensor(out=m0, in0=m0, in1=m1, op=OP.add))
        v_op(mk(m1, MIDS[3]))
        v_op(lambda e: e.tensor_tensor(out=m0, in0=m0, in1=m1, op=OP.add))
        v_op(mk(m2, MIDS[4]))
        v_op(mk(m1, MIDS[5]))
        v_op(lambda e: e.tensor_tensor(out=m2, in0=m2, in1=m1, op=OP.add))
        v_op(mk(m1, MIDS[6]))
        v_op(lambda e: e.scalar_tensor_tensor(out=qa, in0=m0, scalar=0.5, in1=m2,
                                              op0=OP.mult, op1=OP.add))
        v_op(lambda e: e.scalar_tensor_tensor(out=qa, in0=m1, scalar=2.0, in1=qa,
                                              op0=OP.mult, op1=OP.add))
        v_op(lambda e: e.tensor_scalar(out=m1, in0=xn, scalar1=0.0, scalar2=2.0,
                                       op0=OP.is_ge, op1=OP.mult))
        v_op(lambda e: e.tensor_tensor(out=xn, in0=qa, in1=m1, op=OP.mult))
        v_op(lambda e: e.tensor_tensor(out=qa, in0=xn, in1=qa, op=OP.subtract))
        v_op(lambda e: e.tensor_tensor(
            out=dst.rearrange("p (b k) -> p b k", k=16),
            in0=qa.rearrange("p (b k) -> p b k", k=16),
            in1=scl.rearrange("p (b o) -> p b o", o=1).broadcast_to([128, nb, 16]),
            op=OP.mult))

    def transpose_tiles(src, ntile, dst_ap_of_t):
        for t in range(ntile):
            pb = pst[t % 4]
            bank_wait("tensor", pb)
            t_op(lambda e, pb=pb, t=t: e.transpose(
                out=pb[:], in_=src[:, t * 128:(t + 1) * 128], identity=identb[:]))
            p.wait("scalar", SEM_MM, p.val(SEM_MM))
            s_op(lambda e, pb=pb, d=dst_ap_of_t(t): e.copy(out=d, in_=pb[:]))
            bank_mark(pb, SEM_ACT)

    def quant_weights():
        bigBf0 = bigB.bitcast(F32)
        wtiles = [bigA[:, 9000:9000 + 2816], bigBf0[:, 8192:8192 + 2816]]
        qtiles = [bigB[:, 0:2816], bigB[:, 2816:5632]]
        jobs = []
        for whbm, dst, rows, cols in ((wg_h, wgT_q, NIT, DM), (wu_h, wuT_q, NIT, DM),
                                      (wd_h, wdT_q, 8, II)):
            for r in range(rows):
                jobs.append((whbm, dst, r, cols))
        lds = {}
        wfree = {0: [], 1: []}
        qfree = {0: [], 1: []}
        for i in range(len(jobs) + 1):
            if i < len(jobs):
                whbm, dst, r, cols = jobs[i]
                par = i % 2
                for sem, v in wfree[par]:
                    p.wait("sync", sem, v)
                wfree[par] = []
                lds[i] = dma_in(wtiles[par][:, 0:cols], whbm[r * 128:(r + 1) * 128, :])
            if i == 0:
                continue
            whbm, dst, r, cols = jobs[i - 1]
            par = (i - 1) % 2
            wt_, qt_ = wtiles[par], qtiles[par]
            p.wait("vector", SEM_IN, lds[i - 1])
            for sem, v in qfree[par]:
                p.wait("vector", sem, v)
            qfree[par] = []
            quant(qt_[:, 0:cols], wt_[:, 0:cols], cols, 0, 22016)
            wfree[par] = [(SEM_DVE, p.val(SEM_DVE))]
            dv = p.val(SEM_DVE)
            p.wait("tensor", SEM_DVE, dv)
            for t in range(cols // 128):
                pb = pst[t % 4]
                ev = wb_bf16(2048 + (t % 4) * 64, 128)
                bank_wait("tensor", pb)
                t_op(lambda e, pb=pb, t=t, qt_=qt_: e.transpose(
                    out=pb[:], in_=qt_[:, t * 128:(t + 1) * 128],
                    identity=identb[:]))
                p.wait("scalar", SEM_MM, p.val(SEM_MM))
                p.wait("scalar", SEM_OUT, p.val(SEM_OUT))
                av = s_op(lambda e, ev=ev, pb=pb: e.copy(out=ev, in_=pb[:]))
                bank_mark(pb, SEM_ACT)
                p.wait("sync", SEM_ACT, av)
                dma_out(dst[t * 128:(t + 1) * 128, r * 128:(r + 1) * 128], ev)
            qfree[par] = [(SEM_MM, p.val(SEM_MM))]

    quant_weights()

    def ln_tile(src_t, dst):
        s1, s2, s3 = stat_sb[:, 0:1], stat_sb[:, 1:2], stat_sb[:, 2:3]
        t2 = bigA[:, 4096:5120]
        v_op(lambda e: e.tensor_reduce(out=s1, in_=src_t, axis=AX.X, op=OP.add))
        dv = p.val(SEM_DVE)
        p.wait("scalar", SEM_DVE, dv)
        av = s_op(lambda e: e.activation(out=t2, in_=src_t, func=AF.Square))
        p.wait("vector", SEM_ACT, av)
        v_op(lambda e: e.tensor_reduce(out=s2, in_=t2, axis=AX.X, op=OP.add))
        v_op(lambda e: e.tensor_single_scalar(out=s1, in_=s1, scalar=1.0 / DM, op=OP.mult))
        v_op(lambda e: e.tensor_single_scalar(out=s2, in_=s2, scalar=1.0 / DM, op=OP.mult))
        v_op(lambda e: e.tensor_tensor(out=s3, in0=s1, in1=s1, op=OP.mult))
        vpad()
        v_op(lambda e: e.tensor_tensor(out=s2, in0=s2, in1=s3, op=OP.subtract))
        vpad()
        v_op(lambda e: e.tensor_single_scalar(out=s2, in_=s2, scalar=EPS, op=OP.add))
        dv = p.val(SEM_DVE)
        p.wait("scalar", SEM_DVE, dv)
        av = s_op(lambda e: e.activation(out=s3, in_=s2, func=AF.Sqrt))
        p.wait("vector", SEM_ACT, av)
        v_op(lambda e: e.reciprocal(out=s2, in_=s3))
        vpad()
        v_op(lambda e: e.tensor_scalar(out=dst, in0=src_t, scalar1=s1, scalar2=s2,
                                       op0=OP.subtract, op1=OP.mult))

    def rms_rstd(src_t, rstd_col):
        s2, s3 = stat_sb[:, 4:5], stat_sb[:, 5:6]
        t2 = bigA[:, 4096:5120]
        av = s_op(lambda e: e.activation(out=t2, in_=src_t, func=AF.Square))
        p.wait("vector", SEM_ACT, av)
        v_op(lambda e: e.tensor_reduce(out=s2, in_=t2, axis=AX.X, op=OP.add))
        vpad()
        v_op(lambda e: e.tensor_scalar(out=s2, in0=s2, scalar1=1.0 / DM, scalar2=EPS,
                                       op0=OP.mult, op1=OP.add))
        dv = p.val(SEM_DVE)
        p.wait("scalar", SEM_DVE, dv)
        av = s_op(lambda e: e.activation(out=s3, in_=s2, func=AF.Sqrt))
        p.wait("vector", SEM_ACT, av)
        v_op(lambda e, r=rstd_col: e.reciprocal(out=r, in_=s3))
        vpad()

    NCH = [512, 512, 512, 512, 512, 256]

    def ffn_call(dump=False):
        p.wait("sync", SEM_OUT, p.val(SEM_OUT))
        for th in range(2):
            for tt in range(8):
                ti = th * 8 + tt
                src_t = stream[:, ti * DM:(ti + 1) * DM]
                xln = bigA[:, 5120:6144]
                ln_tile(src_t, xln)
                xq = bigB[:, 8192:9216]
                p.wait("vector", SEM_MM, p.val(SEM_MM))
                if dump:
                    dvx = p.val(SEM_DVE)
                    p.wait("sync", SEM_DVE, dvx)
                    dma_out(xln_dbg[ti], xln)
                quant(xq, xln, DM, 0, 9216)
                dv = p.val(SEM_DVE)
                p.wait("tensor", SEM_DVE, dv)
                if dump:
                    p.wait("sync", SEM_DVE, dv)
                    dma_out(xq_dbg[ti], xq)
                transpose_tiles(xq, 8,
                                lambda t, tt=tt: bigB[:, t * 1024 + tt * 128:
                                                      t * 1024 + tt * 128 + 128])
            xqT = bigB[:, 0:8192].rearrange("p (a b) -> p a b", b=1024)
            avq = p.val(SEM_ACT)

            hq_base = 8192
            for ci, ch in enumerate(NCH):
                c0 = sum(NCH[:ci])
                wgc = bigA[:, 7168:7168 + 2048].bitcast(BF16)[:, 0:8 * ch] \
                    .rearrange("p (a b) -> p a b", b=ch)
                wuc = bigA[:, 9216:9216 + 2048].bitcast(BF16)[:, 0:8 * ch] \
                    .rearrange("p (a b) -> p a b", b=ch)
                p.wait("sync", SEM_MM, p.val(SEM_MM))
                dma_in(wgc, wgT_q[:, c0:c0 + ch].rearrange("(a p) b -> p a b", p=128))
                l2 = dma_in(wuc, wuT_q[:, c0:c0 + ch].rearrange("(a p) b -> p a b", p=128))
                p.wait("tensor", SEM_IN, l2)
                p.wait("tensor", SEM_ACT, avq)
                for tt in range(8):
                    pg, pu = ps[2 * (tt % 2)], ps[2 * (tt % 2) + 1]
                    bank_wait("tensor", pg)
                    for k in range(8):
                        t_op(lambda e, k=k, tt=tt, ch=ch, pg=pg, wgc=wgc: e.matmul(
                            pg[:, 0:ch], xqT[:, k, tt * 128:(tt + 1) * 128],
                            wgc[:, k, :], start=(k == 0), stop=(k == 7)))
                    bank_wait("tensor", pu)
                    for k in range(8):
                        t_op(lambda e, k=k, tt=tt, ch=ch, pu=pu, wuc=wuc: e.matmul(
                            pu[:, 0:ch], xqT[:, k, tt * 128:(tt + 1) * 128],
                            wuc[:, k, :], start=(k == 0), stop=(k == 7)))
                    mmv = p.val(SEM_MM)
                    p.wait("scalar", SEM_MM, mmv)
                    hsil = bigA[:, 6144:6656]
                    hch = bigA[:, 6656:7168]
                    p.wait("scalar", SEM_DVE, p.val(SEM_DVE))
                    av = s_op(lambda e, pg=pg, ch=ch, hsil=hsil: e.activation(
                        out=hsil[:, 0:ch], in_=pg[:, 0:ch], func=AF.Silu))
                    bank_mark(pg, SEM_ACT)
                    p.wait("vector", SEM_ACT, av)
                    v_op(lambda e, pu=pu, ch=ch, hsil=hsil, hch=hch: e.tensor_tensor(
                        out=hch[:, 0:ch], in0=hsil[:, 0:ch], in1=pu[:, 0:ch],
                        op=OP.mult))
                    bank_mark(pu, SEM_DVE)
                    if dump:
                        dvh = p.val(SEM_DVE)
                        p.wait("sync", SEM_DVE, dvh)
                        dma_out(h_dbg[th * 8 + tt][:, c0:c0 + ch], hch[:, 0:ch])
                    hqc = bigB[:, 30720:30720 + ch]
                    p.wait("vector", SEM_MM, p.val(SEM_MM))
                    quant(hqc, hch[:, 0:ch], ch, 0, 31232)
                    dv = p.val(SEM_DVE)
                    p.wait("tensor", SEM_DVE, dv)
                    transpose_tiles(hqc, ch // 128,
                                    lambda t, c0=c0, tt=tt: bigB[
                                        :, hq_base + (c0 // 128 + t) * 1024 + tt * 128:
                                        hq_base + (c0 // 128 + t) * 1024 + tt * 128 + 128])
            hqT = bigB[:, hq_base:hq_base + 22528].rearrange("p (a b) -> p a b", b=1024)
            avh = p.val(SEM_ACT)

            for dc in range(4):
                wdc = wb_bf16(0, 22 * 256).rearrange("p (a b) -> p a b", b=256)
                p.wait("sync", SEM_MM, p.val(SEM_MM))
                ld = dma_in(wdc, wdT_q[:, dc * 256:(dc + 1) * 256]
                            .rearrange("(a p) b -> p a b", p=128))
                p.wait("tensor", SEM_IN, ld)
                p.wait("tensor", SEM_ACT, avh)
                for tt in range(8):
                    ti = th * 8 + tt
                    pb = ps[2 + (tt % 2)]
                    bank_wait("tensor", pb)
                    for k in range(22):
                        t_op(lambda e, k=k, tt=tt, pb=pb, wdc=wdc: e.matmul(
                            pb[:, 0:256], hqT[:, k, tt * 128:(tt + 1) * 128],
                            wdc[:, k, :], start=(k == 0), stop=(k == 21)))
                    mmv = p.val(SEM_MM)
                    p.wait("vector", SEM_MM, mmv)
                    dst = stream[:, ti * DM + dc * 256:ti * DM + (dc + 1) * 256]
                    v_op(lambda e, pb=pb, dst=dst: e.scalar_tensor_tensor(
                        out=dst, in0=pb[:, 0:256], scalar=0.5, in1=dst,
                        op0=OP.mult, op1=OP.add))
                    bank_mark(pb, SEM_DVE)

    def mamba_layer(li):
        xn = bigB[:, 0:16384].rearrange("p (a b) -> p a b", b=DM)
        rstd = stat_sb[:, 6:7]
        p.wait("scalar", SEM_DVE, p.val(SEM_DVE))
        p.wait("sync", SEM_DVE, p.val(SEM_DVE))
        for i in range(NT):
            src_t = stream[:, i * DM:(i + 1) * DM]
            rms_rstd(src_t, rstd)
            v_op(lambda e, s=src_t, i=i: e.tensor_single_scalar(
                out=xn[:, i, :], in_=s, scalar=rstd, op=OP.mult))
            dma_out(h_sp[i], src_t)
        hspv = p.val(SEM_OUT)
        dv = p.val(SEM_DVE)
        p.wait("tensor", SEM_DVE, dv)
        p.wait("scalar", SEM_OUT, hspv)

        xnT = stream_b[:, 0:16384].rearrange("p (a b) -> p a b", b=2048)
        for i in range(NT):
            transpose_tiles(xn[:, i, :], 8,
                            lambda t, i=i: xnT[:, t, i * 128:(i + 1) * 128])
        avT = p.val(SEM_ACT)
        p.wait("tensor", SEM_ACT, avT)

        xs = bigB[:, 0:16 * 2048].rearrange("p (a b) -> p a b", b=2048)
        wic_f = bigA[:, 0:1024].rearrange("p (a b) -> p a b", b=128)
        wic = wb_bf16(2048, 8 * 128).rearrange("p (a b) -> p a b", b=128)
        rtmp = bigA[:, 1024:1536]
        for mt in range(32):
            p.wait("sync", SEM_MM, p.val(SEM_MM))
            ld = dma_in(wic_f, wiT_h[li][:, mt * 128:(mt + 1) * 128]
                        .rearrange("(a p) b -> p a b", p=128))
            p.wait("vector", SEM_IN, ld)
            v_op(lambda e: e.tensor_tensor(
                out=wic, in0=wic_f,
                in1=nw_sb[:, li * 8:(li + 1) * 8].rearrange("p (a o) -> p a o", o=1)
                    .broadcast_to([128, 8, 128]),
                op=OP.mult))
            cv = p.val(SEM_DVE)
            p.wait("tensor", SEM_DVE, cv)
            for tb in range(NTB):
                pb = ps[tb % 2]
                bank_wait("tensor", pb)
                for k in range(8):
                    t_op(lambda e, k=k, tb=tb, pb=pb: e.matmul(
                        pb[:], wic[:, k, :], xnT[:, k, tb * 512:(tb + 1) * 512],
                        start=(k == 0), stop=(k == 7)))
                p.wait("scalar", SEM_MM, p.val(SEM_MM))
                bcol = bi_sb[:, li * 32 + mt:li * 32 + mt + 1]
                if mt < 16:
                    s_op(lambda e, pb=pb, mt=mt, tb=tb, bcol=bcol: e.add(
                        out=xs[:, mt, tb * 512:(tb + 1) * 512], in_=pb[:], add=bcol))
                else:
                    p.wait("scalar", SEM_OUT, p.val(SEM_OUT))
                    av = s_op(lambda e, pb=pb, bcol=bcol: e.add(
                        out=rtmp, in_=pb[:], add=bcol))
                    p.wait("sync", SEM_ACT, av)
                    dma_out(res_sp[mt - 16][:, tb * 512:(tb + 1) * 512], rtmp)
                bank_mark(pb, SEM_ACT)

        hpf = bigA[:, 2048:2048 + 64]
        xsv = p.val(SEM_ACT)
        p.wait("vector", SEM_ACT, xsv)
        for dt in range(NDT):
            v_op(lambda e, dt=dt: e.tensor_copy(
                out=hpf[:, dt * 4:dt * 4 + 2], in_=xs[:, dt, 0:2]))
            v_op(lambda e, dt=dt: e.tensor_copy(
                out=hpf[:, dt * 4 + 2:dt * 4 + 4], in_=xs[:, dt, 2046:2048]))
        hv = p.val(SEM_DVE)
        p.wait("sync", SEM_DVE, hv)
        gv = dma_out(gin_h, hpf)
        p.wait("gpsimd", SEM_OUT, gv)
        gp_slack(2)
        p.op("gpsimd", lambda e: e.collective_compute(
            "AllGather", OP.bypass, replica_groups=[list(range(NCORES))],
            ins=[gin_h], outs=[gout_h]), SEM_CC)
        p.wait("sync", SEM_CC, p.val(SEM_CC))
        dma_slack(2)
        gl = bigA[:, 2112:2112 + 512].rearrange("p (a b) -> p a b", b=64)
        lg = dma_in(gl, gout_h.rearrange("c p f -> p c f"))
        p.wait("vector", SEM_IN, lg)
        v_op(lambda e: e.memset(halo_sb[:], 0.0))
        vpad()
        vpad()
        hl4 = halo_sb.rearrange("p (a b) -> p a b", b=4)
        for s in range(NCORES):
            v_op(lambda e, s=s: e.scalar_tensor_tensor(
                out=hl4[:, :, 0:2],
                in0=gl[:, s, :].rearrange("p (a b) -> p a b", b=4)[:, :, 2:4],
                scalar=oh_sb[:, 8 + s:9 + s],
                in1=hl4[:, :, 0:2],
                op0=OP.mult, op1=OP.add))
            v_op(lambda e, s=s: e.scalar_tensor_tensor(
                out=hl4[:, :, 2:4],
                in0=gl[:, s, :].rearrange("p (a b) -> p a b", b=4)[:, :, 0:2],
                scalar=oh_sb[:, 16 + s:17 + s],
                in1=hl4[:, :, 2:4],
                op0=OP.mult, op1=OP.add))
        halo_b = wb_bf16(1280, NDT * 4).rearrange("p (a b) -> p a b", b=4)
        vpad()
        vpad()
        v_op(lambda e: e.tensor_copy(out=halo_b.rearrange("p a b -> p (a b)"),
                                     in_=halo_sb[:]))
        hv2 = p.val(SEM_DVE)
        p.wait("tensor", SEM_DVE, hv2)

        ub = bigA[:, 0:4096].bitcast(BF16).rearrange("p (a b) -> p a b", b=512)
        uf = bigA[:, 4096:4608]
        r_t = bigA[:, 4608:5120]
        m_t = bigA[:, 5120:5632]
        e_t = bigA[:, 5632:6144]
        d_t = bigA[:, 6144:6656]
        w_t = bigA[:, 6656:7168]
        y_t = bigA[:, 7168:7680]
        ur_t = bigA[:, 7680:8192]
        diag = wb_bf16(0, 5 * 128).rearrange("p (a b) -> p a b", b=128)
        wxc_f = wbuf[:, 1536:1536 + 1536].rearrange("p (a b) -> p a b", b=96)
        wxc = wb_bf16(512, 16 * 96).rearrange("p (a b) -> p a b", b=96)
        ldx = dma_in(wxc_f, wxT_h[li].rearrange("(a p) b -> p a b", p=128))
        p.wait("vector", SEM_IN, ldx)
        v_op(lambda e: e.tensor_copy(out=wxc.rearrange("p a b -> p (a b)"),
                                     in_=wxc_f.rearrange("p a b -> p (a b)")))
        wxv = p.val(SEM_DVE)
        p.wait("tensor", SEM_DVE, wxv)
        xdbl = stream[0:96, 8192:10240]
        dtwb = [wbuf[0:64, 1536:1536 + 128], wbuf[0:64, 1664:1664 + 128]]
        dtw_ld = [0, 0]
        dtw_mm = [0, 0]
        cdT = scol[:, 768:784]
        for tb in range(NTB):
            for dt in range(NDT):
                p.wait("vector", SEM_MM, p.val(SEM_MM))
                for j in range(5):
                    v_op(lambda e, j=j, dt=dt: e.tensor_single_scalar(
                        out=diag[:, j, :], in_=identf[:],
                        scalar=cw_sb[:, (li * NDT + dt) * 5 + j:
                                     (li * NDT + dt) * 5 + j + 1], op=OP.mult))
                dv = p.val(SEM_DVE)
                p.wait("tensor", SEM_DVE, dv)
                pb = ps[dt % 2]
                bank_wait("tensor", pb)
                segs = []
                for j in range(5):
                    lo = tb * 512 + j - 2
                    hi = lo + 512
                    if lo < 0:
                        nh = -lo
                        segs.append((j, pb[:, 0:nh], halo_b[:, dt, 2 - nh:2]))
                        segs.append((j, pb[:, nh:512], xs[:, dt, 0:512 - nh]))
                    elif hi > 2048:
                        nh = hi - 2048
                        segs.append((j, pb[:, 0:512 - nh], xs[:, dt, lo:2048]))
                        segs.append((j, pb[:, 512 - nh:512], halo_b[:, dt, 2:2 + nh]))
                    else:
                        segs.append((j, pb[:], xs[:, dt, lo:lo + 512]))
                for si, (j, pslice, mov) in enumerate(segs):
                    t_op(lambda e, j=j, pslice=pslice, mov=mov, si=si,
                         last=(si == len(segs) - 1): e.matmul(
                        pslice, diag[:, j, :], mov, start=(si == 0), stop=last))
                p.wait("scalar", SEM_MM, p.val(SEM_MM))
                ccol = cb_sb[:, li * NDT + dt:li * NDT + dt + 1]
                p.wait("scalar", SEM_OUT, p.val(SEM_OUT))
                p.wait("scalar", SEM_DVE, p.val(SEM_DVE))
                av = s_op(lambda e, pb=pb, ccol=ccol: e.activation(
                    out=uf, in_=pb[:], func=AF.Silu, bias=ccol))
                bank_mark(pb, SEM_ACT)
                p.wait("vector", SEM_ACT, av)
                v_op(lambda e, dt=dt: e.tensor_copy(out=ub[:, dt, :], in_=uf))
                p.wait("sync", SEM_ACT, av)
                dma_out(u_sp[dt][:, tb * 512:(tb + 1) * 512], uf)
            p.wait("sync", SEM_OUT, p.val(SEM_OUT))
            dv = p.val(SEM_DVE)
            p.wait("tensor", SEM_DVE, dv)
            pbx = ps[2]
            bank_wait("tensor", pbx)
            for k in range(NDT):
                t_op(lambda e, k=k, pbx=pbx: e.matmul(
                    pbx[0:96, :], wxc[:, k, :], ub[:, k, :],
                    start=(k == 0), stop=(k == NDT - 1)))
            mv = p.val(SEM_MM)
            p.wait("scalar", SEM_MM, mv)
            av = s_op(lambda e, pbx=pbx, tb=tb: e.copy(
                out=xdbl[:, tb * 512:(tb + 1) * 512], in_=pbx[0:96, :]))
            bank_mark(pbx, SEM_ACT)
            p.wait("tensor", SEM_ACT, av)
            p.wait("sync", SEM_MM, p.val(SEM_MM))
            dtw_ld[0] = dma_in(dtwb[0], wdtT_h[li][:, 0:128])
            for mt in range(NDT):
                wpar = mt % 2
                if mt + 1 < NDT:
                    p.wait("sync", SEM_MM, dtw_mm[1 - wpar])
                    dtw_ld[1 - wpar] = dma_in(
                        dtwb[1 - wpar], wdtT_h[li][:, (mt + 1) * 128:(mt + 2) * 128])
                p.wait("sync", SEM_DVE, p.val(SEM_DVE))
                dma_in(ur_t, u_sp[mt][:, tb * 512:(tb + 1) * 512])
                p.wait("tensor", SEM_IN, dtw_ld[wpar])
                pb2 = ps[3]
                bank_wait("tensor", pb2)
                t_op(lambda e, tb=tb, pb2=pb2, wpar=wpar: e.matmul(
                    pb2[:], dtwb[wpar], xdbl[0:64, tb * 512:(tb + 1) * 512],
                    start=True, stop=True))
                dtw_mm[wpar] = p.val(SEM_MM)
                mv = p.val(SEM_MM)
                p.wait("scalar", SEM_MM, mv)
                p.wait("vector", SEM_MM, mv)
                bcol = dtb_sb[:, li * NDT + mt:li * NDT + mt + 1]
                p.wait("scalar", SEM_OUT, p.val(SEM_OUT))
                p.wait("vector", SEM_OUT, p.val(SEM_OUT))
                av = s_op(lambda e, pb2=pb2, bcol=bcol: e.activation(
                    out=r_t, in_=pb2[:], func=AF.Relu, bias=bcol))
                p.wait("vector", SEM_ACT, p.val(SEM_ACT))
                v_op(lambda e, pb2=pb2: e.scalar_tensor_tensor(
                    out=m_t, in0=r_t, scalar=2.0, in1=pb2[:],
                    op0=OP.mult, op1=OP.subtract))
                v_op(lambda e, bcol=bcol: e.tensor_single_scalar(
                    out=m_t, in_=m_t, scalar=bcol, op=OP.subtract))
                bank_mark(pb2, SEM_DVE)
                dv = p.val(SEM_DVE)
                p.wait("scalar", SEM_DVE, dv)
                av = s_op(lambda e: e.activation(out=e_t, in_=m_t, func=AF.Exp,
                                                 scale=-1.0))
                p.wait("vector", SEM_ACT, av)
                v_op(lambda e: e.tensor_single_scalar(out=e_t, in_=e_t, scalar=1.0,
                                                      op=OP.add))
                dv = p.val(SEM_DVE)
                p.wait("scalar", SEM_DVE, dv)
                av = s_op(lambda e: e.activation(out=m_t, in_=e_t, func=AF.Ln))
                p.wait("vector", SEM_ACT, av)
                p.wait("vector", SEM_IN, p.val(SEM_IN))
                v_op(lambda e: e.tensor_tensor(out=d_t, in0=r_t, in1=m_t, op=OP.add))
                v_op(lambda e: e.tensor_reduce(out=stat_sb[:, 8:9], in_=d_t,
                                               axis=AX.X, op=OP.add))
                vpad()
                if tb == 0:
                    v_op(lambda e, mt=mt: e.tensor_copy(out=cdT[:, mt:mt + 1],
                                                        in_=stat_sb[:, 8:9]))
                else:
                    v_op(lambda e, mt=mt: e.tensor_tensor(
                        out=cdT[:, mt:mt + 1], in0=cdT[:, mt:mt + 1],
                        in1=stat_sb[:, 8:9], op=OP.add))
                v_op(lambda e: e.tensor_tensor(out=w_t, in0=d_t,
                                                   in1=ur_t, op=OP.mult))
                dcol = d_sb[:, li * NDT + mt:li * NDT + mt + 1]
                v_op(lambda e, dcol=dcol: e.tensor_single_scalar(
                    out=y_t, in_=ur_t, scalar=dcol, op=OP.mult))
                dv = p.val(SEM_DVE)
                p.wait("sync", SEM_DVE, dv)
                dma_out(del_sp[mt][:, tb * 512:(tb + 1) * 512], d_t)
                dma_out(w_sp[mt][:, tb * 512:(tb + 1) * 512], w_t)
                dma_out(yd_sp[mt][:, tb * 512:(tb + 1) * 512], y_t)
        p.wait("sync", SEM_ACT, p.val(SEM_ACT))
        bcv = dma_out(bc_sp, xdbl[64:96, :])
        p.wait("sync", SEM_OUT, p.val(SEM_OUT))

        reps = stream
        bigBf = bigB.bitcast(F32)
        dltb = [bigA[:, 0:2048], bigA[:, 2048:4096]]
        wtb = [bigA[:, 4096:6144], bigA[:, 6144:8192]]
        ytb = [bigA[:, 8192:10240], bigA[:, 10240:12288]]
        ab = [bigBf[:, 0:2048], bigBf[:, 2048:4096]]
        zb = [bigBf[:, 4096:6144], bigBf[:, 6144:8192]]
        hb = [bigBf[:, 8192:10240], bigBf[:, 10240:12288]]
        tb_ = [bigBf[:, 12288:14336], bigBf[:, 14336:16384]]
        Mv = scol[:, 0:256]
        hend = scol[:, 256:512]
        hin = scol[:, 512:768]
        tc = scol[:, 800:1056]
        ld_free = {0: [], 1: []}
        ab_free = {0: [], 1: []}
        zb_free = {0: [], 1: []}
        for ng in range(4):
            p.wait("sync", SEM_GP, p.val(SEM_GP))
            p.wait("sync", SEM_DVE, p.val(SEM_DVE))
            p.wait("sync", SEM_MM, p.val(SEM_MM))
            p.wait("sync", SEM_ACT, p.val(SEM_ACT))
            p.wait("sync", SEM_OUT, p.val(SEM_OUT))
            if ng == 0:
                dma_slack(1)
            rl = None
            for j in range(4):
                n = ng * 4 + j
                dma_in(reps[:, (2 * j) * 2048:(2 * j + 1) * 2048],
                       bc_sp[n:n + 1, :].partition_broadcast(128))
                rl = dma_in(reps[:, (2 * j + 1) * 2048:(2 * j + 2) * 2048],
                            bc_sp[16 + n:17 + n, :].partition_broadcast(128))
            p.wait("vector", SEM_IN, rl)
            p.wait("gpsimd", SEM_IN, rl)
            ld_done = {}
            for dt in range(NDT + 1):
                if dt < NDT:
                    par = dt % 2
                    for sem, v in ld_free[par]:
                        p.wait("sync", sem, v)
                    ld_free[par] = []
                    dma_in(dltb[par], del_sp[dt])
                    dma_in(wtb[par], w_sp[dt])
                    ysrc = yd_sp[dt] if ng == 0 else y_sp[dt]
                    ld_done[dt] = dma_in(ytb[par], ysrc)
                if dt == 0:
                    continue
                cdt = dt - 1
                par = cdt % 2
                dlt_c, wt_c, yt_c = dltb[par], wtb[par], ytb[par]
                p.wait("scalar", SEM_IN, ld_done[cdt])
                p.wait("gpsimd", SEM_IN, ld_done[cdt])
                p.wait("vector", SEM_IN, ld_done[cdt])
                for j in range(4):
                    n = ng * 4 + j
                    bpar = j % 2
                    acol = a_sb[:, (li * NDT + cdt) * NST + n:
                                (li * NDT + cdt) * NST + n + 1]
                    for sem, v in ab_free[bpar]:
                        p.wait("scalar", sem, v)
                    ab_free[bpar] = []
                    av = s_op(lambda e, acol=acol, A=ab[bpar], D=dlt_c: e.activation(
                        out=A, in_=D, func=AF.Exp, scale=acol))
                    for sem, v in zb_free[bpar]:
                        p.wait("vector", sem, v)
                    zb_free[bpar] = []
                    v_op(lambda e, j=j, Z=zb[bpar], W=wt_c: e.tensor_tensor(
                        out=Z, in0=W,
                        in1=reps[:, (2 * j) * 2048:(2 * j + 1) * 2048], op=OP.mult))
                    p.wait("vector", SEM_ACT, av)
                    v_op(lambda e, A=ab[bpar], Z=zb[bpar], H=hb[bpar]:
                         e.tensor_tensor_scan(out=H, data0=A, data1=Z, initial=0.0,
                                              op0=OP.mult, op1=OP.add))
                    ab_free[bpar].append((SEM_DVE, p.val(SEM_DVE)))
                    zb_free[bpar].append((SEM_DVE, p.val(SEM_DVE)))
                    p.wait("gpsimd", SEM_DVE, p.val(SEM_DVE))
                    gc = g_op(lambda e, j=j, H=hb[bpar], T=tb_[bpar]: e.tensor_tensor(
                        out=T[:, 1280:2048], in0=H[:, 1280:2048],
                        in1=reps[:, (2 * j + 1) * 2048 + 1280:(2 * j + 2) * 2048],
                        op=OP.mult))
                    g_op(lambda e, cdt=cdt, n=n, H=hb[bpar]: e.tensor_copy(
                        out=hend[:, cdt * 16 + n:cdt * 16 + n + 1],
                        in_=H[:, 2047:2048]))
                    v_op(lambda e, j=j, H=hb[bpar], T=tb_[bpar]: e.tensor_tensor(
                        out=T[:, 0:1280], in0=H[:, 0:1280],
                        in1=reps[:, (2 * j + 1) * 2048:(2 * j + 1) * 2048 + 1280],
                        op=OP.mult))
                    p.wait("vector", SEM_GP, gc)
                    v_op(lambda e, T=tb_[bpar], Y=yt_c: e.tensor_tensor(
                        out=Y, in0=Y, in1=T, op=OP.add))
                    ab_free[bpar].append((SEM_GP, p.val(SEM_GP)))
                dv = p.val(SEM_DVE)
                p.wait("sync", SEM_DVE, dv)
                ow = dma_out(y_sp[cdt], yt_c)
                ld_free[par] = [(SEM_ACT, p.val(SEM_ACT)),
                                (SEM_GP, p.val(SEM_GP)),
                                (SEM_OUT, ow)]

        p.wait("scalar", SEM_DVE, p.val(SEM_DVE))
        cdT = scol[:, 768:784]
        for dt in range(NDT):
            s_op(lambda e, dt=dt: e.activation(
                out=Mv[:, dt * 16:(dt + 1) * 16],
                in_=a_sb[:, (li * NDT + dt) * NST:(li * NDT + dt + 1) * NST],
                func=AF.Exp, scale=cdT[:, dt:dt + 1]))
        sv = p.val(SEM_ACT)
        p.wait("sync", SEM_ACT, sv)
        p.wait("sync", SEM_DVE, p.val(SEM_DVE))
        p.wait("sync", SEM_GP, p.val(SEM_GP))
        dma_out(gin_s[:, 0:256], Mv)
        g2 = dma_out(gin_s[:, 256:512], hend)
        p.wait("gpsimd", SEM_OUT, g2)
        gp_slack(2)
        p.op("gpsimd", lambda e: e.collective_compute(
            "AllGather", OP.bypass, replica_groups=[list(range(NCORES))],
            ins=[gin_s], outs=[gout_s]), SEM_CC)
        p.wait("sync", SEM_CC, p.val(SEM_CC))
        p.wait("sync", SEM_GP, p.val(SEM_GP))
        dma_slack(2)
        gat = bigBf[:, 14336 - 4096 + 4096:16384]
        gat = bigA[:, 0:4096].rearrange("p (a b) -> p a b", b=512)
        lg = dma_in(gat, gout_s.rearrange("c p f -> p c f"))
        p.wait("vector", SEM_IN, lg)
        v_op(lambda e: e.memset(hin, 0.0))
        v_op(lambda e: e.memset(tc, 0.0))
        vpad()
        vpad()
        for sc_ in range(NCORES):
            if sc_ % 4 == 0:
                v_op(lambda e: e.memset(tc, 0.0))
                vpad()
                vpad()
            else:
                v_op(lambda e, sc_=sc_: e.tensor_tensor(
                    out=tc, in0=tc, in1=gat[:, sc_ - 1, 0:256], op=OP.mult))
                v_op(lambda e, sc_=sc_: e.tensor_tensor(
                    out=tc, in0=tc, in1=gat[:, sc_ - 1, 256:512], op=OP.add))
                v_op(lambda e, sc_=sc_: e.scalar_tensor_tensor(
                    out=hin, in0=tc, scalar=oh_sb[:, sc_:sc_ + 1], in1=hin,
                    op0=OP.mult, op1=OP.add))

        zeros_t = wtb[0]
        p.wait("vector", SEM_IN, p.val(SEM_IN))
        p.wait("vector", SEM_GP, p.val(SEM_GP))
        v_op(lambda e: e.memset(zeros_t, 0.0))
        ld_free = {0: [], 1: []}
        gb_free = {0: [], 1: []}
        for ng in range(4):
            p.wait("sync", SEM_DVE, p.val(SEM_DVE))
            p.wait("sync", SEM_OUT, p.val(SEM_OUT))
            if ng == 0:
                dma_slack(1)
            rl = None
            for j in range(4):
                n = ng * 4 + j
                rl = dma_in(reps[:, (2 * j + 1) * 2048:(2 * j + 2) * 2048],
                            bc_sp[16 + n:17 + n, :].partition_broadcast(128))
            p.wait("vector", SEM_IN, rl)
            ld_done = {}
            for dt in range(NDT + 1):
                if dt < NDT:
                    par = dt % 2
                    for sem, v in ld_free[par]:
                        p.wait("sync", sem, v)
                    ld_free[par] = []
                    dma_in(dltb[par], del_sp[dt])
                    ld_done[dt] = dma_in(ytb[par], y_sp[dt])
                if dt == 0:
                    continue
                cdt = dt - 1
                par = cdt % 2
                dlt_c, yt_c = dltb[par], ytb[par]
                cd_c = hb[par]
                p.wait("vector", SEM_IN, ld_done[cdt])
                p.wait("vector", SEM_ACT, p.val(SEM_ACT))
                v_op(lambda e, D=dlt_c, CD=cd_c: e.tensor_tensor_scan(
                    out=CD, data0=D, data1=zeros_t, initial=0.0,
                    op0=OP.add, op1=OP.add))
                dvc = p.val(SEM_DVE)
                p.wait("scalar", SEM_DVE, dvc)
                for j in range(4):
                    n = ng * 4 + j
                    bpar = j % 2
                    acol = a_sb[:, (li * NDT + cdt) * NST + n:
                                (li * NDT + cdt) * NST + n + 1]
                    for sem, v in gb_free[bpar]:
                        p.wait("scalar", sem, v)
                    gb_free[bpar] = []
                    av = s_op(lambda e, acol=acol, G=ab[bpar], CD=cd_c: e.activation(
                        out=G, in_=CD, func=AF.Exp, scale=acol))
                    p.wait("vector", SEM_ACT, av)
                    hcol = hin[:, cdt * 16 + n:cdt * 16 + n + 1]
                    v_op(lambda e, j=j, hcol=hcol, G=ab[bpar], T=tb_[bpar]:
                         e.scalar_tensor_tensor(
                        out=T, in0=G, scalar=hcol,
                        in1=reps[:, (2 * j + 1) * 2048:(2 * j + 2) * 2048],
                        op0=OP.mult, op1=OP.mult))
                    gb_free[bpar].append((SEM_DVE, p.val(SEM_DVE)))
                    v_op(lambda e, T=tb_[bpar], Y=yt_c: e.tensor_tensor(
                        out=Y, in0=Y, in1=T, op=OP.add))
                dv = p.val(SEM_DVE)
                p.wait("sync", SEM_DVE, dv)
                ow = dma_out(y_sp[cdt], yt_c)
                ld_free[par] = [(SEM_DVE, dv), (SEM_OUT, ow)]

        p.wait("sync", SEM_OUT, p.val(SEM_OUT))
        p.wait("sync", SEM_ACT, p.val(SEM_ACT))
        p.wait("sync", SEM_DVE, p.val(SEM_DVE))
        p.wait("scalar", SEM_DVE, p.val(SEM_DVE))
        gld = {}
        gfree = {0: [], 1: []}
        for dt in range(NDT + 1):
            if dt < NDT:
                par = dt % 2
                for sem, v in gfree[par]:
                    p.wait("sync", sem, v)
                gfree[par] = []
                dma_in(ytb[par], y_sp[dt])
                gld[dt] = dma_in(dltb[par], res_sp[dt])
            if dt == 0:
                continue
            cdt = dt - 1
            par = cdt % 2
            p.wait("scalar", SEM_IN, gld[cdt])
            av = s_op(lambda e, R=dltb[par], A=wtb[par]: e.activation(
                out=A, in_=R, func=AF.Silu))
            p.wait("vector", SEM_ACT, av)
            p.wait("vector", SEM_IN, gld[cdt])
            v_op(lambda e, cdt=cdt, Y=ytb[par], A=wtb[par]: e.tensor_tensor(
                out=bigB[:, cdt * 2048:(cdt + 1) * 2048], in0=Y, in1=A,
                op=OP.mult))
            gfree[par] = [(SEM_DVE, p.val(SEM_DVE)), (SEM_ACT, p.val(SEM_ACT))]
        y_g = bigB[:, 0:16 * 2048].rearrange("p (a b) -> p a b", b=2048)
        ygv = p.val(SEM_DVE)
        p.wait("tensor", SEM_DVE, ygv)

        woc = bigA[:, 0:8192].bitcast(BF16).rearrange("p (a b) -> p a b", b=1024)
        wof = bigA[:, 8192:10240]
        for dt2 in range(8):
            p.wait("sync", SEM_DVE, p.val(SEM_DVE))
            ld = dma_in(wof.rearrange("p (a b) -> p a b", b=DM),
                        woT_h[li][dt2 * 256:(dt2 + 1) * 256, :]
                        .rearrange("(a p) b -> p a b", p=128))
            p.wait("vector", SEM_IN, ld)
            v_op(lambda e, dt2=dt2: e.tensor_copy(
                out=woc[:, 2 * dt2:2 * dt2 + 2, :].rearrange("p a b -> p (a b)"),
                in_=wof))
        wov = p.val(SEM_DVE)
        p.wait("tensor", SEM_DVE, wov)
        p.wait("sync", SEM_OUT, p.val(SEM_OUT))
        for i in range(NT):
            p.wait("sync", SEM_DVE, p.val(SEM_DVE))
            hld = dma_in(wof[:, 0:1024], h_sp[i])
            for c in range(2):
                pb = ps[c]
                bank_wait("tensor", pb)
                for k in range(NDT):
                    t_op(lambda e, k=k, i=i, c=c, pb=pb: e.matmul(
                        pb[:], y_g[:, k, i * 128:(i + 1) * 128],
                        woc[:, k, c * 512:(c + 1) * 512],
                        start=(k == 0), stop=False))
                t_op(lambda e, c=c, pb=pb: e.matmul(
                    pb[:], onesr[0:1, 0:128],
                    bo_sb[0:1, li * DM + c * 512:li * DM + (c + 1) * 512],
                    start=False, stop=True))
            mv = p.val(SEM_MM)
            p.wait("vector", SEM_MM, mv)
            p.wait("vector", SEM_IN, hld)
            for c in range(2):
                pb = ps[c]
                v_op(lambda e, i=i, c=c, pb=pb: e.tensor_tensor(
                    out=stream[:, i * DM + c * 512:i * DM + (c + 1) * 512],
                    in0=wof[:, c * 512:(c + 1) * 512], in1=pb[:], op=OP.add))
                bank_mark(pb, SEM_DVE)
        if dbg:
            p.wait("sync", SEM_DVE, p.val(SEM_DVE))
            for i2 in range(NT):
                dma_out(xln_dbg[i2], stream[:, i2 * DM:(i2 + 1) * DM])
            p.wait("gpsimd", SEM_OUT, p.val(SEM_OUT))
            p.wait("vector", SEM_OUT, p.val(SEM_OUT))
            p.wait("scalar", SEM_OUT, p.val(SEM_OUT))

    ffn_call(dump=dbg)
    p.wait("sync", SEM_DVE, p.val(SEM_DVE))
    for i in range(NT):
        src_t = stream[:, i * DM:(i + 1) * DM]
        xv = dma_out(x_sp[i], src_t)
        xln = bigA[:, 5120:6144]
        ln_tile(src_t, xln)
        p.wait("vector", SEM_OUT, xv)
        v_op(lambda e, s=src_t, xln=xln: e.tensor_copy(out=s, in_=xln))
    for li in range(nlay):
        mamba_layer(li)
    rstd2 = stat_sb[:, 7:8]
    xr_t = bigA[:, 6144:7168]
    nfw_sb = bigA[:, 7168:8192]
    p.wait("sync", SEM_DVE, p.val(SEM_DVE))
    p.wait("sync", SEM_ACT, p.val(SEM_ACT))
    p.wait("sync", SEM_OUT, p.val(SEM_OUT))
    nfl = dma_in(nfw_sb, nfw_h.partition_broadcast(128))
    p.wait("vector", SEM_IN, nfl)
    p.wait("scalar", SEM_DVE, p.val(SEM_DVE))
    for i in range(NT):
        src_t = stream[:, i * DM:(i + 1) * DM]
        rms_rstd(src_t, rstd2)
        t1 = bigA[:, 5120:6144]
        v_op(lambda e, s=src_t, t1=t1: e.scalar_tensor_tensor(
            out=t1, in0=s, scalar=rstd2, in1=nfw_sb, op0=OP.mult, op1=OP.mult))
        p.wait("sync", SEM_DVE, p.val(SEM_DVE))
        xld = dma_in(xr_t, x_sp[i])
        p.wait("vector", SEM_IN, xld)
        vpad()
        v_op(lambda e, s=src_t, t1=t1: e.tensor_tensor(
            out=s, in0=t1, in1=xr_t, op=OP.add))
    ffn_call()
    fv = p.val(SEM_DVE)
    p.wait("sync", SEM_DVE, fv)
    ov = dma_out(out_h.rearrange("(a p) b -> p a b", p=128), stream.rearrange("p (a b) -> p a b", b=DM))
    p.wait("sync", SEM_OUT, ov)

    with nc.Block() as block:
        p.replay(block)
    return nc


_NC_CACHE = [None]


def _stage_inputs(inputs):
    x = np.asarray(inputs["L0_attn_x"], np.float32)
    ipw = np.asarray(inputs["L0_attn_in_proj_w"], np.float32)
    ipb = np.asarray(inputs["L0_attn_in_proj_b"], np.float32)
    cw = np.asarray(inputs["L0_attn_conv_w"], np.float32)
    cb = np.asarray(inputs["L0_attn_conv_b"], np.float32)
    xpw = np.asarray(inputs["L0_attn_x_proj_w"], np.float32)
    dtw = np.asarray(inputs["L0_attn_dt_proj_w"], np.float32)
    dtb = np.asarray(inputs["L0_attn_dt_proj_b"], np.float32)
    alog = np.asarray(inputs["L0_attn_A_log"], np.float32)
    dd = np.asarray(inputs["L0_attn_D"], np.float32)
    opw = np.asarray(inputs["L0_attn_out_proj_w"], np.float32)
    opb = np.asarray(inputs["L0_attn_out_proj_b"], np.float32)
    nw = np.asarray(inputs["L0_attn_norm_w"], np.float32)
    nfw = np.asarray(inputs["L0_attn_norm_f_w"], np.float32)
    wg = np.asarray(inputs["L0_ffn_w_gate"], np.float32)
    wu = np.asarray(inputs["L0_ffn_w_up"], np.float32)
    wd = np.asarray(inputs["L0_ffn_w_down"], np.float32)

    def colmap(a):
        L = a.shape[0]
        G = a.shape[1] // 128
        return np.ascontiguousarray(a.reshape(L, G, 128).transpose(2, 0, 1)
                                    .reshape(128, L * G))

    shared = {
        "wiT": np.ascontiguousarray(ipw.transpose(0, 2, 1)),
        "wxT": np.ascontiguousarray(xpw.transpose(0, 2, 1)),
        "wdtT": np.ascontiguousarray(dtw.transpose(0, 2, 1)),
        "woT": np.ascontiguousarray(opw.transpose(0, 2, 1)),
        "bi_c": colmap(ipb),
        "cw_c": np.ascontiguousarray(cw.reshape(NLAYER, NDT, 128, 5)
                                     .transpose(2, 0, 1, 3).reshape(128, -1)),
        "cb_c": colmap(cb), "dtb_c": colmap(dtb),
        "alog_c": np.ascontiguousarray(alog.reshape(NLAYER, NDT, 128, NST)
                                       .transpose(2, 0, 1, 3).reshape(128, -1)),
        "d_c": colmap(dd), "nw_c": colmap(nw),
        "nfw_r": np.ascontiguousarray(nfw.reshape(1, DM)),
        "bo_r": np.ascontiguousarray(opb.reshape(1, NLAYER * DM)),
        "wg": wg, "wu": wu, "wd": wd,
        "idm": np.eye(128, dtype=np.float32),
        "idmb": np.eye(128, dtype=np.float32).astype(ml_dtypes.bfloat16),
    }
    in_maps = []
    for c in range(NCORES):
        b, q = c // 4, c % 4
        oh = np.zeros((1, 24), np.float32)
        oh[0, c] = 1.0
        if q > 0:
            oh[0, 8 + (c - 1)] = 1.0
        if q < 3:
            oh[0, 16 + (c + 1)] = 1.0
        m = dict(shared)
        m["x"] = np.ascontiguousarray(x[b, q * TAU:(q + 1) * TAU, :])
        m["oh"] = oh
        in_maps.append(m)
    return in_maps


def kernel(**inputs):
    if _NC_CACHE[0] is None:
        _NC_CACHE[0] = build_program()
    nc = _NC_CACHE[0]
    in_maps = _stage_inputs(inputs)
    res = run_bass_kernel_spmd(nc, in_maps, core_ids=list(range(NCORES)))
    out = np.empty((2, 8192, DM), np.float32)
    for c in range(NCORES):
        b, q = c // 4, c % 4
        out[b, q * TAU:(q + 1) * TAU, :] = res.results[c]["out"]
    return out
